# version 04
# baseline (speedup 1.0000x reference)
import contextlib
import math

import numpy as np
import ml_dtypes

import concourse.bass as bass
import concourse.mybir as mybir
from concourse.bass_utils import run_bass_kernel_spmd

F32 = mybir.dt.float32
BF16 = mybir.dt.bfloat16
I32 = mybir.dt.int32
AF = mybir.ActivationFunctionType
ALU = mybir.AluOpType
BF = ml_dtypes.bfloat16

S, D, H, DC, DP, LQ, LKV, FH = 8192, 2048, 16, 128, 64, 1536, 512, 8192
NCH, CH = 16, 512
SCALE = 1.0 / math.sqrt(DC + DP)
EPS_LN, EPS_RMS = 1e-5, 1e-8
NEG = -30000.0
RH = DP // 2
NKA, NKB = 8, 16
DIAG_A, DIAG_B = 7, 15
VW = 132



def _host_prep(inputs):
    x = np.ascontiguousarray(np.asarray(inputs["L0_attn_x"], np.float32).reshape(S, D))
    xc = x.reshape(NCH, CH, D)
    import jax
    import jax.numpy as jnp
    _cpu = jax.devices("cpu")[0]
    with jax.default_device(_cpu):
        _invf = 1.0 / 10000.0 ** (jnp.arange(RH, dtype=jnp.float32) / RH)

    def g(k):
        return np.ascontiguousarray(np.asarray(inputs[k], np.float32))

    shared = {
        "wq_d": g("L0_attn_Wq_d"), "wq_uc": g("L0_attn_Wq_uc"), "wq_up": g("L0_attn_Wq_up"),
        "wkv_d": g("L0_attn_Wkv_d"), "wk_uc": g("L0_attn_Wk_uc"), "wk_up": g("L0_attn_Wk_up"),
        "wv_u": g("L0_attn_Wv_u"),
        "bq_d": g("L0_attn_bq_d"), "bq_uc": g("L0_attn_bq_uc"),
        "bq_up": g("L0_attn_bq_up").reshape(1, H * DP),
        "bkv_d": g("L0_attn_bkv_d"), "bk_uc": g("L0_attn_bk_uc"),
        "bk_up": g("L0_attn_bk_up").reshape(1, DP),
        "bv_u": g("L0_attn_bv_u").reshape(1, H * DC),
        "wgT": np.ascontiguousarray(np.asarray(inputs["L0_ffn_W_g"], np.float32).T),
        "wuT": np.ascontiguousarray(np.asarray(inputs["L0_ffn_W_u"], np.float32).T),
        "wdT": np.ascontiguousarray(np.asarray(inputs["L0_ffn_W_d"], np.float32).T),
        "cvec": g("L0_c"), "wmod": g("L0_wmod"),
    }

    in_maps, plans = [], []
    for core in range(8):
        a, b = core, 15 - core
        others = [c for c in range(NCH) if c not in (a, b)]
        slots = others[:7] + [a] + others[7:] + [b]
        perm = np.asarray(slots)

        x_kv = np.ascontiguousarray(xc[perm].reshape(S, D))
        x_own = np.ascontiguousarray(np.concatenate([xc[a], xc[b]], axis=0))

        ppos = (perm[:, None] * CH + np.arange(CH)[None, :]).reshape(S)
        with jax.default_device(_cpu):
            _ang = jnp.asarray(ppos)[:, None].astype(jnp.float32) * _invf[None, :]
            _cs = np.asarray(jnp.cos(_ang))
            _sn = np.asarray(jnp.sin(_ang))
        rope = np.ascontiguousarray(np.concatenate([_cs, _sn], axis=1).astype(np.float32))

        mb = np.zeros((128, (NKA + NKB) * 4), dtype=np.float32)
        col = 0
        for qchunk, nslots, diag in ((a, NKA, DIAG_A), (b, NKB, DIAG_B)):
            for sl in range(nslots):
                keep = (sl == diag) or (slots[sl] < qchunk)
                mb[:, col:col + 4] = 0.0 if keep else NEG
                col += 4

        m = dict(shared)
        m.update({"x_own": x_own, "x_kv": x_kv, "rope_cs": rope,
                  "mbias": np.ascontiguousarray(mb)})
        in_maps.append(m)
        plans.append((a, b))
    return in_maps, plans


def _tri_masks():
    t = np.zeros((4 * 128, CH), dtype=BF)
    for j in range(4):
        kk = j * 128 + np.arange(128)
        t[j * 128:(j + 1) * 128] = (kk[:, None] <= np.arange(CH)[None, :]).astype(BF)
    return t



def build_program(stop=None):
    nc = bass.Bass("TRN2", target_bir_lowering=False)

    def din(name, shape):
        return nc.dram_tensor(name, shape, F32, kind="ExternalInput").ap()

    x_own = din("x_own", [1024, D])
    x_kv = din("x_kv", [S, D])
    rope_cs = din("rope_cs", [S, 2 * RH])
    mbias = din("mbias", [128, 96])
    wq_d = din("wq_d", [D, LQ]);      wq_uc = din("wq_uc", [LQ, H * DC])
    wq_up = din("wq_up", [LQ, H * DP]); wkv_d = din("wkv_d", [D, LKV])
    wk_uc = din("wk_uc", [LKV, H * DC]); wk_up = din("wk_up", [LKV, DP])
    wv_u = din("wv_u", [LKV, H * DC])
    bq_d = din("bq_d", [LQ]);   bq_uc = din("bq_uc", [H * DC])
    bq_up = din("bq_up", [1, H * DP]); bkv_d = din("bkv_d", [LKV])
    bk_uc = din("bk_uc", [H * DC]); bk_up = din("bk_up", [1, DP])
    bv_u = din("bv_u", [1, H * DC])
    wgT = din("wgT", [D, FH]); wuT = din("wuT", [D, FH]); wdT = din("wdT", [FH, D])
    cvec = din("cvec", [1, D]); wmod = din("wmod", [D, 6 * D])

    out_own = nc.dram_tensor("out_own", [1024, D], F32, kind="ExternalOutput").ap()

    tri_h = nc.inline_tensor(_tri_masks(), name="tri_h").ap()

    xn_sc = nc.dram_tensor("xn_sc", [S, D], BF16).ap()
    k_sc = nc.dram_tensor("k_sc", [H * DC, S], BF16).ap()
    v_sc = nc.dram_tensor("v_sc", [S, H * DC], BF16).ap()
    kr_sc = nc.dram_tensor("kr_sc", [DP, S], BF16).ap()
    ql_sc = nc.dram_tensor("ql_sc", [LQ, 1024], BF16).ap()
    qr_sc = nc.dram_tensor("qr_sc", [DP, H * 1024], BF16).ap()
    wg_sc = nc.dram_tensor("wg_sc", [D, FH], BF16).ap()
    wu_sc = nc.dram_tensor("wu_sc", [D, FH], BF16).ap()
    wd_sc = nc.dram_tensor("wd_sc", [FH, D], BF16).ap()
    xq_sc = nc.dram_tensor("xq_sc", [1024, D], BF16).ap()
    x1_sc = nc.dram_tensor("x1_sc", [1024, D], F32).ap()
    row_sc = nc.dram_tensor("row_sc", [1, 8 * D], F32).ap()

    sbt = nc.alloc_sbuf_tensor
    ARN = 50816
    ar = sbt("arena", [128, ARN], BF16)

    def A(off, ln):
        assert 0 <= off and off + ln <= ARN, (off, ln)
        return ar[:, off:off + ln]

    def AAP(off, pattern, part=128):
        return bass.AP(ar, off, [[ARN, part]] + pattern)

    fst = sbt("fst", [128, D], F32)
    f2 = sbt("f2", [128, D], F32)
    xout = sbt("xout", [128, 4 * D], F32)
    sgst = sbt("sgst", [128, 2 * D], BF16)
    ost = sbt("ost", [128, 2 * DC], F32)
    mrow = sbt("mrow", [1, CH], F32)
    brow = sbt("brow", [1, D], BF16)
    brow2 = sbt("brow2", [1, D], BF16)
    modt = sbt("modt", [128, 96], F32)
    gtmp = sbt("gtmp", [128, 64], F32)
    ca1 = sbt("ca1", [128, 16], F32)
    cb1_rep = sbt("cb1_rep", [128, D], F32)
    sb_rep = sbt("sb_rep", [128, D], F32)
    ga_rep = sbt("ga_rep", [128, D], BF16)
    gb_rep = sbt("gb_rep", [128, D], BF16)
    biasq2 = sbt("biasq2", [128, 12], F32)
    biaskv2 = sbt("biaskv2", [128, 4], F32)
    bias_kc = sbt("bias_kc", [128, 16], F32)
    bias_qc = sbt("bias_qc", [128, 16], F32)
    brow_qup = sbt("brow_qup", [1, H * DP], BF16)
    brow_kup = sbt("brow_kup", [1, DP], BF16)
    brow_vu = sbt("brow_vu", [1, H * DC], BF16)
    ones1 = sbt("ones1", [1, 128], BF16)
    mb_sb = sbt("mb_sb", [128, 96], F32)
    tri_sb = sbt("tri_sb", [128, 4 * CH], BF16)
    bn24 = sbt("bn24", [128, 24], F32)
    st2 = sbt("st2", [128, 2], F32)
    rstd = sbt("rstd", [128, 1], F32)
    nmean = sbt("nmean", [128, 1], F32)
    st2b = sbt("st2b", [128, 2], F32)
    rstd2 = sbt("rstd2", [128, 1], F32)
    amax_t = sbt("amax_t", [128, 1], F32)
    srec = sbt("srec", [128, 1], F32)
    sc_ap = sbt("sc_ap", [128, 1], F32)
    nm2 = sbt("nm2", [128, 1], F32)
    lrec = sbt("lrec", [128, 1], F32)
    rcs = sbt("rcs", [128, 4 * RH], F32)
    cst = sbt("cst", [128, 4], F32)
    sa2b = sbt("sa2b", [128, 16], BF16)
    ti32 = sbt("ti32", [128, 1024], I32)

    psA = nc.alloc_psum_tensor("psA", [128, 512], F32)
    psB = nc.alloc_psum_tensor("psB", [128, 512], F32)
    psO = nc.alloc_psum_tensor("psO", [128, 4 * 512], F32)
    psG = nc.alloc_psum_tensor("psG", [128, 512], F32)
    psU = nc.alloc_psum_tensor("psU", [128, 512], F32)

    es = contextlib.ExitStack()
    es.enter_context(nc.allow_non_contiguous_dma("tiny strided bias/mod-column loads"))

    LONGLIVED = {"a0g", "a2", "g2"}
    SWDGE = {"a0g", "d0", "f0", "g2", "j_wqc0", "j_wqc1", "j_acc0", "j_acc1"}
    phys = {}
    phys_cnt = {}
    free_pool = []
    alias = {}
    dead = set()
    nphys = [0]

    def _new_phys():
        nm = f"p{nphys[0]}"
        nphys[0] += 1
        phys[nm] = es.enter_context(nc.semaphore(nm))
        phys_cnt[nm] = 0
        return nm

    def sem(name):
        if name in alias:
            return phys[alias[name][0]]
        assert name not in dead, f"logical sem {name} reused after its barrier"
        if name in LONGLIVED or name in SWDGE:
            pn = _new_phys()
        elif free_pool:
            pn = free_pool.pop()
        else:
            pn = _new_phys()
        alias[name] = (pn, phys_cnt[pn])
        return phys[pn]

    cnt = {}
    first_inc = set()

    def inc(ins, name, k=1):
        h = sem(name)
        pn, base = alias[name]
        if name not in first_inc:
            first_inc.add(name)
            if base > 0:
                ins._wait_ge(h, base)
        ins.then_inc(h, k)
        phys_cnt[pn] += k
        cnt[name] = cnt.get(name, 0) + k
        return cnt[name]

    def tot(name):
        return cnt.get(name, 0)

    def wge(eng, name, val=None):
        h = sem(name)
        _, base = alias[name]
        eng.wait_ge(h, base + (tot(name) if val is None else val))

    def BAR():
        nc.all_engine_barrier()
        for name, (pn, _) in list(alias.items()):
            if name in LONGLIVED or name in SWDGE:
                continue
            del alias[name]
            dead.add(name)
            if pn not in free_pool:
                free_pool.append(pn)

    with nc.Block() as blk:
        @blk.sync
        def _(sy):
            for o_, i_ in (
                (fst[0:1, 0:D], cvec),
                (biasq2[:, :], bass.AP(bq_d.tensor, 0, [[1, 128], [128, 12]])),
                (biaskv2[:, :], bass.AP(bkv_d.tensor, 0, [[1, 128], [128, 4]])),
                (bias_kc[:, :], bass.AP(bk_uc.tensor, 0, [[1, 128], [128, 16]])),
                (bias_qc[:, :], bass.AP(bq_uc.tensor, 0, [[1, 128], [128, 16]])),
                (mb_sb[:, :], mbias),
                (tri_sb[:, :], bass.AP(tri_h.tensor, 0, [[512, 128], [65536, 4], [1, 512]])),
            ):
                ins = sy.dma_start(out=o_, in_=i_)
                inc(ins, "a0", 16)

        @blk.vector
        def _(v):
            v.memset(ones1[:, :], 1.0)
            v.memset(cst[:, 0:1], EPS_LN)
            v.memset(cst[:, 1:2], EPS_RMS)
            v.memset(cst[:, 2:3], 1.0)

        @blk.gpsimd
        def _(gp):
            ins = gp.dma_start(out=brow_qup[:, :], in_=bq_up)
            inc(ins, "a0g", 16)
            ins = gp.dma_start(out=brow_kup[:, :], in_=bk_up)
            inc(ins, "a0g", 16)

        @blk.scalar
        def _(sc):
            wge(sc, "a0")
            ins = sc.activation(f2[0:1, 0:D], fst[0:1, 0:D], AF.Silu)
            inc(ins, "a1", 1)
            wge(sc, "a1")
            ins = sc.dma_start(out=row_sc[0:1, 0:D], in_=f2[0:1, 0:D])
            inc(ins, "a2", 16)

    BAR()
    if stop == 'B0a':
        es.close()
        return nc

    with nc.Block() as blk:
        @blk.sync
        def _(sy):
            wge(sy, "a2", 16)
            ins = sy.dma_start(out=modt[:, 0:16],
                               in_=bass.AP(row_sc.tensor, 0, [[1, 128], [128, 16]]))
            inc(ins, "a3", 16)
            for cg in range(24):
                for sub in range(8):
                    j = cg * 8 + sub
                    if j >= 2:
                        wge(sy, "modmm", j - 1)
                    ins = sy.dma_start(
                        out=(fst if j % 2 == 0 else f2)[:, 0:1024],
                        in_=bass.AP(wmod.tensor, 2 * sub * 128 * 6 * D + cg * 512,
                                    [[6 * D, 128], [128 * 6 * D, 2], [1, 512]]))
                    inc(ins, "modld%d" % (j % 2), 16)

        @blk.tensor
        def _(t):
            wge(t, "a3", 16)
            for cg in range(24):
                for sub in range(8):
                    j = cg * 8 + sub
                    wge(t, "modld%d" % (j % 2), 16 * (j // 2 + 1))
                    if cg >= 1 and sub == 0:
                        wge(t, "modcp", cg)
                    st = fst if j % 2 == 0 else f2
                    for dc in range(2):
                        ins = t.matmul(psA[0:1, :], modt[:, 2 * sub + dc:2 * sub + dc + 1],
                                       st[:, dc * 512:(dc + 1) * 512],
                                       start=(sub == 0 and dc == 0),
                                       stop=(sub == 7 and dc == 1))
                    inc(ins, "modmm", 1)

        @blk.scalar
        def _(sc):
            for cg in range(24):
                wge(sc, "modmm", 8 * (cg + 1))
                if cg >= 1:
                    wge(sc, "modst", 16 * cg)
                ins = sc.copy(mrow[:, :], psA[0:1, :])
                inc(ins, "modcp", 1)
                wge(sc, "modcp")
                ins = sc.dma_start(out=row_sc[0:1, cg * 512:(cg + 1) * 512], in_=mrow[:, :])
                inc(ins, "modst", 16)
            wge(sc, "modst")

    BAR()
    if stop == 'B0b':
        es.close()
        return nc

    with nc.Block() as blk:
        @blk.sync
        def _(sy):
            ins = sy.dma_start(out=modt[:, 0:96],
                               in_=bass.AP(row_sc.tensor, 0, [[1, 128], [128, 96]]))
            inc(ins, "c0", 16)

        @blk.scalar
        def _(sc):
            wge(sc, "c0", 16)
            sc.activation(gtmp[:, 0:16], modt[:, 32:48], AF.Abs)
            sc.activation(gtmp[:, 16:32], modt[:, 80:96], AF.Abs)
            sc.activation(gtmp[:, 32:48], modt[:, 32:48], AF.Sign)
            ins = sc.activation(gtmp[:, 48:64], modt[:, 80:96], AF.Sign)
            inc(ins, "c1", 1)

        @blk.vector
        def _(v):
            wge(v, "c1", 1)
            v.tensor_scalar_max(gtmp[:, 0:16], gtmp[:, 0:16], 0.25)
            ins = v.tensor_scalar_max(gtmp[:, 16:32], gtmp[:, 16:32], 0.25)
            inc(ins, "c2", 1)
            wge(v, "c2")
            v.tensor_mul(modt[:, 32:48], gtmp[:, 0:16], gtmp[:, 32:48])
            v.tensor_mul(modt[:, 80:96], gtmp[:, 16:32], gtmp[:, 48:64])
            ins = v.tensor_scalar_add(ca1[:, :], modt[:, 16:32], 1.0)
            inc(ins, "c3", 1)

        @blk.scalar
        def _(sc):
            wge(sc, "c3", 1)
            ins = sc.dma_start(out=bass.AP(row_sc.tensor, 2 * D, [[1, 128], [128, 16]]),
                                in_=modt[:, 32:48])
            inc(ins, "c4", 16)
            ins = sc.dma_start(out=bass.AP(row_sc.tensor, 5 * D, [[1, 128], [128, 16]]),
                               in_=modt[:, 80:96])
            inc(ins, "c4", 16)
            wge(sc, "c4")

    BAR()
    if stop == 'B0c':
        es.close()
        return nc

    with nc.Block() as blk:
        @blk.gpsimd
        def _(gp):
            for o_, i_ in (
                (brow[0:1, :], bass.AP(row_sc.tensor, 4 * D, [[1, 1], [1, D]])),
                (sgst[0:1, 0:D], bass.AP(row_sc.tensor, 3 * D, [[1, 1], [1, D]])),
                (sgst[0:1, D:2 * D], bass.AP(row_sc.tensor, 2 * D, [[1, 1], [1, D]])),
                (brow2[0:1, :], bass.AP(row_sc.tensor, 5 * D, [[1, 1], [1, D]])),
                (brow_vu[0:1, :], bv_u),
            ):
                ins = gp.dma_start(out=o_, in_=i_)
                inc(ins, "d0", 16)

        @blk.tensor
        def _(t):
            wge(t, "d0")
            for j in range(4):
                if j >= 1:
                    wge(t, "d2", j)
                t.matmul(psA[:, :], ones1[:, :], brow[0:1, j * 512:(j + 1) * 512],
                         start=True, stop=True)
                t.matmul(psB[:, :], ones1[:, :], sgst[0:1, j * 512:(j + 1) * 512],
                         start=True, stop=True)
                t.matmul(psG[:, :], ones1[:, :], sgst[0:1, D + j * 512:D + (j + 1) * 512],
                         start=True, stop=True)
                ins = t.matmul(psU[:, :], ones1[:, :], brow2[0:1, j * 512:(j + 1) * 512],
                               start=True, stop=True)
                inc(ins, "d1", 1)

        @blk.scalar
        def _(sc):
            for j in range(4):
                wge(sc, "d1", j + 1)
                sc.activation(cb1_rep[:, j * 512:(j + 1) * 512], psA[:, :], AF.Identity, bias=cst[:, 2:3])
                sc.copy(sb_rep[:, j * 512:(j + 1) * 512], psB[:, :])
                sc.copy(ga_rep[:, j * 512:(j + 1) * 512], psG[:, :])
                ins = sc.copy(gb_rep[:, j * 512:(j + 1) * 512], psU[:, :])
                inc(ins, "d2", 1)

        @blk.vector
        def _(v):
            wge(v, "d2", 4)
            ins = v.tensor_mul(brow_vu[0:1, :], brow_vu[0:1, :], ga_rep[0:1, :])
            inc(ins, "d3", 1)

    BAR()
    if stop == 'B0d':
        es.close()
        return nc

    ld0 = fst[:, :]
    ld1 = f2[:, :]

    with nc.Block() as blk:
        @blk.sync
        def _(sy):
            for i in range(64):
                if i >= 2:
                    wge(sy, "b1n", i - 1)
                ins = sy.dma_start(out=(ld0 if i % 2 == 0 else ld1),
                                   in_=x_kv[i * 128:(i + 1) * 128, :])
                inc(ins, "b1x%d" % (i % 2), 16)

        @blk.vector
        def _(v):
            for i in range(64):
                wge(v, "b1x%d" % (i % 2), 16 * (i // 2 + 1))
                if i >= 1:
                    wge(v, "b1ln", i)
                src = ld0 if i % 2 == 0 else ld1
                for j in range(4):
                    ins = v.bn_stats(bn24[:, j * 6:(j + 1) * 6], src[:, j * 512:(j + 1) * 512])
                inc(ins, "b1bn", 1)
                wge(v, "b1bn")
                ins = v.bn_aggr(st2[:, :], bn24[:, :])
                inc(ins, "b1ag", 1)
                wge(v, "b1ex", i + 1)
                if i >= 1:
                    wge(v, "b1n", i)
                ins = v.tensor_scalar(nmean[:, :], st2[:, 0:1], rstd[:, 0:1], -1.0,
                                      ALU.mult, ALU.mult)
                inc(ins, "b1nm", 1)

        @blk.scalar
        def _(sc):
            for i in range(64):
                src = ld0 if i % 2 == 0 else ld1
                wge(sc, "b1ag", i + 1)
                ins = sc.activation(rstd[:, :], st2[:, 1:2], AF.Ln, bias=cst[:, 0:1])
                inc(ins, "b1ln", 1)
                wge(sc, "b1ln")
                ins = sc.activation(rstd[:, :], rstd[:, :], AF.Exp, scale=-0.5)
                inc(ins, "b1ex", 1)
                wge(sc, "b1nm", i + 1)
                if i >= 2:
                    wge(sc, "b1s%d" % (i % 2), 16 * (i // 2))
                ins = sc.activation(sgst[:, (i % 2) * D:(i % 2 + 1) * D], src, AF.Identity,
                                    scale=rstd[:, 0:1], bias=nmean[:, 0:1])
                inc(ins, "b1n", 1)
                wge(sc, "b1n")
                ins = sc.dma_start(out=xn_sc[i * 128:(i + 1) * 128, :],
                                   in_=sgst[:, (i % 2) * D:(i % 2 + 1) * D])
                inc(ins, "b1s%d" % (i % 2), 16)
            wge(sc, "b1s0")
            wge(sc, "b1s1")

    BAR()
    if stop == 'B1':
        es.close()
        return nc

    O_WQD = 0

    with nc.Block() as blk:
        @blk.sync
        def _(sy):
            for dc in range(16):
                if dc >= 2:
                    wge(sy, "e0", dc - 1)
                ins = sy.dma_start(out=(fst if dc % 2 == 0 else f2)[:, 0:1536],
                                   in_=wq_d[dc * 128:(dc + 1) * 128, :])
                inc(ins, "e0ld%d" % (dc % 2), 16)

        @blk.vector
        def _(v):
            ins = v.reciprocal(gtmp[:, 0:16], ca1[:, :])
            inc(ins, "e_sa2r", 1)
            wge(v, "e_sa2r")
            ins = v.tensor_tensor(sa2b[:, :], modt[:, 0:16], gtmp[:, 0:16], ALU.mult)
            inc(ins, "e_sa2", 1)
            for dc in range(16):
                wge(v, "e0ld%d" % (dc % 2), 16 * (dc // 2 + 1))
                src = fst if dc % 2 == 0 else f2
                ins = v.tensor_scalar_mul(A(O_WQD + dc * LQ, LQ), src[:, 0:1536],
                                          ca1[:, dc:dc + 1])
                inc(ins, "e0", 1)

        @blk.tensor
        def _(t):
            wge(t, "e0", 16)
            wge(t, "e_sa2", 1)
            for grp in range(3):
                for dc in range(16):
                    ins = t.matmul(psO[0:1, grp * 512:grp * 512 + 512],
                                   sa2b[:, dc:dc + 1],
                                   A(O_WQD + dc * LQ + grp * 512, 512),
                                   start=(dc == 0), stop=(dc == 15))
                inc(ins, "e1", 1)

        @blk.scalar
        def _(sc):
            wge(sc, "e1", 3)
            for grp in range(3):
                if grp >= 1:
                    wge(sc, "e3", 16 * grp)
                ins = sc.copy(mrow[:, :], psO[0:1, grp * 512:grp * 512 + 512])
                inc(ins, "e2", 1)
                wge(sc, "e2")
                ins = sc.dma_start(out=row_sc[0:1, 6 * D + grp * 512:6 * D + (grp + 1) * 512],
                                   in_=mrow[:, :])
                inc(ins, "e3", 16)
            wge(sc, "e3")

    BAR()
    if stop == 'B2a':
        es.close()
        return nc

    with nc.Block() as blk:
        @blk.sync
        def _(sy):
            ins = sy.dma_start(out=gtmp[:, 32:44],
                               in_=bass.AP(row_sc.tensor, 6 * D, [[1, 128], [128, 12]]))
            inc(ins, "e4", 16)

        @blk.vector
        def _(v):
            wge(v, "e4", 16)
            ins = v.tensor_add(biasq2[:, :], biasq2[:, :], gtmp[:, 32:44])
            inc(ins, "e5", 1)

    BAR()
    if stop == 'B2a2':
        es.close()
        return nc

    O_XNTQ = 24576

    with nc.Block() as blk:
        @blk.sync
        def _(sy):
            for dc in range(16):
                for half, slot in ((0, DIAG_A), (1, DIAG_B)):
                    ins = sy.dma_start(
                        out=A(O_XNTQ + dc * 1024 + half * 512, 512),
                        in_=xn_sc[slot * CH:(slot + 1) * CH, dc * 128:(dc + 1) * 128],
                        transpose=True)
                    inc(ins, "e6", 16)

        @blk.tensor
        def _(t):
            wge(t, "e6", 16 * 32)
            k = 0
            for lqs in range(12):
                for half in range(2):
                    k += 1
                    ps = psA if half == 0 else psB
                    if k > 2:
                        wge(t, "e8", k - 2)
                    for dc in range(16):
                        ins = t.matmul(ps[:, :], A(O_WQD + dc * LQ + lqs * 128, 128),
                                       A(O_XNTQ + dc * 1024 + half * 512, 512),
                                       start=(dc == 0), stop=(dc == 15))
                    inc(ins, "e7", 1)

        @blk.scalar
        def _(sc):
            k = 0
            for lqs in range(12):
                for half in range(2):
                    k += 1
                    ps = psA if half == 0 else psB
                    wge(sc, "e7", k)
                    if lqs >= 1:
                        wge(sc, "e9h%d" % half, 16 * lqs)
                    ins = sc.activation(sgst[:, half * 512:half * 512 + 512], ps[:, :],
                                        AF.Identity, bias=biasq2[:, lqs:lqs + 1])
                    inc(ins, "e8", 1)
                    wge(sc, "e8")
                    ins = sc.dma_start(out=ql_sc[lqs * 128:(lqs + 1) * 128,
                                                 half * 512:(half + 1) * 512],
                                       in_=sgst[:, half * 512:half * 512 + 512])
                    inc(ins, "e9h%d" % half, 16)
            wge(sc, "e9h0")
            wge(sc, "e9h1")

    BAR()
    if stop == 'B2b':
        es.close()
        return nc

    O_WQUP, O_QLST, O_QRT = 0, 12288, 15360

    with nc.Block() as blk:
        @blk.gpsimd
        def _(gp):
            ins = gp.dma_start(
                out=AAP(O_WQUP, [[1024, 12], [1, 1024]]),
                in_=bass.AP(wq_up.tensor, 0, [[1024, 128], [128 * 1024, 12], [1, 1024]]))
            inc(ins, "f0", 16)

        @blk.sync
        def _(sy):
            for qs in range(8):
                if qs >= 2:
                    wge(sy, "f_mm", 2 * (qs - 1))
                ins = sy.dma_start(
                    out=AAP(O_QLST + (qs % 2) * 1536, [[128, 12], [1, 128]]),
                    in_=bass.AP(ql_sc.tensor, qs * 128,
                                [[1024, 128], [128 * 1024, 12], [1, 128]]))
                inc(ins, "f_ql%d" % (qs % 2), 16)
                if qs >= 1:
                    wge(sy, "f_ra", qs)
                slot = DIAG_A if qs < 4 else DIAG_B
                roff = slot * CH + (qs % 4) * 128
                ins = sy.dma_start(out=rcs[:, 0:64], in_=rope_cs[roff:roff + 128, :])
                inc(ins, "f_rc", 16)

        @blk.tensor
        def _(t):
            wge(t, "f0", 16)
            wge(t, "a0g", 32)
            for qs in range(8):
                wge(t, "f_ql%d" % (qs % 2), 16 * (qs // 2 + 1))
                for half in range(2):
                    ps = psA if half == 0 else psB
                    if qs * 2 + half >= 2:
                        wge(t, "f_ev", qs * 2 + half - 1)
                    for lq in range(12):
                        t.matmul(ps[:, :],
                                 AAP(O_QLST + (qs % 2) * 1536 + lq * 128, [[1, 128]]),
                                 A(O_WQUP + lq * 1024 + half * 512, 512),
                                 start=(lq == 0), stop=False)
                    ins = t.matmul(ps[:, :], ones1[:, :],
                                   brow_qup[0:1, half * 512:(half + 1) * 512],
                                   start=False, stop=True)
                    inc(ins, "f_mm", 1)

        @blk.scalar
        def _(sc):
            for qs in range(8):
                for half in range(2):
                    ps = psA if half == 0 else psB
                    wge(sc, "f_mm", qs * 2 + half + 1)
                    if qs >= 1 and half == 0:
                        wge(sc, "f_ra", qs)
                    ins = sc.activation(fst[:, half * 512:(half + 1) * 512], ps[:, :],
                                        AF.Identity, scale=SCALE)
                    inc(ins, "f_ev", 1)
                wge(sc, "f_bf", qs + 1)
                for hp in range(8):
                    wge(sc, "f_tp")
                    ins = sc.dma_start(out=A(O_QRT + (hp % 2) * 128, 128),
                                       in_=sgst[:, hp * 128:(hp + 1) * 128],
                                       transpose=True)
                    inc(ins, "f_tp", 16)
                    wge(sc, "f_tp")
                    ins = sc.dma_start(
                        out=bass.AP(qr_sc.tensor, (2 * hp) * 1024 + qs * 128,
                                    [[H * 1024, 64], [1, 128]]),
                        in_=ar[0:64, O_QRT + (hp % 2) * 128:O_QRT + (hp % 2) * 128 + 128])
                    inc(ins, "f_tp", 16)
                    ins = sc.dma_start(
                        out=bass.AP(qr_sc.tensor, (2 * hp + 1) * 1024 + qs * 128,
                                    [[H * 1024, 64], [1, 128]]),
                        in_=ar[64:128, O_QRT + (hp % 2) * 128:O_QRT + (hp % 2) * 128 + 128])
                    inc(ins, "f_tp", 16)
            wge(sc, "f_tp")

        @blk.vector
        def _(v):
            for qs in range(8):
                wge(v, "f_ev", qs * 2 + 2)
                wge(v, "f_rc", 16 * (qs + 1))
                if qs >= 1:
                    wge(v, "f_tp", 16 * 24 * qs)
                x1v = bass.AP(fst, 0, [[D, 128], [64, 16], [1, 32]])
                x2v = bass.AP(fst, 32, [[D, 128], [64, 16], [1, 32]])
                cosv = bass.AP(rcs, 0, [[4 * RH, 128], [0, 16], [1, 32]])
                sinv = bass.AP(rcs, RH, [[4 * RH, 128], [0, 16], [1, 32]])
                o1 = bass.AP(f2, 0, [[D, 128], [64, 16], [1, 32]])
                o2 = bass.AP(f2, 32, [[D, 128], [64, 16], [1, 32]])
                g1 = bass.AP(f2, 1024, [[D, 128], [64, 16], [1, 32]])
                g2 = bass.AP(f2, 1024 + 32, [[D, 128], [64, 16], [1, 32]])
                v.tensor_tensor(o1, x1v, cosv, ALU.mult)
                v.tensor_tensor(o2, x1v, sinv, ALU.mult)
                v.tensor_tensor(g1, x2v, sinv, ALU.mult)
                ins = v.tensor_tensor(g2, x2v, cosv, ALU.mult)
                inc(ins, "f_ra", 1)
                wge(v, "f_ra")
                v.tensor_tensor(o1, o1, g1, ALU.subtract)
                ins = v.tensor_tensor(o2, o2, g2, ALU.add)
                inc(ins, "f_rb", 1)
                wge(v, "f_rb")
                ins = v.tensor_copy(sgst[:, 0:1024], f2[:, 0:1024])
                inc(ins, "f_bf", 1)


    BAR()
    if stop == 'B2c':
        es.close()
        return nc

    O_WKVD, O_WKUC, O_WVU, O_WKUP = 0, 8192, 16384, 24576

    with nc.Block() as blk:
        @blk.sync
        def _(sy):
            for dc in range(16):
                if dc >= 2:
                    wge(sy, "g0", dc - 1)
                ins = sy.dma_start(out=(fst if dc % 2 == 0 else f2)[:, 0:512],
                                   in_=wkv_d[dc * 128:(dc + 1) * 128, :])
                inc(ins, "g0ld%d" % (dc % 2), 16)
            for lk in range(4):
                if lk == 0:
                    wge(sy, "g0", 15)
                else:
                    wge(sy, "g1", lk)
                ins = sy.dma_start(out=(fst if lk % 2 == 0 else f2)[:, 0:D],
                                   in_=wv_u[lk * 128:(lk + 1) * 128, :])
                inc(ins, "g1ld%d" % (lk % 2), 16)

        @blk.gpsimd
        def _(gp):
            ins = gp.dma_start(
                out=AAP(O_WKUC, [[2048, 4], [1, 2048]]),
                in_=bass.AP(wk_uc.tensor, 0, [[2048, 128], [128 * 2048, 4], [1, 2048]]))
            inc(ins, "g2", 16)
            ins = gp.dma_start(
                out=AAP(O_WKUP, [[64, 4], [1, 64]]),
                in_=bass.AP(wk_up.tensor, 0, [[64, 128], [128 * 64, 4], [1, 64]]))
            inc(ins, "g2", 16)

        @blk.vector
        def _(v):
            for dc in range(16):
                wge(v, "g0ld%d" % (dc % 2), 16 * (dc // 2 + 1))
                src = fst if dc % 2 == 0 else f2
                ins = v.tensor_scalar_mul(A(O_WKVD + dc * 512, 512), src[:, 0:512],
                                          ca1[:, dc:dc + 1])
                inc(ins, "g0", 1)
            for lk in range(4):
                wge(v, "g1ld%d" % (lk % 2), 16 * (lk // 2 + 1))
                src = fst if lk % 2 == 0 else f2
                ins = v.tensor_tensor(A(O_WVU + lk * 2048, 2048), src[:, 0:D],
                                      ga_rep[:, :], ALU.mult)
                inc(ins, "g1", 1)

        @blk.tensor
        def _(t):
            wge(t, "g0", 16)
            for dc in range(16):
                ins = t.matmul(psO[0:1, 0:512], sa2b[:, dc:dc + 1],
                               A(O_WKVD + dc * 512, 512),
                               start=(dc == 0), stop=(dc == 15))
            inc(ins, "g3", 1)

        @blk.scalar
        def _(sc):
            wge(sc, "g3", 1)
            ins = sc.copy(mrow[:, :], psO[0:1, 0:512])
            inc(ins, "g4", 1)
            wge(sc, "g4")
            ins = sc.dma_start(out=row_sc[0:1, 6 * D:6 * D + 512], in_=mrow[:, :])
            inc(ins, "g5", 16)
            wge(sc, "g5")

    BAR()
    if stop == 'B3a':
        es.close()
        return nc

    with nc.Block() as blk:
        @blk.sync
        def _(sy):
            ins = sy.dma_start(out=gtmp[:, 32:36],
                               in_=bass.AP(row_sc.tensor, 6 * D, [[1, 128], [128, 4]]))
            inc(ins, "g6", 16)

        @blk.vector
        def _(v):
            wge(v, "g6", 16)
            ins = v.tensor_add(biaskv2[:, :], biaskv2[:, :], gtmp[:, 32:36])
            inc(ins, "g7", 1)

    BAR()
    if stop == 'B3a2':
        es.close()
        return nc

    O_XNT, O_KVL = 24832, 41216

    def kvl(buf, lk):
        return A(O_KVL + buf * 2048 + lk * 512, 512)

    with nc.Block() as blk:
        @blk.sync
        def _(sy):
            for kc in range(16):
                if kc >= 2:
                    wge(sy, "h_kv", 4 * (kc - 1))
                for dc in range(16):
                    ins = sy.dma_start(
                        out=A(O_XNT + (kc % 2) * 8192 + dc * 512, 512),
                        in_=xn_sc[kc * CH:(kc + 1) * CH, dc * 128:(dc + 1) * 128],
                        transpose=True)
                    inc(ins, "h_xn%d" % (kc % 2), 16)
                for ks in range(4):
                    i = kc * 4 + ks
                    if i >= 2:
                        wge(sy, "h_ra", i - 1)
                    ins = sy.dma_start(
                        out=rcs[:, (i % 2) * 64:(i % 2) * 64 + 64],
                        in_=rope_cs[kc * CH + ks * 128:kc * CH + (ks + 1) * 128, :])
                    inc(ins, "h_rc%d" % (i % 2), 16)

        @blk.tensor
        def _(t):
            wge(t, "g2", 32)
            for kc in range(16):
                wge(t, "h_xn%d" % (kc % 2), 256 * (kc // 2 + 1))
                if kc >= 1:
                    wge(t, "h_kev", 16 * kc)
                for ls in range(4):
                    ps = psA if ls % 2 == 0 else psB
                    if kc * 4 + ls >= 2:
                        wge(t, "h_kvev", kc * 4 + ls - 1)
                    for dc in range(16):
                        ins = t.matmul(ps[:, :], A(O_WKVD + dc * 512 + ls * 128, 128),
                                       A(O_XNT + (kc % 2) * 8192 + dc * 512, 512),
                                       start=(dc == 0), stop=(dc == 15))
                    inc(ins, "h_kv", 1)
                wge(t, "h_kvev", 4 * (kc + 1))
                for hs in range(16):
                    ps = psA if hs % 2 == 0 else psB
                    if kc * 16 + hs >= 2:
                        wge(t, "h_kev", kc * 16 + hs - 1)
                    for lk in range(4):
                        ins = t.matmul(ps[:, :], A(O_WKUC + lk * 2048 + hs * 128, 128),
                                       kvl(kc % 2, lk),
                                       start=(lk == 0), stop=(lk == 3))
                    inc(ins, "h_k", 1)
                for ks in range(4):
                    if kc * 4 + ks >= 1:
                        wge(t, "h_vev", (kc * 4 + ks) * 4)
                    for lk in range(4):
                        for hq in range(4):
                            t.matmul(psO[:, hq * 512:(hq + 1) * 512],
                                     AAP(O_KVL + (kc % 2) * 2048 + lk * 512 + ks * 128,
                                         [[1, 128]]),
                                     A(O_WVU + lk * 2048 + hq * 512, 512),
                                     start=(lk == 0), stop=False)
                    for hq in range(4):
                        ins = t.matmul(psO[:, hq * 512:(hq + 1) * 512], ones1[:, :],
                                       brow_vu[0:1, hq * 512:(hq + 1) * 512],
                                       start=False, stop=True)
                        inc(ins, "h_v", 1)
                for ks in range(4):
                    if kc * 4 + ks >= 1:
                        wge(t, "h_rev", kc * 4 + ks)
                    for lk in range(4):
                        t.matmul(psG[:, 0:64],
                                 AAP(O_KVL + (kc % 2) * 2048 + lk * 512 + ks * 128,
                                     [[1, 128]]),
                                 A(O_WKUP + lk * 64, 64),
                                 start=(lk == 0), stop=False)
                    ins = t.matmul(psG[:, 0:64], ones1[:, :], brow_kup[0:1, :],
                                   start=False, stop=True)
                    inc(ins, "h_r", 1)

        @blk.scalar
        def _(sc):
            for kc in range(16):
                for ls in range(4):
                    ps = psA if ls % 2 == 0 else psB
                    wge(sc, "h_kv", kc * 4 + ls + 1)
                    ins = sc.activation(kvl(kc % 2, ls), ps[:, :], AF.Identity,
                                        bias=biaskv2[:, ls:ls + 1])
                    inc(ins, "h_kvev", 1)
                for hs in range(16):
                    ps = psA if hs % 2 == 0 else psB
                    wge(sc, "h_k", kc * 16 + hs + 1)
                    q = hs % 4
                    nuse = kc * 4 + hs // 4
                    if nuse >= 1:
                        wge(sc, "h_ks%d" % q, 16 * nuse)
                    if kc >= 1 and hs < 4:
                        wge(sc, "h_vst", 64 * kc)
                    ins = sc.activation(sgst[:, q * 512:(q + 1) * 512], ps[:, :],
                                        AF.Identity, bias=bias_kc[:, hs:hs + 1])
                    inc(ins, "h_kev", 1)
                    wge(sc, "h_kev")
                    ins = sc.dma_start(out=k_sc[hs * 128:(hs + 1) * 128,
                                                kc * 512:(kc + 1) * 512],
                                       in_=sgst[:, q * 512:(q + 1) * 512])
                    inc(ins, "h_ks%d" % q, 16)
                for ks in range(4):
                    for hq in range(4):
                        wge(sc, "h_v", kc * 16 + ks * 4 + hq + 1)
                        if ks == 0:
                            wge(sc, "h_ks%d" % hq, 16 * (kc + 1) * 4)
                        ins = sc.copy(sgst[:, hq * 512:(hq + 1) * 512],
                                      psO[:, hq * 512:(hq + 1) * 512])
                        inc(ins, "h_vev", 1)
                    wge(sc, "h_vev")
                    if ks >= 1:
                        pass
                    ins = sc.dma_start(out=v_sc[kc * CH + ks * 128:kc * CH + (ks + 1) * 128, :],
                                       in_=sgst[:, 0:2048])
                    inc(ins, "h_vst", 16)
                    if ks < 3:
                        wge(sc, "h_vst", 16 * (kc * 4 + ks + 1))
                for ks in range(4):
                    wge(sc, "h_r", kc * 4 + ks + 1)
                    if kc * 4 + ks >= 4:
                        wge(sc, "h_ra", kc * 4 + ks - 3)
                    ins = sc.activation(fst[:, ks * 64:(ks + 1) * 64], psG[:, 0:64],
                                        AF.Identity)
                    inc(ins, "h_rev", 1)
                for pair in range(2):
                    wge(sc, "h_bf", kc * 4 + pair * 2 + 2)
                    wge(sc, "h_tp")
                    ins = sc.dma_start(out=A(45312, 128),
                                       in_=sgst[:, D:D + 128], transpose=True)
                    inc(ins, "h_tp", 16)
                    wge(sc, "h_tp")
                    k0 = kc * CH + pair * 256
                    ins = sc.dma_start(
                        out=bass.AP(kr_sc.tensor, k0, [[S, 64], [1, 128]]),
                        in_=ar[0:64, 45312:45312 + 128])
                    inc(ins, "h_tp", 16)
                    ins = sc.dma_start(
                        out=bass.AP(kr_sc.tensor, k0 + 128, [[S, 64], [1, 128]]),
                        in_=ar[64:128, 45312:45312 + 128])
                    inc(ins, "h_tp", 16)
            wge(sc, "h_vst")
            wge(sc, "h_tp")


        @blk.vector
        def _(v):
            for kc in range(16):
                for ks in range(4):
                    i = kc * 4 + ks
                    pair = i // 2
                    wge(v, "h_rev", i + 1)
                    wge(v, "h_rc%d" % (i % 2), 16 * (i // 2 + 1))
                    if pair >= 1 and i % 2 == 0:
                        wge(v, "h_tp", 48 * pair)
                    src1 = bass.AP(fst, ks * 64, [[D, 128], [1, 32]])
                    src2 = bass.AP(fst, ks * 64 + 32, [[D, 128], [1, 32]])
                    cosv = bass.AP(rcs, (i % 2) * 64, [[4 * RH, 128], [1, 32]])
                    sinv = bass.AP(rcs, (i % 2) * 64 + RH, [[4 * RH, 128], [1, 32]])
                    o1 = bass.AP(f2, (ks % 2) * 64, [[D, 128], [1, 32]])
                    o2 = bass.AP(f2, (ks % 2) * 64 + 32, [[D, 128], [1, 32]])
                    g1 = bass.AP(f2, 1024 + (ks % 2) * 64, [[D, 128], [1, 32]])
                    g2 = bass.AP(f2, 1024 + (ks % 2) * 64 + 32, [[D, 128], [1, 32]])
                    v.tensor_tensor(o1, src1, cosv, ALU.mult)
                    v.tensor_tensor(o2, src1, sinv, ALU.mult)
                    v.tensor_tensor(g1, src2, sinv, ALU.mult)
                    ins = v.tensor_tensor(g2, src2, cosv, ALU.mult)
                    inc(ins, "h_ra", 1)
                    wge(v, "h_ra")
                    v.tensor_tensor(o1, o1, g1, ALU.subtract)
                    ins = v.tensor_tensor(o2, o2, g2, ALU.add)
                    inc(ins, "h_rb", 1)
                    wge(v, "h_rb")
                    ins = v.tensor_copy(sgst[:, D + (ks % 2) * 64:D + (ks % 2) * 64 + 64],
                                        f2[:, (ks % 2) * 64:(ks % 2) * 64 + 64])
                    inc(ins, "h_bf", 1)


    BAR()
    if stop == 'B3b':
        es.close()
        return nc

    if stop == 'B4':
        es.close()
        return nc

    O_KH, O_VA, O_KRT, O_PST, O_QCH, O_WQC, O_QLM, O_QRH = \
        0, 16384, 33280, 41472, 42496, 43520, 46592, 48640

    sign_jobs = []
    for mat, src_t, dst_t in (("g", wgT, wg_sc), ("u", wuT, wu_sc), ("d", wdT, wd_sc)):
        nr = 16 if mat != "d" else 64
        ncc = 4 if mat != "d" else 1
        for r in range(nr):
            for c in range(ncc):
                sign_jobs.append((mat, src_t, dst_t, r, c))
    assert len(sign_jobs) == 192

    NKS = (NKA * 4, NKB * 4)
    DIAG0 = (28, 60)
    SPH = NKS[0] + NKS[1]

    def pv_groups_before(h, slot):
        return h * SPH + (NKS[0] if slot == 1 else 0)

    with nc.Block() as blk:
        @blk.sync
        def _(sy):
            ins = sy.dma_start(out=ar[0:64, O_KRT:O_KRT + 8192], in_=kr_sc)
            inc(ins, "j_kr", 16)
            for h in range(16):
                buf = h % 2
                if h >= 2:
                    wge(sy, "j_st", SPH * (h - 1))
                ins = sy.dma_start(out=A(O_KH + buf * 8192, 8192),
                                   in_=k_sc[h * 128:(h + 1) * 128, :])
                inc(ins, "j_k%d" % buf, 16)
                ins = sy.dma_start(
                    out=ar[0:64, O_QRH + buf * 1024:O_QRH + (buf + 1) * 1024],
                    in_=bass.AP(qr_sc.tensor, h * 1024, [[H * 1024, 64], [1, 1024]]))
                inc(ins, "j_qr%d" % buf, 16)
                if h >= 2:
                    wge(sy, "j_pvall", SPH * (h - 1))
                ins = sy.dma_start(
                    out=bass.AP(ar, O_VA + buf * 8448,
                                [[ARN, 128], [VW, 64], [1, 128]]),
                    in_=bass.AP(v_sc.tensor, h * 128,
                                [[H * DC, 128], [128 * H * DC, 64], [1, 128]]))
                inc(ins, "j_v%d" % buf, 16)
                for slot in range(2):
                    m = h * 2 + slot
                    if m >= 2:
                        wge(sy, "j_oxa", 4 * (m - 1))
                    ins = sy.dma_start(
                        out=bass.AP(xout, (m % 2) * 512, [[4 * D, 128], [128, 4], [1, 128]]),
                        in_=bass.AP(x_own.tensor, slot * 512 * D + h * 128,
                                    [[D, 128], [128 * D, 4], [1, 128]]))
                    inc(ins, "j_xsl%d" % (m % 2), 16)
                for lq in range(12):
                    i2 = h * 12 + lq
                    if i2 >= 2:
                        wge(sy, "j_qc", i2 - 1)
                    ins = sy.dma_start(
                        out=A(O_QLM + (lq % 2) * 1024, 1024),
                        in_=bass.AP(ql_sc.tensor, lq * 128 * 1024,
                                    [[1024, 128], [1, 1024]]))
                    inc(ins, "j_ql%d" % (lq % 2), 16)
                for sj in range(12):
                    idx = h * 12 + sj
                    mat, src_t, dst_t, r, c = sign_jobs[idx]
                    if idx >= 2:
                        wge(sy, "j_sg", idx - 1)
                    ins = sy.dma_start(out=(fst if idx % 2 == 0 else f2)[:, :],
                                       in_=src_t[r * 128:(r + 1) * 128,
                                                 c * 2048:(c + 1) * 2048])
                    inc(ins, "j_sgld%d" % (idx % 2), 16)

        @blk.gpsimd
        def _(gp):
            for h in range(16):
                buf = h % 2
                if h >= 2:
                    wge(gp, "j_qcb", h - 1)
                ins = gp.dma_start(
                    out=AAP(O_WQC + buf * 1536, [[128, 12], [1, 128]]),
                    in_=bass.AP(wq_uc.tensor, h * 128,
                                [[2048, 128], [128 * 2048, 12], [1, 128]]))
                inc(ins, "j_wqc%d" % buf, 16)
                for slot in range(2):
                    for qs in range(4):
                        k = h * 8 + slot * 4 + qs
                        wge(gp, "j_oxa", k + 1)
                        rows = slot * 512 + qs * 128
                        ins = gp.dma_start(
                            out=x1_sc[rows:rows + 128, h * 128:(h + 1) * 128],
                            in_=ost[:, (k % 2) * 128:(k % 2) * 128 + 128])
                        inc(ins, "j_acc%d" % (k % 2), 16)
            wge(gp, "j_acc0", 16 * 64)
            wge(gp, "j_acc1", 16 * 64)

        @blk.tensor
        def _(t):
            wge(t, "j_kr", 16)
            wge(t, "j_vone", 1)
            gidx = 0
            tri_n = 0
            for h in range(16):
                buf = h % 2
                wge(t, "j_wqc%d" % buf, 16 * (h // 2 + 1))
                if h >= 1:
                    wge(t, "j_qcb", h)
                for lq in range(12):
                    wge(t, "j_ql%d" % (lq % 2), 16 * ((h * 12 + lq) // 2 + 1))
                    for qh in range(2):
                        ps = psG if qh == 0 else psU
                        ins = t.matmul(ps[:, :],
                                       A(O_WQC + buf * 1536 + lq * 128, 128),
                                       A(O_QLM + (lq % 2) * 1024 + qh * 512, 512),
                                       start=(lq == 0), stop=(lq == 11))
                    inc(ins, "j_qc", 1)
                wge(t, "j_k%d" % buf, 16 * (h // 2 + 1))
                wge(t, "j_qr%d" % buf, 16 * (h // 2 + 1))
                wge(t, "j_v%d" % buf, 16 * (h // 2 + 1))
                wge(t, "j_qcb", h + 1)
                for slot in range(2):
                    nks = NKS[slot]
                    qc0 = slot * 512
                    d0 = DIAG0[slot]
                    base_pv = pv_groups_before(h, slot)
                    for ks in range(nks + 1):
                        if ks < nks:
                            gidx += 1
                            if gidx >= 3:
                                wge(t, "j_ex", gidx - 2)
                            ps = psA if ks % 2 == 0 else psB
                            t.matmul(ps[:, :],
                                     A(O_KH + buf * 8192 + ks * 128, 128),
                                     A(O_QCH + qc0, 512),
                                     start=True, stop=False)
                            ins = t.matmul(ps[:, :],
                                           bass.AP(ar, O_KRT + ks * 128,
                                                   [[ARN, 64], [1, 128]]),
                                           bass.AP(ar, O_QRH + buf * 1024 + qc0,
                                                   [[ARN, 64], [1, 512]]),
                                           start=False, stop=True)
                            inc(ins, "j_st", 1)
                        if ks >= 1:
                            pk = ks - 1
                            exp_n = gidx if ks == nks else gidx - 1
                            wge(t, "j_ex", exp_n)
                            if pk >= d0:
                                wge(t, "j_tri", tri_n + (pk - d0) + 1)
                            for qs in range(4):
                                ins = t.matmul(
                                    psO[:, qs * 512:qs * 512 + 129],
                                    A(O_PST + (pk % 2) * 512 + qs * 128, 128),
                                    A(O_VA + buf * 8448 + pk * VW, 129),
                                    start=(pk == 0), stop=(pk == nks - 1))
                            inc(ins, "j_pvall", 1)
                    tri_n += 4
                    wge(t, "j_oev", h * 8 + slot * 4 + 4)

        @blk.scalar
        def _(sc):
            gidx = 0
            sgi = 0
            for h in range(16):
                for qh in range(2):
                    ps = psG if qh == 0 else psU
                    wge(sc, "j_qc", h * 12 + 12)
                    if h >= 1 and qh == 0:
                        wge(sc, "j_st", SPH * h)
                    ins = sc.activation(A(O_QCH + qh * 512, 512), ps[:, :], AF.Identity,
                                        scale=SCALE, bias=bias_qc[:, h:h + 1])
                inc(ins, "j_qcb", 1)
                for slot in range(2):
                    nks = NKS[slot]
                    base_pv = pv_groups_before(h, slot)
                    for ks in range(nks):
                        gidx += 1
                        wge(sc, "j_st", gidx)
                        if ks >= 2:
                            wge(sc, "j_pvall", base_pv + ks - 1)
                        elif base_pv >= 1:
                            wge(sc, "j_pvall", base_pv)
                        ps = psA if ks % 2 == 0 else psB
                        mcol = ks if slot == 0 else 32 + ks
                        ins = sc.activation(A(O_PST + (ks % 2) * 512, 512), ps[:, :],
                                            AF.Exp, bias=mb_sb[:, mcol:mcol + 1])
                        inc(ins, "j_ex", 1)
                    for qs in range(4):
                        k = h * 8 + slot * 4 + qs
                        wge(sc, "j_lr", k + 1)
                        if k >= 2:
                            wge(sc, "j_acc%d" % (k % 2), 16 * (k // 2))
                        ins = sc.activation(ost[:, (k % 2) * 128:(k % 2) * 128 + 128],
                                            psO[:, qs * 512:qs * 512 + 128], AF.Identity,
                                            scale=lrec[:, 0:1])
                        inc(ins, "j_oev", 1)
                    for _ in range(6):
                        idx = sgi
                        sgi += 1
                        mat, src_t, dst_t, r, c = sign_jobs[idx]
                        wge(sc, "j_sgld%d" % (idx % 2), 16 * (idx // 2 + 1))
                        if idx >= 2:
                            wge(sc, "j_sgst%d" % (idx % 2), 16 * (idx // 2))
                        ins = sc.activation(sgst[:, (idx % 2) * D:(idx % 2 + 1) * D],
                                            (fst if idx % 2 == 0 else f2)[:, :], AF.Sign)
                        inc(ins, "j_sg", 1)
                        if mat == "d":
                            dcount = sum(1 for q2 in range(idx + 1)
                                         if sign_jobs[q2][0] == "d")
                            wge(sc, "j_gbm", dcount)
                        else:
                            wge(sc, "j_sg")
                        ins = sc.dma_start(
                            out=dst_t[r * 128:(r + 1) * 128, c * 2048:(c + 1) * 2048],
                            in_=sgst[:, (idx % 2) * D:(idx % 2 + 1) * D])
                        inc(ins, "j_sgst%d" % (idx % 2), 16)
            wge(sc, "j_sgst0", 16 * 96)
            wge(sc, "j_sgst1", 16 * 96)

        @blk.vector
        def _(v):
            ins = v.memset(bass.AP(ar, O_VA + 128, [[ARN, 128], [VW, 128]]), 1.0)
            inc(ins, "j_vone", 1)
            gidx = 0
            sgi = 0
            for h in range(16):
                for slot in range(2):
                    nks = NKS[slot]
                    d0 = DIAG0[slot]
                    base_pv = pv_groups_before(h, slot)
                    for ks in range(nks):
                        gidx += 1
                        if ks >= d0:
                            j = ks - d0
                            wge(v, "j_ex", gidx)
                            ins = v.tensor_mul(A(O_PST + (ks % 2) * 512, 512),
                                               A(O_PST + (ks % 2) * 512, 512),
                                               tri_sb[:, j * 512:(j + 1) * 512])
                            inc(ins, "j_tri", 1)
                    m = h * 2 + slot
                    for qs in range(4):
                        k = h * 8 + slot * 4 + qs
                        if qs == 0:
                            wge(v, "j_pvall", base_pv + nks)
                        if k >= 1:
                            wge(v, "j_oev", k)
                        ins = v.reciprocal(lrec[:, 0:1],
                                           psO[:, qs * 512 + 128:qs * 512 + 129])
                        inc(ins, "j_lr", 1)
                        wge(v, "j_oev", k + 1)
                        wge(v, "j_xsl%d" % (m % 2), 16 * (m // 2 + 1))
                        ins = v.tensor_tensor(
                            ost[:, (k % 2) * 128:(k % 2) * 128 + 128],
                            ost[:, (k % 2) * 128:(k % 2) * 128 + 128],
                            bass.AP(xout, (m % 2) * 512 + qs * 128, [[4 * D, 128], [1, 128]]),
                            ALU.add)
                        inc(ins, "j_oxa", 1)
                    for _ in range(6):
                        idx = sgi
                        sgi += 1
                        if sign_jobs[idx][0] == "d":
                            wge(v, "j_sg", idx + 1)
                            ins = v.tensor_mul(sgst[:, (idx % 2) * D:(idx % 2 + 1) * D],
                                               sgst[:, (idx % 2) * D:(idx % 2 + 1) * D],
                                               gb_rep[:, :])
                            inc(ins, "j_gbm", 1)

    BAR()
    if stop == 'B5':
        es.close()
        return nc

    O_XQT, O_PGR, O_WGS, O_WUS, O_WDS, O_GU = 0, 8192, 16384, 20480, 24576, 25600

    with nc.Block() as blk:
        @blk.sync
        def _(sy):
            for half in range(2):
                for tq in range(4):
                    i = half * 4 + tq
                    if half == 1:
                        wge(sy, "m_out", 16 * 4)
                    ins = sy.dma_start(out=xout[:, tq * D:(tq + 1) * D],
                                       in_=x1_sc[i * 128:(i + 1) * 128, :])
                    inc(ins, "m_x1", 16)
                wge(sy, "m_xq", 16 * 4 * (half + 1))
                for dc in range(16):
                    ins = sy.dma_start(
                        out=A(O_XQT + dc * 512, 512),
                        in_=xq_sc[half * 512:(half + 1) * 512, dc * 128:(dc + 1) * 128],
                        transpose=True)
                    inc(ins, "m_xqt", 16)
                for hg in range(4):
                    for hs in range(16):
                        habs = hg * 16 + hs
                        i = half * 64 + habs
                        if i >= 2:
                            wge(sy, "m_p", i - 1)
                        ins = sy.dma_start(
                            out=AAP(O_WGS + (i % 2) * 2048, [[128, 16], [1, 128]]),
                            in_=bass.AP(wg_sc.tensor, habs * 128,
                                        [[FH, 128], [128 * FH, 16], [1, 128]]))
                        inc(ins, "m_wg%d" % (i % 2), 16)
                        ins = sy.dma_start(
                            out=AAP(O_WUS + (i % 2) * 2048, [[128, 16], [1, 128]]),
                            in_=bass.AP(wu_sc.tensor, habs * 128,
                                        [[FH, 128], [128 * FH, 16], [1, 128]]))
                        inc(ins, "m_wu%d" % (i % 2), 16)
                    for dq in range(4):
                        for hs in range(16):
                            habs = hg * 16 + hs
                            i = ((half * 4 + hg) * 4 + dq) * 16 + hs
                            if i >= 2:
                                j = i - 2
                                if j % 16 == 15:
                                    wge(sy, "m_pvd", j // 16 + 1)
                                else:
                                    wge(sy, "m_dn", 15 * (j // 16) + j % 16 + 1)
                            ins = sy.dma_start(
                                out=A(O_WDS + (i % 2) * 512, 512),
                                in_=wd_sc[habs * 128:(habs + 1) * 128,
                                          dq * 512:(dq + 1) * 512])
                            inc(ins, "m_wd%d" % (i % 2), 16)
                for tq in range(4):
                    i = half * 4 + tq
                    wge(sy, "m_fl", (half + 1) * 16 * 4)
                    ins = sy.dma_start(out=out_own[i * 128:(i + 1) * 128, :],
                                       in_=xout[:, tq * D:(tq + 1) * D])
                    inc(ins, "m_out", 16)
            wge(sy, "m_out", 16 * 8)

        @blk.vector
        def _(v):
            for half in range(2):
                for tq in range(4):
                    i = half * 4 + tq
                    wge(v, "m_x1", 16 * 4 * (half + 1))
                    if i >= 1:
                        wge(v, "m_ln1", i)
                    for j in range(4):
                        ins = v.bn_stats(bn24[:, j * 6:(j + 1) * 6],
                                         xout[:, tq * D + j * 512:tq * D + (j + 1) * 512])
                    inc(ins, "m_bn", 1)
                    wge(v, "m_bn")
                    ins = v.bn_aggr(st2[:, :], bn24[:, :])
                    inc(ins, "m_ag", 1)
                    wge(v, "m_rs1", i + 1)
                    if i >= 1:
                        wge(v, "m_xn1", i)
                    ins = v.tensor_scalar(nmean[:, :], st2[:, 0:1], rstd[:, 0:1], -1.0,
                                          ALU.mult, ALU.mult)
                    inc(ins, "m_nm", 1)
                    wge(v, "m_xn1", i + 1)
                    ins = v.tensor_tensor(f2[:, :], fst[:, :], cb1_rep[:, :], ALU.mult)
                    inc(ins, "m_h2a", 1)
                    wge(v, "m_h2a")
                    ins = v.tensor_tensor(f2[:, :], f2[:, :], sb_rep[:, :], ALU.add)
                    inc(ins, "m_h2", 1)
                    wge(v, "m_h2")
                    if i >= 1:
                        wge(v, "m_ln2", i)
                    for j in range(4):
                        ins = v.bn_stats(bn24[:, j * 6:(j + 1) * 6],
                                         f2[:, j * 512:(j + 1) * 512])
                    inc(ins, "m_bn2", 1)
                    wge(v, "m_bn2")
                    ins = v.bn_aggr(st2b[:, :], bn24[:, :])
                    inc(ins, "m_ag2", 1)
                    wge(v, "m_rs2", i + 1)
                    if i >= 1:
                        wge(v, "m_y0", i)
                    ins = v.tensor_scalar(nm2[:, :], st2b[:, 0:1], rstd2[:, 0:1], -1.0,
                                          ALU.mult, ALU.mult)
                    inc(ins, "m_nm2", 1)
                    wge(v, "m_y0", i + 1)
                    ins = v.tensor_reduce(amax_t[:, :], fst[:, :], mybir.AxisListType.X,
                                          ALU.max, apply_absolute_value=True)
                    inc(ins, "m_am", 1)
                    wge(v, "m_am")
                    ins = v.tensor_scalar_mul(srec[:, :], amax_t[:, :], 1.0 / 127.0)
                    inc(ins, "m_sr", 1)
                    wge(v, "m_sr")
                    ins = v.reciprocal(sc_ap[:, :], amax_t[:, :])
                    inc(ins, "m_rc2", 1)
                    wge(v, "m_rc2")
                    ins = v.tensor_scalar(fst[:, :], fst[:, :], sc_ap[:, 0:1], 127.0,
                                          ALU.mult, ALU.mult)
                    inc(ins, "m_ys", 1)
                    wge(v, "m_ys")
                    for hh in range(2):
                        c0, c1 = hh * 1024, (1 - hh) * 1024
                        if hh == 1:
                            wge(v, "m_xqv", 2 * i + 1)
                        ins = v.tensor_copy(ti32[:, :], fst[:, c0:c0 + 1024])
                        inc(ins, "m_rnd", 1)
                        wge(v, "m_rnd")
                        ins = v.tensor_copy(f2[:, c0:c0 + 1024], ti32[:, :])
                        inc(ins, "m_rnd", 1)
                        wge(v, "m_rnd")
                        ins = v.tensor_tensor(fst[:, c0:c0 + 1024], fst[:, c0:c0 + 1024],
                                              f2[:, c0:c0 + 1024], ALU.subtract)
                        inc(ins, "m_rnd", 1)
                        wge(v, "m_rnd")
                        for rung in range(4):
                            thr = rung + 0.5
                            ins = v.tensor_scalar(f2[:, c1:c1 + 1024], fst[:, c0:c0 + 1024],
                                                  thr, None, ALU.is_ge)
                            inc(ins, "m_rnd", 1)
                            wge(v, "m_rnd")
                            ins = v.tensor_tensor(f2[:, c0:c0 + 1024], f2[:, c0:c0 + 1024],
                                                  f2[:, c1:c1 + 1024], ALU.add)
                            inc(ins, "m_rnd", 1)
                            wge(v, "m_rnd")
                            ins = v.tensor_scalar(f2[:, c1:c1 + 1024], fst[:, c0:c0 + 1024],
                                                  -thr, None, ALU.is_le)
                            inc(ins, "m_rnd", 1)
                            wge(v, "m_rnd")
                            ins = v.tensor_tensor(f2[:, c0:c0 + 1024], f2[:, c0:c0 + 1024],
                                                  f2[:, c1:c1 + 1024], ALU.subtract)
                            inc(ins, "m_rnd", 1)
                            wge(v, "m_rnd")
                        if i >= 1:
                            wge(v, "m_xq", 16 * i)
                        ins = v.tensor_scalar_mul(sgst[:, D + c0:D + c0 + 1024],
                                                  f2[:, c0:c0 + 1024], srec[:, 0:1])
                        inc(ins, "m_xqv", 1)
                for hg in range(4):
                    g = half * 4 + hg
                    for hs in range(16):
                        i = half * 64 + hg * 16 + hs
                        wge(v, "m_u", i + 1)
                        if hs == 0 and g >= 1:
                            wge(v, "m_pvd", 4 * g)
                        if i >= 1:
                            wge(v, "m_p", i)
                        ins = v.tensor_copy(A(O_GU + 512, 512), psU[:, :])
                        inc(ins, "m_ue", 1)
                        wge(v, "m_ge", i + 1)
                        wge(v, "m_ue")
                        ins = v.tensor_mul(A(O_PGR + hs * 512, 512), A(O_GU, 512),
                                           A(O_GU + 512, 512))
                        inc(ins, "m_p", 1)
                    for dq in range(4):
                        gi = half * 16 + hg * 4 + dq
                        wge(v, "m_pvd", gi + 1)
                        for ss in range(4):
                            ins = v.tensor_tensor(
                                xout[:, ss * D + dq * 512:ss * D + (dq + 1) * 512],
                                xout[:, ss * D + dq * 512:ss * D + (dq + 1) * 512],
                                psO[:, ss * 512:(ss + 1) * 512], ALU.add)
                            inc(ins, "m_fl", 1)

        @blk.scalar
        def _(sc):
            for half in range(2):
                for tq in range(4):
                    i = half * 4 + tq
                    wge(sc, "m_ag", i + 1)
                    ins = sc.activation(rstd[:, :], st2[:, 1:2], AF.Ln, bias=cst[:, 0:1])
                    inc(ins, "m_ln1", 1)
                    wge(sc, "m_ln1")
                    ins = sc.activation(rstd[:, :], rstd[:, :], AF.Exp, scale=-0.5)
                    inc(ins, "m_rs1", 1)
                    wge(sc, "m_nm", i + 1)
                    ins = sc.activation(fst[:, :], xout[:, tq * D:(tq + 1) * D], AF.Identity,
                                        scale=rstd[:, 0:1], bias=nmean[:, 0:1])
                    inc(ins, "m_xn1", 1)
                    wge(sc, "m_ag2", i + 1)
                    ins = sc.activation(rstd2[:, :], st2b[:, 1:2], AF.Ln, bias=cst[:, 1:2])
                    inc(ins, "m_ln2", 1)
                    wge(sc, "m_ln2")
                    ins = sc.activation(rstd2[:, :], rstd2[:, :], AF.Exp, scale=-0.5)
                    inc(ins, "m_rs2", 1)
                    wge(sc, "m_nm2", i + 1)
                    ins = sc.activation(fst[:, :], f2[:, :], AF.Identity,
                                        scale=rstd2[:, 0:1], bias=nm2[:, 0:1])
                    inc(ins, "m_y0", 1)
                    wge(sc, "m_xqv", 2 * (i + 1))
                    ins = sc.dma_start(out=xq_sc[i * 128:(i + 1) * 128, :],
                                       in_=sgst[:, D:2 * D])
                    inc(ins, "m_xq", 16)
                for hg in range(4):
                    for hs in range(16):
                        i = half * 64 + hg * 16 + hs
                        wge(sc, "m_g", i + 1)
                        if i >= 1:
                            wge(sc, "m_p", i)
                        ins = sc.activation(A(O_GU, 512), psG[:, :], AF.Silu)
                        inc(ins, "m_ge", 1)

        @blk.tensor
        def _(t):
            for half in range(2):
                wge(t, "m_xqt", 16 * 16 * (half + 1))
                for hg in range(4):
                    for hs in range(16):
                        i = half * 64 + hg * 16 + hs
                        wge(t, "m_wg%d" % (i % 2), 16 * (i // 2 + 1))
                        wge(t, "m_wu%d" % (i % 2), 16 * (i // 2 + 1))
                        if i >= 1:
                            wge(t, "m_ge", i)
                            wge(t, "m_ue", i)
                        for dc in range(16):
                            ins = t.matmul(psG[:, :],
                                           A(O_WGS + (i % 2) * 2048 + dc * 128, 128),
                                           A(O_XQT + dc * 512, 512),
                                           start=(dc == 0), stop=(dc == 15))
                        inc(ins, "m_g", 1)
                        for dc in range(16):
                            ins = t.matmul(psU[:, :],
                                           A(O_WUS + (i % 2) * 2048 + dc * 128, 128),
                                           A(O_XQT + dc * 512, 512),
                                           start=(dc == 0), stop=(dc == 15))
                        inc(ins, "m_u", 1)
                    for dq in range(4):
                        gi = half * 16 + hg * 4 + dq
                        if gi >= 1:
                            wge(t, "m_fl", gi * 4)
                        wge(t, "m_p", (half * 4 + hg) * 16 + 16)
                        for hs in range(16):
                            i = ((half * 4 + hg) * 4 + dq) * 16 + hs
                            wge(t, "m_wd%d" % (i % 2), 16 * (i // 2 + 1))
                            for ss in range(4):
                                ins = t.matmul(psO[:, ss * 512:(ss + 1) * 512],
                                               AAP(O_PGR + hs * 512 + ss * 128, [[1, 128]]),
                                               A(O_WDS + (i % 2) * 512, 512),
                                               start=(hs == 0), stop=(hs == 15))
                            if hs == 15:
                                inc(ins, "m_pvd", 1)
                            else:
                                inc(ins, "m_dn", 1)

    BAR()
    if stop == 'X14':
        es.close()
        return nc
    es.close()
    return nc


def kernel(**inputs):
    in_maps, plans = _host_prep(inputs)
    nc = build_program()
    res = run_bass_kernel_spmd(nc, in_maps, core_ids=list(range(8)))
    out = np.empty((S, D), np.float32)
    for core in range(8):
        a, b = plans[core]
        oo = res.results[core]["out_own"]
        out[a * CH:(a + 1) * CH] = oo[0:CH]
        out[b * CH:(b + 1) * CH] = oo[CH:2 * CH]
    return out.reshape(1, S, D)


# version 05
# speedup vs baseline: 1.0110x; 1.0110x over previous
import contextlib
import math

import numpy as np
import ml_dtypes

import concourse.bass as bass
import concourse.mybir as mybir
from concourse.bass_utils import run_bass_kernel_spmd

F32 = mybir.dt.float32
BF16 = mybir.dt.bfloat16
I32 = mybir.dt.int32
AF = mybir.ActivationFunctionType
ALU = mybir.AluOpType
BF = ml_dtypes.bfloat16

S, D, H, DC, DP, LQ, LKV, FH = 8192, 2048, 16, 128, 64, 1536, 512, 8192
NCH, CH = 16, 512
SCALE = 1.0 / math.sqrt(DC + DP)
EPS_LN, EPS_RMS = 1e-5, 1e-8
NEG = -30000.0
RH = DP // 2
NKA, NKB = 8, 16
DIAG_A, DIAG_B = 7, 15
VW = 132



def _host_prep(inputs):
    x = np.ascontiguousarray(np.asarray(inputs["L0_attn_x"], np.float32).reshape(S, D))
    xc = x.reshape(NCH, CH, D)
    import jax
    import jax.numpy as jnp
    _cpu = jax.devices("cpu")[0]
    with jax.default_device(_cpu):
        _invf = 1.0 / 10000.0 ** (jnp.arange(RH, dtype=jnp.float32) / RH)

    def g(k):
        return np.ascontiguousarray(np.asarray(inputs[k], np.float32))

    shared = {
        "wq_d": g("L0_attn_Wq_d"), "wq_uc": g("L0_attn_Wq_uc"), "wq_up": g("L0_attn_Wq_up"),
        "wkv_d": g("L0_attn_Wkv_d"), "wk_uc": g("L0_attn_Wk_uc"), "wk_up": g("L0_attn_Wk_up"),
        "wv_u": g("L0_attn_Wv_u"),
        "bq_d": g("L0_attn_bq_d"), "bq_uc": g("L0_attn_bq_uc"),
        "bq_up": g("L0_attn_bq_up").reshape(1, H * DP),
        "bkv_d": g("L0_attn_bkv_d"), "bk_uc": g("L0_attn_bk_uc"),
        "bk_up": g("L0_attn_bk_up").reshape(1, DP),
        "bv_u": g("L0_attn_bv_u").reshape(1, H * DC),
        "wgT": np.ascontiguousarray(np.asarray(inputs["L0_ffn_W_g"], np.float32).T),
        "wuT": np.ascontiguousarray(np.asarray(inputs["L0_ffn_W_u"], np.float32).T),
        "wdT": np.ascontiguousarray(np.asarray(inputs["L0_ffn_W_d"], np.float32).T),
        "cvec": g("L0_c"), "wmod": g("L0_wmod"),
    }

    in_maps, plans = [], []
    for core in range(8):
        a, b = core, 15 - core
        others = [c for c in range(NCH) if c not in (a, b)]
        slots = others[:7] + [a] + others[7:] + [b]
        perm = np.asarray(slots)

        x_kv = np.ascontiguousarray(xc[perm].reshape(S, D))
        x_own = np.ascontiguousarray(np.concatenate([xc[a], xc[b]], axis=0))

        ppos = (perm[:, None] * CH + np.arange(CH)[None, :]).reshape(S)
        with jax.default_device(_cpu):
            _ang = jnp.asarray(ppos)[:, None].astype(jnp.float32) * _invf[None, :]
            _cs = np.asarray(jnp.cos(_ang))
            _sn = np.asarray(jnp.sin(_ang))
        rope = np.ascontiguousarray(np.concatenate([_cs, _sn], axis=1).astype(np.float32))

        mb = np.zeros((128, (NKA + NKB) * 4), dtype=np.float32)
        col = 0
        for qchunk, nslots, diag in ((a, NKA, DIAG_A), (b, NKB, DIAG_B)):
            for sl in range(nslots):
                keep = (sl == diag) or (slots[sl] < qchunk)
                mb[:, col:col + 4] = 0.0 if keep else NEG
                col += 4

        m = dict(shared)
        m.update({"x_own": x_own, "x_kv": x_kv, "rope_cs": rope,
                  "mbias": np.ascontiguousarray(mb)})
        in_maps.append(m)
        plans.append((a, b))
    return in_maps, plans


def _tri_masks():
    t = np.zeros((4 * 128, CH), dtype=BF)
    for j in range(4):
        kk = j * 128 + np.arange(128)
        t[j * 128:(j + 1) * 128] = (kk[:, None] <= np.arange(CH)[None, :]).astype(BF)
    return t



def build_program(stop=None):
    nc = bass.Bass("TRN2", target_bir_lowering=False)

    def din(name, shape):
        return nc.dram_tensor(name, shape, F32, kind="ExternalInput").ap()

    x_own = din("x_own", [1024, D])
    x_kv = din("x_kv", [S, D])
    rope_cs = din("rope_cs", [S, 2 * RH])
    mbias = din("mbias", [128, 96])
    wq_d = din("wq_d", [D, LQ]);      wq_uc = din("wq_uc", [LQ, H * DC])
    wq_up = din("wq_up", [LQ, H * DP]); wkv_d = din("wkv_d", [D, LKV])
    wk_uc = din("wk_uc", [LKV, H * DC]); wk_up = din("wk_up", [LKV, DP])
    wv_u = din("wv_u", [LKV, H * DC])
    bq_d = din("bq_d", [LQ]);   bq_uc = din("bq_uc", [H * DC])
    bq_up = din("bq_up", [1, H * DP]); bkv_d = din("bkv_d", [LKV])
    bk_uc = din("bk_uc", [H * DC]); bk_up = din("bk_up", [1, DP])
    bv_u = din("bv_u", [1, H * DC])
    wgT = din("wgT", [D, FH]); wuT = din("wuT", [D, FH]); wdT = din("wdT", [FH, D])
    cvec = din("cvec", [1, D]); wmod = din("wmod", [D, 6 * D])

    out_own = nc.dram_tensor("out_own", [1024, D], F32, kind="ExternalOutput").ap()

    tri_h = nc.inline_tensor(_tri_masks(), name="tri_h").ap()

    xn_sc = nc.dram_tensor("xn_sc", [S, D], BF16).ap()
    k_sc = nc.dram_tensor("k_sc", [H * DC, S], BF16).ap()
    v_sc = nc.dram_tensor("v_sc", [S, H * DC], BF16).ap()
    kr_sc = nc.dram_tensor("kr_sc", [DP, S], BF16).ap()
    ql_sc = nc.dram_tensor("ql_sc", [LQ, 1024], BF16).ap()
    qr_sc = nc.dram_tensor("qr_sc", [DP, H * 1024], BF16).ap()
    wg_sc = nc.dram_tensor("wg_sc", [D, FH], BF16).ap()
    wu_sc = nc.dram_tensor("wu_sc", [D, FH], BF16).ap()
    wd_sc = nc.dram_tensor("wd_sc", [FH, D], BF16).ap()
    xq_sc = nc.dram_tensor("xq_sc", [1024, D], BF16).ap()
    x1_sc = nc.dram_tensor("x1_sc", [1024, D], F32).ap()
    row_sc = nc.dram_tensor("row_sc", [1, 8 * D], F32).ap()

    sbt = nc.alloc_sbuf_tensor
    ARN = 50816
    ar = sbt("arena", [128, ARN], BF16)

    def A(off, ln):
        assert 0 <= off and off + ln <= ARN, (off, ln)
        return ar[:, off:off + ln]

    def AAP(off, pattern, part=128):
        return bass.AP(ar, off, [[ARN, part]] + pattern)

    fst = sbt("fst", [128, D], F32)
    f2 = sbt("f2", [128, D], F32)
    xout = sbt("xout", [128, 4 * D], F32)
    sgst = sbt("sgst", [128, 2 * D], BF16)
    ost = sbt("ost", [128, 2 * DC], F32)
    mrow = sbt("mrow", [1, CH], F32)
    brow = sbt("brow", [1, D], BF16)
    brow2 = sbt("brow2", [1, D], BF16)
    modt = sbt("modt", [128, 96], F32)
    gtmp = sbt("gtmp", [128, 64], F32)
    ca1 = sbt("ca1", [128, 16], F32)
    cb1_rep = sbt("cb1_rep", [128, D], F32)
    sb_rep = sbt("sb_rep", [128, D], F32)
    ga_rep = sbt("ga_rep", [128, D], BF16)
    gb_rep = sbt("gb_rep", [128, D], BF16)
    biasq2 = sbt("biasq2", [128, 12], F32)
    biaskv2 = sbt("biaskv2", [128, 4], F32)
    bias_kc = sbt("bias_kc", [128, 16], F32)
    bias_qc = sbt("bias_qc", [128, 16], F32)
    brow_qup = sbt("brow_qup", [1, H * DP], BF16)
    brow_kup = sbt("brow_kup", [1, DP], BF16)
    brow_vu = sbt("brow_vu", [1, H * DC], BF16)
    ones1 = sbt("ones1", [1, 128], BF16)
    mb_sb = sbt("mb_sb", [128, 96], F32)
    tri_sb = sbt("tri_sb", [128, 4 * CH], BF16)
    bn24 = sbt("bn24", [128, 24], F32)
    st2 = sbt("st2", [128, 2], F32)
    rstd = sbt("rstd", [128, 1], F32)
    nmean = sbt("nmean", [128, 1], F32)
    st2b = sbt("st2b", [128, 2], F32)
    rstd2 = sbt("rstd2", [128, 1], F32)
    amax_t = sbt("amax_t", [128, 1], F32)
    srec = sbt("srec", [128, 1], F32)
    sc_ap = sbt("sc_ap", [128, 1], F32)
    nm2 = sbt("nm2", [128, 1], F32)
    lrec = sbt("lrec", [128, 1], F32)
    rcs = sbt("rcs", [128, 4 * RH], F32)
    cst = sbt("cst", [128, 4], F32)
    sa2b = sbt("sa2b", [128, 16], BF16)
    ti32 = sbt("ti32", [128, 1024], I32)

    psA = nc.alloc_psum_tensor("psA", [128, 512], F32)
    psB = nc.alloc_psum_tensor("psB", [128, 512], F32)
    psO = nc.alloc_psum_tensor("psO", [128, 4 * 512], F32)
    psG = nc.alloc_psum_tensor("psG", [128, 512], F32)
    psU = nc.alloc_psum_tensor("psU", [128, 512], F32)

    es = contextlib.ExitStack()
    es.enter_context(nc.allow_non_contiguous_dma("tiny strided bias/mod-column loads"))

    LONGLIVED = {"a0g", "a2", "g2"}
    SWDGE = {"a0g", "d0", "f0", "g2", "j_wqc0", "j_wqc1", "j_acc0", "j_acc1",
             "h_ks0", "h_ks1", "h_ks2", "h_ks3", "h_vst", "j_sgst0", "j_sgst1"}
    phys = {}
    phys_cnt = {}
    free_pool = []
    alias = {}
    dead = set()
    nphys = [0]

    def _new_phys():
        nm = f"p{nphys[0]}"
        nphys[0] += 1
        phys[nm] = es.enter_context(nc.semaphore(nm))
        phys_cnt[nm] = 0
        return nm

    def sem(name):
        if name in alias:
            return phys[alias[name][0]]
        assert name not in dead, f"logical sem {name} reused after its barrier"
        if name in LONGLIVED or name in SWDGE:
            pn = _new_phys()
        elif free_pool:
            pn = free_pool.pop()
        else:
            pn = _new_phys()
        alias[name] = (pn, phys_cnt[pn])
        return phys[pn]

    cnt = {}
    first_inc = set()

    def inc(ins, name, k=1):
        h = sem(name)
        pn, base = alias[name]
        if name not in first_inc:
            first_inc.add(name)
            if base > 0:
                ins._wait_ge(h, base)
        ins.then_inc(h, k)
        phys_cnt[pn] += k
        cnt[name] = cnt.get(name, 0) + k
        return cnt[name]

    def tot(name):
        return cnt.get(name, 0)

    def wge(eng, name, val=None):
        h = sem(name)
        _, base = alias[name]
        eng.wait_ge(h, base + (tot(name) if val is None else val))

    def BAR():
        nc.all_engine_barrier()
        for name, (pn, _) in list(alias.items()):
            if name in LONGLIVED or name in SWDGE:
                continue
            del alias[name]
            dead.add(name)
            if pn not in free_pool:
                free_pool.append(pn)

    with nc.Block() as blk:
        @blk.sync
        def _(sy):
            for o_, i_ in (
                (fst[0:1, 0:D], cvec),
                (biasq2[:, :], bass.AP(bq_d.tensor, 0, [[1, 128], [128, 12]])),
                (biaskv2[:, :], bass.AP(bkv_d.tensor, 0, [[1, 128], [128, 4]])),
                (bias_kc[:, :], bass.AP(bk_uc.tensor, 0, [[1, 128], [128, 16]])),
                (bias_qc[:, :], bass.AP(bq_uc.tensor, 0, [[1, 128], [128, 16]])),
                (mb_sb[:, :], mbias),
                (tri_sb[:, :], bass.AP(tri_h.tensor, 0, [[512, 128], [65536, 4], [1, 512]])),
            ):
                ins = sy.dma_start(out=o_, in_=i_)
                inc(ins, "a0", 16)

        @blk.vector
        def _(v):
            v.memset(ones1[:, :], 1.0)
            v.memset(cst[:, 0:1], EPS_LN)
            v.memset(cst[:, 1:2], EPS_RMS)
            v.memset(cst[:, 2:3], 1.0)

        @blk.gpsimd
        def _(gp):
            ins = gp.dma_start(out=brow_qup[:, :], in_=bq_up)
            inc(ins, "a0g", 16)
            ins = gp.dma_start(out=brow_kup[:, :], in_=bk_up)
            inc(ins, "a0g", 16)

        @blk.scalar
        def _(sc):
            wge(sc, "a0")
            ins = sc.activation(f2[0:1, 0:D], fst[0:1, 0:D], AF.Silu)
            inc(ins, "a1", 1)
            wge(sc, "a1")
            ins = sc.dma_start(out=row_sc[0:1, 0:D], in_=f2[0:1, 0:D])
            inc(ins, "a2", 16)

    BAR()
    if stop == 'B0a':
        es.close()
        return nc

    with nc.Block() as blk:
        @blk.sync
        def _(sy):
            wge(sy, "a2", 16)
            ins = sy.dma_start(out=modt[:, 0:16],
                               in_=bass.AP(row_sc.tensor, 0, [[1, 128], [128, 16]]))
            inc(ins, "a3", 16)
            for cg in range(24):
                for sub in range(8):
                    j = cg * 8 + sub
                    if j >= 2:
                        wge(sy, "modmm", j - 1)
                    ins = sy.dma_start(
                        out=(fst if j % 2 == 0 else f2)[:, 0:1024],
                        in_=bass.AP(wmod.tensor, 2 * sub * 128 * 6 * D + cg * 512,
                                    [[6 * D, 128], [128 * 6 * D, 2], [1, 512]]))
                    inc(ins, "modld%d" % (j % 2), 16)

        @blk.tensor
        def _(t):
            wge(t, "a3", 16)
            for cg in range(24):
                for sub in range(8):
                    j = cg * 8 + sub
                    wge(t, "modld%d" % (j % 2), 16 * (j // 2 + 1))
                    if cg >= 1 and sub == 0:
                        wge(t, "modcp", cg)
                    st = fst if j % 2 == 0 else f2
                    for dc in range(2):
                        ins = t.matmul(psA[0:1, :], modt[:, 2 * sub + dc:2 * sub + dc + 1],
                                       st[:, dc * 512:(dc + 1) * 512],
                                       start=(sub == 0 and dc == 0),
                                       stop=(sub == 7 and dc == 1))
                    inc(ins, "modmm", 1)

        @blk.scalar
        def _(sc):
            for cg in range(24):
                wge(sc, "modmm", 8 * (cg + 1))
                if cg >= 1:
                    wge(sc, "modst", 16 * cg)
                ins = sc.copy(mrow[:, :], psA[0:1, :])
                inc(ins, "modcp", 1)
                wge(sc, "modcp")
                ins = sc.dma_start(out=row_sc[0:1, cg * 512:(cg + 1) * 512], in_=mrow[:, :])
                inc(ins, "modst", 16)
            wge(sc, "modst")

    BAR()
    if stop == 'B0b':
        es.close()
        return nc

    with nc.Block() as blk:
        @blk.sync
        def _(sy):
            ins = sy.dma_start(out=modt[:, 0:96],
                               in_=bass.AP(row_sc.tensor, 0, [[1, 128], [128, 96]]))
            inc(ins, "c0", 16)

        @blk.scalar
        def _(sc):
            wge(sc, "c0", 16)
            sc.activation(gtmp[:, 0:16], modt[:, 32:48], AF.Abs)
            sc.activation(gtmp[:, 16:32], modt[:, 80:96], AF.Abs)
            sc.activation(gtmp[:, 32:48], modt[:, 32:48], AF.Sign)
            ins = sc.activation(gtmp[:, 48:64], modt[:, 80:96], AF.Sign)
            inc(ins, "c1", 1)

        @blk.vector
        def _(v):
            wge(v, "c1", 1)
            v.tensor_scalar_max(gtmp[:, 0:16], gtmp[:, 0:16], 0.25)
            ins = v.tensor_scalar_max(gtmp[:, 16:32], gtmp[:, 16:32], 0.25)
            inc(ins, "c2", 1)
            wge(v, "c2")
            v.tensor_mul(modt[:, 32:48], gtmp[:, 0:16], gtmp[:, 32:48])
            v.tensor_mul(modt[:, 80:96], gtmp[:, 16:32], gtmp[:, 48:64])
            ins = v.tensor_scalar_add(ca1[:, :], modt[:, 16:32], 1.0)
            inc(ins, "c3", 1)

        @blk.scalar
        def _(sc):
            wge(sc, "c3", 1)
            ins = sc.dma_start(out=bass.AP(row_sc.tensor, 2 * D, [[1, 128], [128, 16]]),
                                in_=modt[:, 32:48])
            inc(ins, "c4", 16)
            ins = sc.dma_start(out=bass.AP(row_sc.tensor, 5 * D, [[1, 128], [128, 16]]),
                               in_=modt[:, 80:96])
            inc(ins, "c4", 16)
            wge(sc, "c4")

    BAR()
    if stop == 'B0c':
        es.close()
        return nc

    with nc.Block() as blk:
        @blk.gpsimd
        def _(gp):
            for o_, i_ in (
                (brow[0:1, :], bass.AP(row_sc.tensor, 4 * D, [[1, 1], [1, D]])),
                (sgst[0:1, 0:D], bass.AP(row_sc.tensor, 3 * D, [[1, 1], [1, D]])),
                (sgst[0:1, D:2 * D], bass.AP(row_sc.tensor, 2 * D, [[1, 1], [1, D]])),
                (brow2[0:1, :], bass.AP(row_sc.tensor, 5 * D, [[1, 1], [1, D]])),
                (brow_vu[0:1, :], bv_u),
            ):
                ins = gp.dma_start(out=o_, in_=i_)
                inc(ins, "d0", 16)

        @blk.tensor
        def _(t):
            wge(t, "d0")
            for j in range(4):
                if j >= 1:
                    wge(t, "d2", j)
                t.matmul(psA[:, :], ones1[:, :], brow[0:1, j * 512:(j + 1) * 512],
                         start=True, stop=True)
                t.matmul(psB[:, :], ones1[:, :], sgst[0:1, j * 512:(j + 1) * 512],
                         start=True, stop=True)
                t.matmul(psG[:, :], ones1[:, :], sgst[0:1, D + j * 512:D + (j + 1) * 512],
                         start=True, stop=True)
                ins = t.matmul(psU[:, :], ones1[:, :], brow2[0:1, j * 512:(j + 1) * 512],
                               start=True, stop=True)
                inc(ins, "d1", 1)

        @blk.scalar
        def _(sc):
            for j in range(4):
                wge(sc, "d1", j + 1)
                sc.activation(cb1_rep[:, j * 512:(j + 1) * 512], psA[:, :], AF.Identity, bias=cst[:, 2:3])
                sc.copy(sb_rep[:, j * 512:(j + 1) * 512], psB[:, :])
                sc.copy(ga_rep[:, j * 512:(j + 1) * 512], psG[:, :])
                ins = sc.copy(gb_rep[:, j * 512:(j + 1) * 512], psU[:, :])
                inc(ins, "d2", 1)

        @blk.vector
        def _(v):
            wge(v, "d2", 4)
            ins = v.tensor_mul(brow_vu[0:1, :], brow_vu[0:1, :], ga_rep[0:1, :])
            inc(ins, "d3", 1)

    BAR()
    if stop == 'B0d':
        es.close()
        return nc

    ld0 = fst[:, :]
    ld1 = f2[:, :]

    with nc.Block() as blk:
        @blk.sync
        def _(sy):
            for i in range(64):
                if i >= 2:
                    wge(sy, "b1n", i - 1)
                ins = sy.dma_start(out=(ld0 if i % 2 == 0 else ld1),
                                   in_=x_kv[i * 128:(i + 1) * 128, :])
                inc(ins, "b1x%d" % (i % 2), 16)

        @blk.vector
        def _(v):
            for i in range(64):
                wge(v, "b1x%d" % (i % 2), 16 * (i // 2 + 1))
                if i >= 1:
                    wge(v, "b1ln", i)
                src = ld0 if i % 2 == 0 else ld1
                for j in range(4):
                    ins = v.bn_stats(bn24[:, j * 6:(j + 1) * 6], src[:, j * 512:(j + 1) * 512])
                inc(ins, "b1bn", 1)
                wge(v, "b1bn")
                ins = v.bn_aggr(st2[:, :], bn24[:, :])
                inc(ins, "b1ag", 1)
                wge(v, "b1ex", i + 1)
                if i >= 1:
                    wge(v, "b1n", i)
                ins = v.tensor_scalar(nmean[:, :], st2[:, 0:1], rstd[:, 0:1], -1.0,
                                      ALU.mult, ALU.mult)
                inc(ins, "b1nm", 1)

        @blk.scalar
        def _(sc):
            for i in range(64):
                src = ld0 if i % 2 == 0 else ld1
                wge(sc, "b1ag", i + 1)
                ins = sc.activation(rstd[:, :], st2[:, 1:2], AF.Ln, bias=cst[:, 0:1])
                inc(ins, "b1ln", 1)
                wge(sc, "b1ln")
                ins = sc.activation(rstd[:, :], rstd[:, :], AF.Exp, scale=-0.5)
                inc(ins, "b1ex", 1)
                wge(sc, "b1nm", i + 1)
                if i >= 2:
                    wge(sc, "b1s%d" % (i % 2), 16 * (i // 2))
                ins = sc.activation(sgst[:, (i % 2) * D:(i % 2 + 1) * D], src, AF.Identity,
                                    scale=rstd[:, 0:1], bias=nmean[:, 0:1])
                inc(ins, "b1n", 1)
                wge(sc, "b1n")
                ins = sc.dma_start(out=xn_sc[i * 128:(i + 1) * 128, :],
                                   in_=sgst[:, (i % 2) * D:(i % 2 + 1) * D])
                inc(ins, "b1s%d" % (i % 2), 16)
            wge(sc, "b1s0")
            wge(sc, "b1s1")

    BAR()
    if stop == 'B1':
        es.close()
        return nc

    O_WQD = 0

    with nc.Block() as blk:
        @blk.sync
        def _(sy):
            for dc in range(16):
                if dc >= 2:
                    wge(sy, "e0", dc - 1)
                ins = sy.dma_start(out=(fst if dc % 2 == 0 else f2)[:, 0:1536],
                                   in_=wq_d[dc * 128:(dc + 1) * 128, :])
                inc(ins, "e0ld%d" % (dc % 2), 16)

        @blk.vector
        def _(v):
            ins = v.reciprocal(gtmp[:, 0:16], ca1[:, :])
            inc(ins, "e_sa2r", 1)
            wge(v, "e_sa2r")
            ins = v.tensor_tensor(sa2b[:, :], modt[:, 0:16], gtmp[:, 0:16], ALU.mult)
            inc(ins, "e_sa2", 1)
            for dc in range(16):
                wge(v, "e0ld%d" % (dc % 2), 16 * (dc // 2 + 1))
                src = fst if dc % 2 == 0 else f2
                ins = v.tensor_scalar_mul(A(O_WQD + dc * LQ, LQ), src[:, 0:1536],
                                          ca1[:, dc:dc + 1])
                inc(ins, "e0", 1)

        @blk.tensor
        def _(t):
            wge(t, "e0", 16)
            wge(t, "e_sa2", 1)
            for grp in range(3):
                for dc in range(16):
                    ins = t.matmul(psO[0:1, grp * 512:grp * 512 + 512],
                                   sa2b[:, dc:dc + 1],
                                   A(O_WQD + dc * LQ + grp * 512, 512),
                                   start=(dc == 0), stop=(dc == 15))
                inc(ins, "e1", 1)

        @blk.scalar
        def _(sc):
            wge(sc, "e1", 3)
            for grp in range(3):
                if grp >= 1:
                    wge(sc, "e3", 16 * grp)
                ins = sc.copy(mrow[:, :], psO[0:1, grp * 512:grp * 512 + 512])
                inc(ins, "e2", 1)
                wge(sc, "e2")
                ins = sc.dma_start(out=row_sc[0:1, 6 * D + grp * 512:6 * D + (grp + 1) * 512],
                                   in_=mrow[:, :])
                inc(ins, "e3", 16)
            wge(sc, "e3")

    BAR()
    if stop == 'B2a':
        es.close()
        return nc

    with nc.Block() as blk:
        @blk.sync
        def _(sy):
            ins = sy.dma_start(out=gtmp[:, 32:44],
                               in_=bass.AP(row_sc.tensor, 6 * D, [[1, 128], [128, 12]]))
            inc(ins, "e4", 16)

        @blk.vector
        def _(v):
            wge(v, "e4", 16)
            ins = v.tensor_add(biasq2[:, :], biasq2[:, :], gtmp[:, 32:44])
            inc(ins, "e5", 1)

    BAR()
    if stop == 'B2a2':
        es.close()
        return nc

    O_XNTQ = 24576

    with nc.Block() as blk:
        @blk.sync
        def _(sy):
            for dc in range(16):
                for half, slot in ((0, DIAG_A), (1, DIAG_B)):
                    ins = sy.dma_start(
                        out=A(O_XNTQ + dc * 1024 + half * 512, 512),
                        in_=xn_sc[slot * CH:(slot + 1) * CH, dc * 128:(dc + 1) * 128],
                        transpose=True)
                    inc(ins, "e6", 16)

        @blk.tensor
        def _(t):
            wge(t, "e6", 16 * 32)
            k = 0
            for lqs in range(12):
                for half in range(2):
                    k += 1
                    ps = psA if half == 0 else psB
                    if k > 2:
                        wge(t, "e8", k - 2)
                    for dc in range(16):
                        ins = t.matmul(ps[:, :], A(O_WQD + dc * LQ + lqs * 128, 128),
                                       A(O_XNTQ + dc * 1024 + half * 512, 512),
                                       start=(dc == 0), stop=(dc == 15))
                    inc(ins, "e7", 1)

        @blk.scalar
        def _(sc):
            k = 0
            for lqs in range(12):
                for half in range(2):
                    k += 1
                    ps = psA if half == 0 else psB
                    wge(sc, "e7", k)
                    if lqs >= 1:
                        wge(sc, "e9h%d" % half, 16 * lqs)
                    ins = sc.activation(sgst[:, half * 512:half * 512 + 512], ps[:, :],
                                        AF.Identity, bias=biasq2[:, lqs:lqs + 1])
                    inc(ins, "e8", 1)
                    wge(sc, "e8")
                    ins = sc.dma_start(out=ql_sc[lqs * 128:(lqs + 1) * 128,
                                                 half * 512:(half + 1) * 512],
                                       in_=sgst[:, half * 512:half * 512 + 512])
                    inc(ins, "e9h%d" % half, 16)
            wge(sc, "e9h0")
            wge(sc, "e9h1")

    BAR()
    if stop == 'B2b':
        es.close()
        return nc

    O_WQUP, O_QLST, O_QRT = 0, 12288, 15360

    with nc.Block() as blk:
        @blk.gpsimd
        def _(gp):
            ins = gp.dma_start(
                out=AAP(O_WQUP, [[1024, 12], [1, 1024]]),
                in_=bass.AP(wq_up.tensor, 0, [[1024, 128], [128 * 1024, 12], [1, 1024]]))
            inc(ins, "f0", 16)

        @blk.sync
        def _(sy):
            for qs in range(8):
                if qs >= 2:
                    wge(sy, "f_mm", 2 * (qs - 1))
                ins = sy.dma_start(
                    out=AAP(O_QLST + (qs % 2) * 1536, [[128, 12], [1, 128]]),
                    in_=bass.AP(ql_sc.tensor, qs * 128,
                                [[1024, 128], [128 * 1024, 12], [1, 128]]))
                inc(ins, "f_ql%d" % (qs % 2), 16)
                if qs >= 1:
                    wge(sy, "f_ra", qs)
                slot = DIAG_A if qs < 4 else DIAG_B
                roff = slot * CH + (qs % 4) * 128
                ins = sy.dma_start(out=rcs[:, 0:64], in_=rope_cs[roff:roff + 128, :])
                inc(ins, "f_rc", 16)

        @blk.tensor
        def _(t):
            wge(t, "f0", 16)
            wge(t, "a0g", 32)
            for qs in range(8):
                wge(t, "f_ql%d" % (qs % 2), 16 * (qs // 2 + 1))
                for half in range(2):
                    ps = psA if half == 0 else psB
                    if qs * 2 + half >= 2:
                        wge(t, "f_ev", qs * 2 + half - 1)
                    for lq in range(12):
                        t.matmul(ps[:, :],
                                 AAP(O_QLST + (qs % 2) * 1536 + lq * 128, [[1, 128]]),
                                 A(O_WQUP + lq * 1024 + half * 512, 512),
                                 start=(lq == 0), stop=False)
                    ins = t.matmul(ps[:, :], ones1[:, :],
                                   brow_qup[0:1, half * 512:(half + 1) * 512],
                                   start=False, stop=True)
                    inc(ins, "f_mm", 1)

        @blk.scalar
        def _(sc):
            for qs in range(8):
                for half in range(2):
                    ps = psA if half == 0 else psB
                    wge(sc, "f_mm", qs * 2 + half + 1)
                    if qs >= 1 and half == 0:
                        wge(sc, "f_ra", qs)
                    ins = sc.activation(fst[:, half * 512:(half + 1) * 512], ps[:, :],
                                        AF.Identity, scale=SCALE)
                    inc(ins, "f_ev", 1)
                wge(sc, "f_bf", qs + 1)
                for hp in range(8):
                    wge(sc, "f_tp")
                    ins = sc.dma_start(out=A(O_QRT + (hp % 2) * 128, 128),
                                       in_=sgst[:, hp * 128:(hp + 1) * 128],
                                       transpose=True)
                    inc(ins, "f_tp", 16)
                    wge(sc, "f_tp")
                    ins = sc.dma_start(
                        out=bass.AP(qr_sc.tensor, (2 * hp) * 1024 + qs * 128,
                                    [[H * 1024, 64], [1, 128]]),
                        in_=ar[0:64, O_QRT + (hp % 2) * 128:O_QRT + (hp % 2) * 128 + 128])
                    inc(ins, "f_tp", 16)
                    ins = sc.dma_start(
                        out=bass.AP(qr_sc.tensor, (2 * hp + 1) * 1024 + qs * 128,
                                    [[H * 1024, 64], [1, 128]]),
                        in_=ar[64:128, O_QRT + (hp % 2) * 128:O_QRT + (hp % 2) * 128 + 128])
                    inc(ins, "f_tp", 16)
            wge(sc, "f_tp")

        @blk.vector
        def _(v):
            for qs in range(8):
                wge(v, "f_ev", qs * 2 + 2)
                wge(v, "f_rc", 16 * (qs + 1))
                if qs >= 1:
                    wge(v, "f_tp", 16 * 24 * qs)
                x1v = bass.AP(fst, 0, [[D, 128], [64, 16], [1, 32]])
                x2v = bass.AP(fst, 32, [[D, 128], [64, 16], [1, 32]])
                cosv = bass.AP(rcs, 0, [[4 * RH, 128], [0, 16], [1, 32]])
                sinv = bass.AP(rcs, RH, [[4 * RH, 128], [0, 16], [1, 32]])
                o1 = bass.AP(f2, 0, [[D, 128], [64, 16], [1, 32]])
                o2 = bass.AP(f2, 32, [[D, 128], [64, 16], [1, 32]])
                g1 = bass.AP(f2, 1024, [[D, 128], [64, 16], [1, 32]])
                g2 = bass.AP(f2, 1024 + 32, [[D, 128], [64, 16], [1, 32]])
                v.tensor_tensor(o1, x1v, cosv, ALU.mult)
                v.tensor_tensor(o2, x1v, sinv, ALU.mult)
                v.tensor_tensor(g1, x2v, sinv, ALU.mult)
                ins = v.tensor_tensor(g2, x2v, cosv, ALU.mult)
                inc(ins, "f_ra", 1)
                wge(v, "f_ra")
                v.tensor_tensor(o1, o1, g1, ALU.subtract)
                ins = v.tensor_tensor(o2, o2, g2, ALU.add)
                inc(ins, "f_rb", 1)
                wge(v, "f_rb")
                ins = v.tensor_copy(sgst[:, 0:1024], f2[:, 0:1024])
                inc(ins, "f_bf", 1)


    BAR()
    if stop == 'B2c':
        es.close()
        return nc

    O_WKVD, O_WKUC, O_WVU, O_WKUP = 0, 8192, 16384, 24576

    with nc.Block() as blk:
        @blk.sync
        def _(sy):
            for dc in range(16):
                if dc >= 2:
                    wge(sy, "g0", dc - 1)
                ins = sy.dma_start(out=(fst if dc % 2 == 0 else f2)[:, 0:512],
                                   in_=wkv_d[dc * 128:(dc + 1) * 128, :])
                inc(ins, "g0ld%d" % (dc % 2), 16)
            for lk in range(4):
                if lk == 0:
                    wge(sy, "g0", 15)
                else:
                    wge(sy, "g1", lk)
                ins = sy.dma_start(out=(fst if lk % 2 == 0 else f2)[:, 0:D],
                                   in_=wv_u[lk * 128:(lk + 1) * 128, :])
                inc(ins, "g1ld%d" % (lk % 2), 16)

        @blk.gpsimd
        def _(gp):
            ins = gp.dma_start(
                out=AAP(O_WKUC, [[2048, 4], [1, 2048]]),
                in_=bass.AP(wk_uc.tensor, 0, [[2048, 128], [128 * 2048, 4], [1, 2048]]))
            inc(ins, "g2", 16)
            ins = gp.dma_start(
                out=AAP(O_WKUP, [[64, 4], [1, 64]]),
                in_=bass.AP(wk_up.tensor, 0, [[64, 128], [128 * 64, 4], [1, 64]]))
            inc(ins, "g2", 16)

        @blk.vector
        def _(v):
            for dc in range(16):
                wge(v, "g0ld%d" % (dc % 2), 16 * (dc // 2 + 1))
                src = fst if dc % 2 == 0 else f2
                ins = v.tensor_scalar_mul(A(O_WKVD + dc * 512, 512), src[:, 0:512],
                                          ca1[:, dc:dc + 1])
                inc(ins, "g0", 1)
            for lk in range(4):
                wge(v, "g1ld%d" % (lk % 2), 16 * (lk // 2 + 1))
                src = fst if lk % 2 == 0 else f2
                ins = v.tensor_tensor(A(O_WVU + lk * 2048, 2048), src[:, 0:D],
                                      ga_rep[:, :], ALU.mult)
                inc(ins, "g1", 1)

        @blk.tensor
        def _(t):
            wge(t, "g0", 16)
            for dc in range(16):
                ins = t.matmul(psO[0:1, 0:512], sa2b[:, dc:dc + 1],
                               A(O_WKVD + dc * 512, 512),
                               start=(dc == 0), stop=(dc == 15))
            inc(ins, "g3", 1)

        @blk.scalar
        def _(sc):
            wge(sc, "g3", 1)
            ins = sc.copy(mrow[:, :], psO[0:1, 0:512])
            inc(ins, "g4", 1)
            wge(sc, "g4")
            ins = sc.dma_start(out=row_sc[0:1, 6 * D:6 * D + 512], in_=mrow[:, :])
            inc(ins, "g5", 16)
            wge(sc, "g5")

    BAR()
    if stop == 'B3a':
        es.close()
        return nc

    with nc.Block() as blk:
        @blk.sync
        def _(sy):
            ins = sy.dma_start(out=gtmp[:, 32:36],
                               in_=bass.AP(row_sc.tensor, 6 * D, [[1, 128], [128, 4]]))
            inc(ins, "g6", 16)

        @blk.vector
        def _(v):
            wge(v, "g6", 16)
            ins = v.tensor_add(biaskv2[:, :], biaskv2[:, :], gtmp[:, 32:36])
            inc(ins, "g7", 1)

    BAR()
    if stop == 'B3a2':
        es.close()
        return nc

    O_XNT, O_KVL = 24832, 41216

    def kvl(buf, lk):
        return A(O_KVL + buf * 2048 + lk * 512, 512)

    with nc.Block() as blk:
        @blk.sync
        def _(sy):
            for kc in range(16):
                if kc >= 2:
                    wge(sy, "h_kv", 4 * (kc - 1))
                for dc in range(16):
                    ins = sy.dma_start(
                        out=A(O_XNT + (kc % 2) * 8192 + dc * 512, 512),
                        in_=xn_sc[kc * CH:(kc + 1) * CH, dc * 128:(dc + 1) * 128],
                        transpose=True)
                    inc(ins, "h_xn%d" % (kc % 2), 16)
                for ks in range(4):
                    i = kc * 4 + ks
                    if i >= 2:
                        wge(sy, "h_ra", i - 1)
                    ins = sy.dma_start(
                        out=rcs[:, (i % 2) * 64:(i % 2) * 64 + 64],
                        in_=rope_cs[kc * CH + ks * 128:kc * CH + (ks + 1) * 128, :])
                    inc(ins, "h_rc%d" % (i % 2), 16)

        @blk.tensor
        def _(t):
            wge(t, "g2", 32)
            for kc in range(16):
                wge(t, "h_xn%d" % (kc % 2), 256 * (kc // 2 + 1))
                if kc >= 1:
                    wge(t, "h_kev", 16 * kc)
                for ls in range(4):
                    ps = psA if ls % 2 == 0 else psB
                    if kc * 4 + ls >= 2:
                        wge(t, "h_kvev", kc * 4 + ls - 1)
                    for dc in range(16):
                        ins = t.matmul(ps[:, :], A(O_WKVD + dc * 512 + ls * 128, 128),
                                       A(O_XNT + (kc % 2) * 8192 + dc * 512, 512),
                                       start=(dc == 0), stop=(dc == 15))
                    inc(ins, "h_kv", 1)
                wge(t, "h_kvev", 4 * (kc + 1))
                for hs in range(16):
                    ps = psA if hs % 2 == 0 else psB
                    if kc * 16 + hs >= 2:
                        wge(t, "h_kev", kc * 16 + hs - 1)
                    for lk in range(4):
                        ins = t.matmul(ps[:, :], A(O_WKUC + lk * 2048 + hs * 128, 128),
                                       kvl(kc % 2, lk),
                                       start=(lk == 0), stop=(lk == 3))
                    inc(ins, "h_k", 1)
                for ks in range(4):
                    if kc * 4 + ks >= 1:
                        wge(t, "h_vev", (kc * 4 + ks) * 4)
                    for lk in range(4):
                        for hq in range(4):
                            t.matmul(psO[:, hq * 512:(hq + 1) * 512],
                                     AAP(O_KVL + (kc % 2) * 2048 + lk * 512 + ks * 128,
                                         [[1, 128]]),
                                     A(O_WVU + lk * 2048 + hq * 512, 512),
                                     start=(lk == 0), stop=False)
                    for hq in range(4):
                        ins = t.matmul(psO[:, hq * 512:(hq + 1) * 512], ones1[:, :],
                                       brow_vu[0:1, hq * 512:(hq + 1) * 512],
                                       start=False, stop=True)
                        inc(ins, "h_v", 1)
                for ks in range(4):
                    if kc * 4 + ks >= 1:
                        wge(t, "h_rev", kc * 4 + ks)
                    for lk in range(4):
                        t.matmul(psG[:, 0:64],
                                 AAP(O_KVL + (kc % 2) * 2048 + lk * 512 + ks * 128,
                                     [[1, 128]]),
                                 A(O_WKUP + lk * 64, 64),
                                 start=(lk == 0), stop=False)
                    ins = t.matmul(psG[:, 0:64], ones1[:, :], brow_kup[0:1, :],
                                   start=False, stop=True)
                    inc(ins, "h_r", 1)

        @blk.scalar
        def _(sc):
            for kc in range(16):
                for ls in range(4):
                    ps = psA if ls % 2 == 0 else psB
                    wge(sc, "h_kv", kc * 4 + ls + 1)
                    ins = sc.activation(kvl(kc % 2, ls), ps[:, :], AF.Identity,
                                        bias=biaskv2[:, ls:ls + 1])
                    inc(ins, "h_kvev", 1)
                for hs in range(16):
                    ps = psA if hs % 2 == 0 else psB
                    wge(sc, "h_k", kc * 16 + hs + 1)
                    q = hs % 4
                    nuse = kc * 4 + hs // 4
                    if nuse >= 1:
                        wge(sc, "h_ks%d" % q, 16 * nuse)
                    if kc >= 1 and hs < 4:
                        wge(sc, "h_vst", 64 * kc)
                    ins = sc.activation(sgst[:, q * 512:(q + 1) * 512], ps[:, :],
                                        AF.Identity, bias=bias_kc[:, hs:hs + 1])
                    inc(ins, "h_kev", 1)
                for ks in range(4):
                    if ks >= 1:
                        wge(sc, "h_vst", 16 * (kc * 4 + ks))
                    for hq in range(4):
                        wge(sc, "h_v", kc * 16 + ks * 4 + hq + 1)
                        if ks == 0:
                            wge(sc, "h_ks%d" % hq, 16 * (kc + 1) * 4)
                        ins = sc.copy(sgst[:, hq * 512:(hq + 1) * 512],
                                      psO[:, hq * 512:(hq + 1) * 512])
                        inc(ins, "h_vev", 1)
                for ks in range(4):
                    wge(sc, "h_r", kc * 4 + ks + 1)
                    if kc * 4 + ks >= 4:
                        wge(sc, "h_ra", kc * 4 + ks - 3)
                    ins = sc.activation(fst[:, ks * 64:(ks + 1) * 64], psG[:, 0:64],
                                        AF.Identity)
                    inc(ins, "h_rev", 1)
                for pair in range(2):
                    wge(sc, "h_bf", kc * 4 + pair * 2 + 2)
                    wge(sc, "h_tp")
                    ins = sc.dma_start(out=A(45312, 128),
                                       in_=sgst[:, D:D + 128], transpose=True)
                    inc(ins, "h_tp", 16)
                    wge(sc, "h_tp")
                    k0 = kc * CH + pair * 256
                    ins = sc.dma_start(
                        out=bass.AP(kr_sc.tensor, k0, [[S, 64], [1, 128]]),
                        in_=ar[0:64, 45312:45312 + 128])
                    inc(ins, "h_tp", 16)
                    ins = sc.dma_start(
                        out=bass.AP(kr_sc.tensor, k0 + 128, [[S, 64], [1, 128]]),
                        in_=ar[64:128, 45312:45312 + 128])
                    inc(ins, "h_tp", 16)
            wge(sc, "h_tp")

        @blk.gpsimd
        def _(gp):
            for kc in range(16):
                for hs in range(16):
                    wge(gp, "h_kev", kc * 16 + hs + 1)
                    q = hs % 4
                    ins = gp.dma_start(out=k_sc[hs * 128:(hs + 1) * 128,
                                                kc * 512:(kc + 1) * 512],
                                       in_=sgst[:, q * 512:(q + 1) * 512])
                    inc(ins, "h_ks%d" % q, 16)
                for ks in range(4):
                    wge(gp, "h_vev", kc * 16 + ks * 4 + 4)
                    ins = gp.dma_start(
                        out=v_sc[kc * CH + ks * 128:kc * CH + (ks + 1) * 128, :],
                        in_=sgst[:, 0:2048])
                    inc(ins, "h_vst", 16)
            wge(gp, "h_vst", 16 * 64)
            for q in range(4):
                wge(gp, "h_ks%d" % q, 16 * 64)

        @blk.vector
        def _(v):
            for kc in range(16):
                for ks in range(4):
                    i = kc * 4 + ks
                    pair = i // 2
                    wge(v, "h_rev", i + 1)
                    wge(v, "h_rc%d" % (i % 2), 16 * (i // 2 + 1))
                    if pair >= 1 and i % 2 == 0:
                        wge(v, "h_tp", 48 * pair)
                    src1 = bass.AP(fst, ks * 64, [[D, 128], [1, 32]])
                    src2 = bass.AP(fst, ks * 64 + 32, [[D, 128], [1, 32]])
                    cosv = bass.AP(rcs, (i % 2) * 64, [[4 * RH, 128], [1, 32]])
                    sinv = bass.AP(rcs, (i % 2) * 64 + RH, [[4 * RH, 128], [1, 32]])
                    o1 = bass.AP(f2, (ks % 2) * 64, [[D, 128], [1, 32]])
                    o2 = bass.AP(f2, (ks % 2) * 64 + 32, [[D, 128], [1, 32]])
                    g1 = bass.AP(f2, 1024 + (ks % 2) * 64, [[D, 128], [1, 32]])
                    g2 = bass.AP(f2, 1024 + (ks % 2) * 64 + 32, [[D, 128], [1, 32]])
                    v.tensor_tensor(o1, src1, cosv, ALU.mult)
                    v.tensor_tensor(o2, src1, sinv, ALU.mult)
                    v.tensor_tensor(g1, src2, sinv, ALU.mult)
                    ins = v.tensor_tensor(g2, src2, cosv, ALU.mult)
                    inc(ins, "h_ra", 1)
                    wge(v, "h_ra")
                    v.tensor_tensor(o1, o1, g1, ALU.subtract)
                    ins = v.tensor_tensor(o2, o2, g2, ALU.add)
                    inc(ins, "h_rb", 1)
                    wge(v, "h_rb")
                    ins = v.tensor_copy(sgst[:, D + (ks % 2) * 64:D + (ks % 2) * 64 + 64],
                                        f2[:, (ks % 2) * 64:(ks % 2) * 64 + 64])
                    inc(ins, "h_bf", 1)


    BAR()
    if stop == 'B3b':
        es.close()
        return nc

    if stop == 'B4':
        es.close()
        return nc

    O_KH, O_VA, O_KRT, O_PST, O_QCH, O_WQC, O_QLM, O_QRH = \
        0, 16384, 33280, 41472, 42496, 43520, 46592, 48640

    sign_jobs = []
    for mat, src_t, dst_t in (("g", wgT, wg_sc), ("u", wuT, wu_sc), ("d", wdT, wd_sc)):
        nr = 16 if mat != "d" else 64
        ncc = 4 if mat != "d" else 1
        for r in range(nr):
            for c in range(ncc):
                sign_jobs.append((mat, src_t, dst_t, r, c))
    assert len(sign_jobs) == 192

    NKS = (NKA * 4, NKB * 4)
    DIAG0 = (28, 60)
    SPH = NKS[0] + NKS[1]

    def pv_groups_before(h, slot):
        return h * SPH + (NKS[0] if slot == 1 else 0)

    with nc.Block() as blk:
        @blk.sync
        def _(sy):
            ins = sy.dma_start(out=ar[0:64, O_KRT:O_KRT + 8192], in_=kr_sc)
            inc(ins, "j_kr", 16)
            for h in range(16):
                buf = h % 2
                if h >= 2:
                    wge(sy, "j_st", SPH * (h - 1))
                ins = sy.dma_start(out=A(O_KH + buf * 8192, 8192),
                                   in_=k_sc[h * 128:(h + 1) * 128, :])
                inc(ins, "j_k%d" % buf, 16)
                ins = sy.dma_start(
                    out=ar[0:64, O_QRH + buf * 1024:O_QRH + (buf + 1) * 1024],
                    in_=bass.AP(qr_sc.tensor, h * 1024, [[H * 1024, 64], [1, 1024]]))
                inc(ins, "j_qr%d" % buf, 16)
                if h >= 2:
                    wge(sy, "j_pvall", SPH * (h - 1))
                ins = sy.dma_start(
                    out=bass.AP(ar, O_VA + buf * 8448,
                                [[ARN, 128], [VW, 64], [1, 128]]),
                    in_=bass.AP(v_sc.tensor, h * 128,
                                [[H * DC, 128], [128 * H * DC, 64], [1, 128]]))
                inc(ins, "j_v%d" % buf, 16)
                for slot in range(2):
                    m = h * 2 + slot
                    if m >= 2:
                        wge(sy, "j_oxa", 4 * (m - 1))
                    ins = sy.dma_start(
                        out=bass.AP(xout, (m % 2) * 512, [[4 * D, 128], [128, 4], [1, 128]]),
                        in_=bass.AP(x_own.tensor, slot * 512 * D + h * 128,
                                    [[D, 128], [128 * D, 4], [1, 128]]))
                    inc(ins, "j_xsl%d" % (m % 2), 16)
                for lq in range(12):
                    i2 = h * 12 + lq
                    if i2 >= 2:
                        wge(sy, "j_qc", i2 - 1)
                    ins = sy.dma_start(
                        out=A(O_QLM + (lq % 2) * 1024, 1024),
                        in_=bass.AP(ql_sc.tensor, lq * 128 * 1024,
                                    [[1024, 128], [1, 1024]]))
                    inc(ins, "j_ql%d" % (lq % 2), 16)
                for sj in range(12):
                    idx = h * 12 + sj
                    mat, src_t, dst_t, r, c = sign_jobs[idx]
                    if idx >= 2:
                        wge(sy, "j_sg", idx - 1)
                    ins = sy.dma_start(out=(fst if idx % 2 == 0 else f2)[:, :],
                                       in_=src_t[r * 128:(r + 1) * 128,
                                                 c * 2048:(c + 1) * 2048])
                    inc(ins, "j_sgld%d" % (idx % 2), 16)

        @blk.gpsimd
        def _(gp):
            for h in range(16):
                buf = h % 2
                if h >= 2:
                    wge(gp, "j_qcb", h - 1)
                ins = gp.dma_start(
                    out=AAP(O_WQC + buf * 1536, [[128, 12], [1, 128]]),
                    in_=bass.AP(wq_uc.tensor, h * 128,
                                [[2048, 128], [128 * 2048, 12], [1, 128]]))
                inc(ins, "j_wqc%d" % buf, 16)
                for slot in range(2):
                    for qs in range(4):
                        k = h * 8 + slot * 4 + qs
                        wge(gp, "j_oxa", k + 1)
                        rows = slot * 512 + qs * 128
                        ins = gp.dma_start(
                            out=x1_sc[rows:rows + 128, h * 128:(h + 1) * 128],
                            in_=ost[:, (k % 2) * 128:(k % 2) * 128 + 128])
                        inc(ins, "j_acc%d" % (k % 2), 16)
                    for sj in range(6):
                        idx = h * 12 + slot * 6 + sj
                        mat, src_t, dst_t, r, c = sign_jobs[idx]
                        if mat == "d":
                            dcount = sum(1 for q2 in range(idx + 1)
                                         if sign_jobs[q2][0] == "d")
                            wge(gp, "j_gbm", dcount)
                        else:
                            wge(gp, "j_sg", idx + 1)
                        ins = gp.dma_start(
                            out=dst_t[r * 128:(r + 1) * 128, c * 2048:(c + 1) * 2048],
                            in_=sgst[:, (idx % 2) * D:(idx % 2 + 1) * D])
                        inc(ins, "j_sgst%d" % (idx % 2), 16)
            wge(gp, "j_acc0", 16 * 64)
            wge(gp, "j_acc1", 16 * 64)
            wge(gp, "j_sgst0", 16 * 96)
            wge(gp, "j_sgst1", 16 * 96)

        @blk.tensor
        def _(t):
            wge(t, "j_kr", 16)
            wge(t, "j_vone", 1)
            gidx = 0
            tri_n = 0
            for h in range(16):
                buf = h % 2
                wge(t, "j_wqc%d" % buf, 16 * (h // 2 + 1))
                if h >= 1:
                    wge(t, "j_qcb", h)
                for lq in range(12):
                    wge(t, "j_ql%d" % (lq % 2), 16 * ((h * 12 + lq) // 2 + 1))
                    for qh in range(2):
                        ps = psG if qh == 0 else psU
                        ins = t.matmul(ps[:, :],
                                       A(O_WQC + buf * 1536 + lq * 128, 128),
                                       A(O_QLM + (lq % 2) * 1024 + qh * 512, 512),
                                       start=(lq == 0), stop=(lq == 11))
                    inc(ins, "j_qc", 1)
                wge(t, "j_k%d" % buf, 16 * (h // 2 + 1))
                wge(t, "j_qr%d" % buf, 16 * (h // 2 + 1))
                wge(t, "j_v%d" % buf, 16 * (h // 2 + 1))
                wge(t, "j_qcb", h + 1)
                for slot in range(2):
                    nks = NKS[slot]
                    qc0 = slot * 512
                    d0 = DIAG0[slot]
                    base_pv = pv_groups_before(h, slot)
                    for ks in range(nks + 1):
                        if ks < nks:
                            gidx += 1
                            if gidx >= 3:
                                wge(t, "j_ex", gidx - 2)
                            ps = psA if ks % 2 == 0 else psB
                            t.matmul(ps[:, :],
                                     A(O_KH + buf * 8192 + ks * 128, 128),
                                     A(O_QCH + qc0, 512),
                                     start=True, stop=False)
                            ins = t.matmul(ps[:, :],
                                           bass.AP(ar, O_KRT + ks * 128,
                                                   [[ARN, 64], [1, 128]]),
                                           bass.AP(ar, O_QRH + buf * 1024 + qc0,
                                                   [[ARN, 64], [1, 512]]),
                                           start=False, stop=True)
                            inc(ins, "j_st", 1)
                        if ks >= 1:
                            pk = ks - 1
                            exp_n = gidx if ks == nks else gidx - 1
                            wge(t, "j_ex", exp_n)
                            if pk >= d0:
                                wge(t, "j_tri", tri_n + (pk - d0) + 1)
                            for qs in range(4):
                                ins = t.matmul(
                                    psO[:, qs * 512:qs * 512 + 129],
                                    A(O_PST + (pk % 2) * 512 + qs * 128, 128),
                                    A(O_VA + buf * 8448 + pk * VW, 129),
                                    start=(pk == 0), stop=(pk == nks - 1))
                            inc(ins, "j_pvall", 1)
                    tri_n += 4
                    wge(t, "j_oev", h * 8 + slot * 4 + 4)

        @blk.scalar
        def _(sc):
            gidx = 0
            sgi = 0
            for h in range(16):
                for qh in range(2):
                    ps = psG if qh == 0 else psU
                    wge(sc, "j_qc", h * 12 + 12)
                    if h >= 1 and qh == 0:
                        wge(sc, "j_st", SPH * h)
                    ins = sc.activation(A(O_QCH + qh * 512, 512), ps[:, :], AF.Identity,
                                        scale=SCALE, bias=bias_qc[:, h:h + 1])
                inc(ins, "j_qcb", 1)
                for slot in range(2):
                    nks = NKS[slot]
                    base_pv = pv_groups_before(h, slot)
                    for ks in range(nks):
                        gidx += 1
                        wge(sc, "j_st", gidx)
                        if ks >= 2:
                            wge(sc, "j_pvall", base_pv + ks - 1)
                        elif base_pv >= 1:
                            wge(sc, "j_pvall", base_pv)
                        ps = psA if ks % 2 == 0 else psB
                        mcol = ks if slot == 0 else 32 + ks
                        ins = sc.activation(A(O_PST + (ks % 2) * 512, 512), ps[:, :],
                                            AF.Exp, bias=mb_sb[:, mcol:mcol + 1])
                        inc(ins, "j_ex", 1)
                    for qs in range(4):
                        k = h * 8 + slot * 4 + qs
                        wge(sc, "j_lr", k + 1)
                        if k >= 2:
                            wge(sc, "j_acc%d" % (k % 2), 16 * (k // 2))
                        ins = sc.activation(ost[:, (k % 2) * 128:(k % 2) * 128 + 128],
                                            psO[:, qs * 512:qs * 512 + 128], AF.Identity,
                                            scale=lrec[:, 0:1])
                        inc(ins, "j_oev", 1)
                    for _ in range(6):
                        idx = sgi
                        sgi += 1
                        mat, src_t, dst_t, r, c = sign_jobs[idx]
                        wge(sc, "j_sgld%d" % (idx % 2), 16 * (idx // 2 + 1))
                        if idx >= 2:
                            wge(sc, "j_sgst%d" % (idx % 2), 16 * (idx // 2))
                        ins = sc.activation(sgst[:, (idx % 2) * D:(idx % 2 + 1) * D],
                                            (fst if idx % 2 == 0 else f2)[:, :], AF.Sign)
                        inc(ins, "j_sg", 1)

        @blk.vector
        def _(v):
            ins = v.memset(bass.AP(ar, O_VA + 128, [[ARN, 128], [VW, 128]]), 1.0)
            inc(ins, "j_vone", 1)
            gidx = 0
            sgi = 0
            for h in range(16):
                for slot in range(2):
                    nks = NKS[slot]
                    d0 = DIAG0[slot]
                    base_pv = pv_groups_before(h, slot)
                    for ks in range(nks):
                        gidx += 1
                        if ks >= d0:
                            j = ks - d0
                            wge(v, "j_ex", gidx)
                            ins = v.tensor_mul(A(O_PST + (ks % 2) * 512, 512),
                                               A(O_PST + (ks % 2) * 512, 512),
                                               tri_sb[:, j * 512:(j + 1) * 512])
                            inc(ins, "j_tri", 1)
                    m = h * 2 + slot
                    for qs in range(4):
                        k = h * 8 + slot * 4 + qs
                        if qs == 0:
                            wge(v, "j_pvall", base_pv + nks)
                        if k >= 1:
                            wge(v, "j_oev", k)
                        ins = v.reciprocal(lrec[:, 0:1],
                                           psO[:, qs * 512 + 128:qs * 512 + 129])
                        inc(ins, "j_lr", 1)
                        wge(v, "j_oev", k + 1)
                        wge(v, "j_xsl%d" % (m % 2), 16 * (m // 2 + 1))
                        ins = v.tensor_tensor(
                            ost[:, (k % 2) * 128:(k % 2) * 128 + 128],
                            ost[:, (k % 2) * 128:(k % 2) * 128 + 128],
                            bass.AP(xout, (m % 2) * 512 + qs * 128, [[4 * D, 128], [1, 128]]),
                            ALU.add)
                        inc(ins, "j_oxa", 1)
                    for _ in range(6):
                        idx = sgi
                        sgi += 1
                        if sign_jobs[idx][0] == "d":
                            wge(v, "j_sg", idx + 1)
                            ins = v.tensor_mul(sgst[:, (idx % 2) * D:(idx % 2 + 1) * D],
                                               sgst[:, (idx % 2) * D:(idx % 2 + 1) * D],
                                               gb_rep[:, :])
                            inc(ins, "j_gbm", 1)

    BAR()
    if stop == 'B5':
        es.close()
        return nc

    O_XQT, O_PGR, O_WGS, O_WUS, O_WDS, O_GU = 0, 8192, 16384, 20480, 24576, 25600

    with nc.Block() as blk:
        @blk.sync
        def _(sy):
            for half in range(2):
                for tq in range(4):
                    i = half * 4 + tq
                    if half == 1:
                        wge(sy, "m_out", 16 * 4)
                    ins = sy.dma_start(out=xout[:, tq * D:(tq + 1) * D],
                                       in_=x1_sc[i * 128:(i + 1) * 128, :])
                    inc(ins, "m_x1", 16)
                wge(sy, "m_xq", 16 * 4 * (half + 1))
                for dc in range(16):
                    ins = sy.dma_start(
                        out=A(O_XQT + dc * 512, 512),
                        in_=xq_sc[half * 512:(half + 1) * 512, dc * 128:(dc + 1) * 128],
                        transpose=True)
                    inc(ins, "m_xqt", 16)
                for hg in range(4):
                    for hs in range(16):
                        habs = hg * 16 + hs
                        i = half * 64 + habs
                        if i >= 2:
                            wge(sy, "m_p", i - 1)
                        ins = sy.dma_start(
                            out=AAP(O_WGS + (i % 2) * 2048, [[128, 16], [1, 128]]),
                            in_=bass.AP(wg_sc.tensor, habs * 128,
                                        [[FH, 128], [128 * FH, 16], [1, 128]]))
                        inc(ins, "m_wg%d" % (i % 2), 16)
                        ins = sy.dma_start(
                            out=AAP(O_WUS + (i % 2) * 2048, [[128, 16], [1, 128]]),
                            in_=bass.AP(wu_sc.tensor, habs * 128,
                                        [[FH, 128], [128 * FH, 16], [1, 128]]))
                        inc(ins, "m_wu%d" % (i % 2), 16)
                    for dq in range(4):
                        for hs in range(16):
                            habs = hg * 16 + hs
                            i = ((half * 4 + hg) * 4 + dq) * 16 + hs
                            if i >= 2:
                                j = i - 2
                                if j % 16 == 15:
                                    wge(sy, "m_pvd", j // 16 + 1)
                                else:
                                    wge(sy, "m_dn", 15 * (j // 16) + j % 16 + 1)
                            ins = sy.dma_start(
                                out=A(O_WDS + (i % 2) * 512, 512),
                                in_=wd_sc[habs * 128:(habs + 1) * 128,
                                          dq * 512:(dq + 1) * 512])
                            inc(ins, "m_wd%d" % (i % 2), 16)
                for tq in range(4):
                    i = half * 4 + tq
                    wge(sy, "m_fl", (half + 1) * 16 * 4)
                    ins = sy.dma_start(out=out_own[i * 128:(i + 1) * 128, :],
                                       in_=xout[:, tq * D:(tq + 1) * D])
                    inc(ins, "m_out", 16)
            wge(sy, "m_out", 16 * 8)

        @blk.vector
        def _(v):
            for half in range(2):
                for tq in range(4):
                    i = half * 4 + tq
                    wge(v, "m_x1", 16 * 4 * (half + 1))
                    if i >= 1:
                        wge(v, "m_ln1", i)
                    for j in range(4):
                        ins = v.bn_stats(bn24[:, j * 6:(j + 1) * 6],
                                         xout[:, tq * D + j * 512:tq * D + (j + 1) * 512])
                    inc(ins, "m_bn", 1)
                    wge(v, "m_bn")
                    ins = v.bn_aggr(st2[:, :], bn24[:, :])
                    inc(ins, "m_ag", 1)
                    wge(v, "m_rs1", i + 1)
                    if i >= 1:
                        wge(v, "m_xn1", i)
                    ins = v.tensor_scalar(nmean[:, :], st2[:, 0:1], rstd[:, 0:1], -1.0,
                                          ALU.mult, ALU.mult)
                    inc(ins, "m_nm", 1)
                    wge(v, "m_xn1", i + 1)
                    ins = v.tensor_tensor(f2[:, :], fst[:, :], cb1_rep[:, :], ALU.mult)
                    inc(ins, "m_h2a", 1)
                    wge(v, "m_h2a")
                    ins = v.tensor_tensor(f2[:, :], f2[:, :], sb_rep[:, :], ALU.add)
                    inc(ins, "m_h2", 1)
                    wge(v, "m_h2")
                    if i >= 1:
                        wge(v, "m_ln2", i)
                    for j in range(4):
                        ins = v.bn_stats(bn24[:, j * 6:(j + 1) * 6],
                                         f2[:, j * 512:(j + 1) * 512])
                    inc(ins, "m_bn2", 1)
                    wge(v, "m_bn2")
                    ins = v.bn_aggr(st2b[:, :], bn24[:, :])
                    inc(ins, "m_ag2", 1)
                    wge(v, "m_rs2", i + 1)
                    if i >= 1:
                        wge(v, "m_y0", i)
                    ins = v.tensor_scalar(nm2[:, :], st2b[:, 0:1], rstd2[:, 0:1], -1.0,
                                          ALU.mult, ALU.mult)
                    inc(ins, "m_nm2", 1)
                    wge(v, "m_y0", i + 1)
                    ins = v.tensor_reduce(amax_t[:, :], fst[:, :], mybir.AxisListType.X,
                                          ALU.max, apply_absolute_value=True)
                    inc(ins, "m_am", 1)
                    wge(v, "m_am")
                    ins = v.tensor_scalar_mul(srec[:, :], amax_t[:, :], 1.0 / 127.0)
                    inc(ins, "m_sr", 1)
                    wge(v, "m_sr")
                    ins = v.reciprocal(sc_ap[:, :], amax_t[:, :])
                    inc(ins, "m_rc2", 1)
                    wge(v, "m_rc2")
                    ins = v.tensor_scalar(fst[:, :], fst[:, :], sc_ap[:, 0:1], 127.0,
                                          ALU.mult, ALU.mult)
                    inc(ins, "m_ys", 1)
                    wge(v, "m_ys")
                    for hh in range(2):
                        c0, c1 = hh * 1024, (1 - hh) * 1024
                        if hh == 1:
                            wge(v, "m_xqv", 2 * i + 1)
                        ins = v.tensor_copy(ti32[:, :], fst[:, c0:c0 + 1024])
                        inc(ins, "m_rnd", 1)
                        wge(v, "m_rnd")
                        ins = v.tensor_copy(f2[:, c0:c0 + 1024], ti32[:, :])
                        inc(ins, "m_rnd", 1)
                        wge(v, "m_rnd")
                        ins = v.tensor_tensor(fst[:, c0:c0 + 1024], fst[:, c0:c0 + 1024],
                                              f2[:, c0:c0 + 1024], ALU.subtract)
                        inc(ins, "m_rnd", 1)
                        wge(v, "m_rnd")
                        for rung in range(4):
                            thr = rung + 0.5
                            ins = v.tensor_scalar(f2[:, c1:c1 + 1024], fst[:, c0:c0 + 1024],
                                                  thr, None, ALU.is_ge)
                            inc(ins, "m_rnd", 1)
                            wge(v, "m_rnd")
                            ins = v.tensor_tensor(f2[:, c0:c0 + 1024], f2[:, c0:c0 + 1024],
                                                  f2[:, c1:c1 + 1024], ALU.add)
                            inc(ins, "m_rnd", 1)
                            wge(v, "m_rnd")
                            ins = v.tensor_scalar(f2[:, c1:c1 + 1024], fst[:, c0:c0 + 1024],
                                                  -thr, None, ALU.is_le)
                            inc(ins, "m_rnd", 1)
                            wge(v, "m_rnd")
                            ins = v.tensor_tensor(f2[:, c0:c0 + 1024], f2[:, c0:c0 + 1024],
                                                  f2[:, c1:c1 + 1024], ALU.subtract)
                            inc(ins, "m_rnd", 1)
                            wge(v, "m_rnd")
                        if i >= 1:
                            wge(v, "m_xq", 16 * i)
                        ins = v.tensor_scalar_mul(sgst[:, D + c0:D + c0 + 1024],
                                                  f2[:, c0:c0 + 1024], srec[:, 0:1])
                        inc(ins, "m_xqv", 1)
                for hg in range(4):
                    g = half * 4 + hg
                    for hs in range(16):
                        i = half * 64 + hg * 16 + hs
                        wge(v, "m_u", i + 1)
                        if hs == 0 and g >= 1:
                            wge(v, "m_pvd", 4 * g)
                        if i >= 1:
                            wge(v, "m_p", i)
                        ins = v.tensor_copy(A(O_GU + 512, 512), psU[:, :])
                        inc(ins, "m_ue", 1)
                        wge(v, "m_ge", i + 1)
                        wge(v, "m_ue")
                        ins = v.tensor_mul(A(O_PGR + hs * 512, 512), A(O_GU, 512),
                                           A(O_GU + 512, 512))
                        inc(ins, "m_p", 1)
                    for dq in range(4):
                        gi = half * 16 + hg * 4 + dq
                        wge(v, "m_pvd", gi + 1)
                        for ss in range(4):
                            ins = v.tensor_tensor(
                                xout[:, ss * D + dq * 512:ss * D + (dq + 1) * 512],
                                xout[:, ss * D + dq * 512:ss * D + (dq + 1) * 512],
                                psO[:, ss * 512:(ss + 1) * 512], ALU.add)
                            inc(ins, "m_fl", 1)

        @blk.scalar
        def _(sc):
            for half in range(2):
                for tq in range(4):
                    i = half * 4 + tq
                    wge(sc, "m_ag", i + 1)
                    ins = sc.activation(rstd[:, :], st2[:, 1:2], AF.Ln, bias=cst[:, 0:1])
                    inc(ins, "m_ln1", 1)
                    wge(sc, "m_ln1")
                    ins = sc.activation(rstd[:, :], rstd[:, :], AF.Exp, scale=-0.5)
                    inc(ins, "m_rs1", 1)
                    wge(sc, "m_nm", i + 1)
                    ins = sc.activation(fst[:, :], xout[:, tq * D:(tq + 1) * D], AF.Identity,
                                        scale=rstd[:, 0:1], bias=nmean[:, 0:1])
                    inc(ins, "m_xn1", 1)
                    wge(sc, "m_ag2", i + 1)
                    ins = sc.activation(rstd2[:, :], st2b[:, 1:2], AF.Ln, bias=cst[:, 1:2])
                    inc(ins, "m_ln2", 1)
                    wge(sc, "m_ln2")
                    ins = sc.activation(rstd2[:, :], rstd2[:, :], AF.Exp, scale=-0.5)
                    inc(ins, "m_rs2", 1)
                    wge(sc, "m_nm2", i + 1)
                    ins = sc.activation(fst[:, :], f2[:, :], AF.Identity,
                                        scale=rstd2[:, 0:1], bias=nm2[:, 0:1])
                    inc(ins, "m_y0", 1)
                    wge(sc, "m_xqv", 2 * (i + 1))
                    ins = sc.dma_start(out=xq_sc[i * 128:(i + 1) * 128, :],
                                       in_=sgst[:, D:2 * D])
                    inc(ins, "m_xq", 16)
                for hg in range(4):
                    for hs in range(16):
                        i = half * 64 + hg * 16 + hs
                        wge(sc, "m_g", i + 1)
                        if i >= 1:
                            wge(sc, "m_p", i)
                        ins = sc.activation(A(O_GU, 512), psG[:, :], AF.Silu)
                        inc(ins, "m_ge", 1)

        @blk.tensor
        def _(t):
            for half in range(2):
                wge(t, "m_xqt", 16 * 16 * (half + 1))
                for hg in range(4):
                    for hs in range(16):
                        i = half * 64 + hg * 16 + hs
                        wge(t, "m_wg%d" % (i % 2), 16 * (i // 2 + 1))
                        wge(t, "m_wu%d" % (i % 2), 16 * (i // 2 + 1))
                        if i >= 1:
                            wge(t, "m_ge", i)
                            wge(t, "m_ue", i)
                        for dc in range(16):
                            ins = t.matmul(psG[:, :],
                                           A(O_WGS + (i % 2) * 2048 + dc * 128, 128),
                                           A(O_XQT + dc * 512, 512),
                                           start=(dc == 0), stop=(dc == 15))
                        inc(ins, "m_g", 1)
                        for dc in range(16):
                            ins = t.matmul(psU[:, :],
                                           A(O_WUS + (i % 2) * 2048 + dc * 128, 128),
                                           A(O_XQT + dc * 512, 512),
                                           start=(dc == 0), stop=(dc == 15))
                        inc(ins, "m_u", 1)
                    for dq in range(4):
                        gi = half * 16 + hg * 4 + dq
                        if gi >= 1:
                            wge(t, "m_fl", gi * 4)
                        wge(t, "m_p", (half * 4 + hg) * 16 + 16)
                        for hs in range(16):
                            i = ((half * 4 + hg) * 4 + dq) * 16 + hs
                            wge(t, "m_wd%d" % (i % 2), 16 * (i // 2 + 1))
                            for ss in range(4):
                                ins = t.matmul(psO[:, ss * 512:(ss + 1) * 512],
                                               AAP(O_PGR + hs * 512 + ss * 128, [[1, 128]]),
                                               A(O_WDS + (i % 2) * 512, 512),
                                               start=(hs == 0), stop=(hs == 15))
                            if hs == 15:
                                inc(ins, "m_pvd", 1)
                            else:
                                inc(ins, "m_dn", 1)

    BAR()
    if stop == 'X14':
        es.close()
        return nc
    es.close()
    return nc


def kernel(**inputs):
    in_maps, plans = _host_prep(inputs)
    nc = build_program()
    res = run_bass_kernel_spmd(nc, in_maps, core_ids=list(range(8)))
    out = np.empty((S, D), np.float32)
    for core in range(8):
        a, b = plans[core]
        oo = res.results[core]["out_own"]
        out[a * CH:(a + 1) * CH] = oo[0:CH]
        out[b * CH:(b + 1) * CH] = oo[CH:2 * CH]
    return out.reshape(1, S, D)


# version 06
# speedup vs baseline: 1.0769x; 1.0652x over previous
import contextlib
import math

import numpy as np
import ml_dtypes

import concourse.bass as bass
import concourse.mybir as mybir
from concourse.bass_utils import run_bass_kernel_spmd

F32 = mybir.dt.float32
BF16 = mybir.dt.bfloat16
I32 = mybir.dt.int32
AF = mybir.ActivationFunctionType
ALU = mybir.AluOpType
BF = ml_dtypes.bfloat16

S, D, H, DC, DP, LQ, LKV, FH = 8192, 2048, 16, 128, 64, 1536, 512, 8192
NCH, CH = 16, 512
SCALE = 1.0 / math.sqrt(DC + DP)
EPS_LN, EPS_RMS = 1e-5, 1e-8
NEG = -30000.0
RH = DP // 2
NKA, NKB = 8, 16
DIAG_A, DIAG_B = 7, 15
VW = 132



def _host_prep(inputs):
    x = np.ascontiguousarray(np.asarray(inputs["L0_attn_x"], np.float32).reshape(S, D))
    xc = x.reshape(NCH, CH, D)
    import jax
    import jax.numpy as jnp
    _cpu = jax.devices("cpu")[0]
    with jax.default_device(_cpu):
        _invf = 1.0 / 10000.0 ** (jnp.arange(RH, dtype=jnp.float32) / RH)

    def g(k):
        return np.ascontiguousarray(np.asarray(inputs[k], np.float32))

    shared = {
        "wq_d": g("L0_attn_Wq_d"), "wq_uc": g("L0_attn_Wq_uc"), "wq_up": g("L0_attn_Wq_up"),
        "wkv_d": g("L0_attn_Wkv_d"), "wk_uc": g("L0_attn_Wk_uc"), "wk_up": g("L0_attn_Wk_up"),
        "wv_u": g("L0_attn_Wv_u"),
        "bq_d": g("L0_attn_bq_d"), "bq_uc": g("L0_attn_bq_uc"),
        "bq_up": g("L0_attn_bq_up").reshape(1, H * DP),
        "bkv_d": g("L0_attn_bkv_d"), "bk_uc": g("L0_attn_bk_uc"),
        "bk_up": g("L0_attn_bk_up").reshape(1, DP),
        "bv_u": g("L0_attn_bv_u").reshape(1, H * DC),
        "wgT": np.ascontiguousarray(np.asarray(inputs["L0_ffn_W_g"], np.float32).T),
        "wuT": np.ascontiguousarray(np.asarray(inputs["L0_ffn_W_u"], np.float32).T),
        "wdT": np.ascontiguousarray(np.asarray(inputs["L0_ffn_W_d"], np.float32).T),
        "cvec": g("L0_c"), "wmod": g("L0_wmod"),
    }

    in_maps, plans = [], []
    for core in range(8):
        a, b = core, 15 - core
        others = [c for c in range(NCH) if c not in (a, b)]
        slots = others[:7] + [a] + others[7:] + [b]
        perm = np.asarray(slots)

        x_kv = np.ascontiguousarray(xc[perm].reshape(S, D))
        x_own = np.ascontiguousarray(np.concatenate([xc[a], xc[b]], axis=0))

        ppos = (perm[:, None] * CH + np.arange(CH)[None, :]).reshape(S)
        with jax.default_device(_cpu):
            _ang = jnp.asarray(ppos)[:, None].astype(jnp.float32) * _invf[None, :]
            _cs = np.asarray(jnp.cos(_ang))
            _sn = np.asarray(jnp.sin(_ang))
        rope = np.ascontiguousarray(np.concatenate([_cs, _sn], axis=1).astype(np.float32))

        mb = np.zeros((128, (NKA + NKB) * 4), dtype=np.float32)
        col = 0
        for qchunk, nslots, diag in ((a, NKA, DIAG_A), (b, NKB, DIAG_B)):
            for sl in range(nslots):
                keep = (sl == diag) or (slots[sl] < qchunk)
                mb[:, col:col + 4] = 0.0 if keep else NEG
                col += 4

        m = dict(shared)
        m.update({"x_own": x_own, "x_kv": x_kv, "rope_cs": rope,
                  "mbias": np.ascontiguousarray(mb)})
        in_maps.append(m)
        plans.append((a, b))
    return in_maps, plans


def _tri_masks():
    t = np.zeros((4 * 128, CH), dtype=BF)
    for j in range(4):
        kk = j * 128 + np.arange(128)
        t[j * 128:(j + 1) * 128] = (kk[:, None] <= np.arange(CH)[None, :]).astype(BF)
    return t



def build_program(stop=None):
    nc = bass.Bass("TRN2", target_bir_lowering=False)

    def din(name, shape):
        return nc.dram_tensor(name, shape, F32, kind="ExternalInput").ap()

    x_own = din("x_own", [1024, D])
    x_kv = din("x_kv", [S, D])
    rope_cs = din("rope_cs", [S, 2 * RH])
    mbias = din("mbias", [128, 96])
    wq_d = din("wq_d", [D, LQ]);      wq_uc = din("wq_uc", [LQ, H * DC])
    wq_up = din("wq_up", [LQ, H * DP]); wkv_d = din("wkv_d", [D, LKV])
    wk_uc = din("wk_uc", [LKV, H * DC]); wk_up = din("wk_up", [LKV, DP])
    wv_u = din("wv_u", [LKV, H * DC])
    bq_d = din("bq_d", [LQ]);   bq_uc = din("bq_uc", [H * DC])
    bq_up = din("bq_up", [1, H * DP]); bkv_d = din("bkv_d", [LKV])
    bk_uc = din("bk_uc", [H * DC]); bk_up = din("bk_up", [1, DP])
    bv_u = din("bv_u", [1, H * DC])
    wgT = din("wgT", [D, FH]); wuT = din("wuT", [D, FH]); wdT = din("wdT", [FH, D])
    cvec = din("cvec", [1, D]); wmod = din("wmod", [D, 6 * D])

    out_own = nc.dram_tensor("out_own", [1024, D], F32, kind="ExternalOutput").ap()

    tri_h = nc.inline_tensor(_tri_masks(), name="tri_h").ap()

    xn_sc = nc.dram_tensor("xn_sc", [S, D], BF16).ap()
    k_sc = nc.dram_tensor("k_sc", [H * DC, S], BF16).ap()
    v_sc = nc.dram_tensor("v_sc", [S, H * DC], BF16).ap()
    kr_sc = nc.dram_tensor("kr_sc", [DP, S], BF16).ap()
    ql_sc = nc.dram_tensor("ql_sc", [LQ, 1024], BF16).ap()
    qr_sc = nc.dram_tensor("qr_sc", [DP, H * 1024], BF16).ap()
    xq_sc = nc.dram_tensor("xq_sc", [1024, D], BF16).ap()
    x1_sc = nc.dram_tensor("x1_sc", [1024, D], F32).ap()
    row_sc = nc.dram_tensor("row_sc", [1, 8 * D], F32).ap()

    sbt = nc.alloc_sbuf_tensor
    ARN = 50816
    ar = sbt("arena", [128, ARN], BF16)

    def A(off, ln):
        assert 0 <= off and off + ln <= ARN, (off, ln)
        return ar[:, off:off + ln]

    def AAP(off, pattern, part=128):
        return bass.AP(ar, off, [[ARN, part]] + pattern)

    fst = sbt("fst", [128, D], F32)
    f2 = sbt("f2", [128, D], F32)
    xout = sbt("xout", [128, 4 * D], F32)
    sgst = sbt("sgst", [128, 2 * D], BF16)
    ost = sbt("ost", [128, 2 * DC], F32)
    mrow = sbt("mrow", [1, CH], F32)
    brow = sbt("brow", [1, D], BF16)
    brow2 = sbt("brow2", [1, D], BF16)
    modt = sbt("modt", [128, 96], F32)
    gtmp = sbt("gtmp", [128, 64], F32)
    ca1 = sbt("ca1", [128, 16], F32)
    cb1_rep = sbt("cb1_rep", [128, D], F32)
    sb_rep = sbt("sb_rep", [128, D], F32)
    ga_rep = sbt("ga_rep", [128, D], BF16)
    gb_rep = sbt("gb_rep", [128, D], BF16)
    biasq2 = sbt("biasq2", [128, 12], F32)
    biaskv2 = sbt("biaskv2", [128, 4], F32)
    bias_kc = sbt("bias_kc", [128, 16], F32)
    bias_qc = sbt("bias_qc", [128, 16], F32)
    brow_qup = sbt("brow_qup", [1, H * DP], BF16)
    brow_kup = sbt("brow_kup", [1, DP], BF16)
    brow_vu = sbt("brow_vu", [1, H * DC], BF16)
    ones1 = sbt("ones1", [1, 128], BF16)
    mb_sb = sbt("mb_sb", [128, 96], F32)
    tri_sb = sbt("tri_sb", [128, 4 * CH], BF16)
    bn24 = sbt("bn24", [128, 24], F32)
    st2 = sbt("st2", [128, 2], F32)
    rstd = sbt("rstd", [128, 1], F32)
    nmean = sbt("nmean", [128, 1], F32)
    st2b = sbt("st2b", [128, 2], F32)
    rstd2 = sbt("rstd2", [128, 1], F32)
    amax_t = sbt("amax_t", [128, 1], F32)
    srec = sbt("srec", [128, 1], F32)
    sc_ap = sbt("sc_ap", [128, 1], F32)
    nm2 = sbt("nm2", [128, 1], F32)
    lrec = sbt("lrec", [128, 1], F32)
    rcs = sbt("rcs", [128, 4 * RH], F32)
    cst = sbt("cst", [128, 4], F32)
    sa2b = sbt("sa2b", [128, 16], BF16)
    ti32 = sbt("ti32", [128, 1024], I32)

    psA = nc.alloc_psum_tensor("psA", [128, 512], F32)
    psB = nc.alloc_psum_tensor("psB", [128, 512], F32)
    psO = nc.alloc_psum_tensor("psO", [128, 4 * 512], F32)
    psG = nc.alloc_psum_tensor("psG", [128, 512], F32)
    psU = nc.alloc_psum_tensor("psU", [128, 512], F32)

    es = contextlib.ExitStack()
    es.enter_context(nc.allow_non_contiguous_dma("tiny strided bias/mod-column loads"))

    LONGLIVED = {"a0g", "a2", "g2"}
    SWDGE = {"a0g", "d0", "f0", "g2", "j_wqc0", "j_wqc1", "j_acc0", "j_acc1",
             "h_ks0", "h_ks1", "h_ks2", "h_ks3", "h_vst",
             "mc_wg0", "mc_wg1", "mc_wu0", "mc_wu1", "mc_wd0", "mc_wd1"}
    phys = {}
    phys_cnt = {}
    free_pool = []
    alias = {}
    dead = set()
    nphys = [0]

    def _new_phys():
        nm = f"p{nphys[0]}"
        nphys[0] += 1
        phys[nm] = es.enter_context(nc.semaphore(nm))
        phys_cnt[nm] = 0
        return nm

    def sem(name):
        if name in alias:
            return phys[alias[name][0]]
        assert name not in dead, f"logical sem {name} reused after its barrier"
        if name in LONGLIVED or name in SWDGE:
            pn = _new_phys()
        elif free_pool:
            pn = free_pool.pop()
        else:
            pn = _new_phys()
        alias[name] = (pn, phys_cnt[pn])
        return phys[pn]

    cnt = {}
    first_inc = set()

    def inc(ins, name, k=1):
        h = sem(name)
        pn, base = alias[name]
        if name not in first_inc:
            first_inc.add(name)
            if base > 0:
                ins._wait_ge(h, base)
        ins.then_inc(h, k)
        phys_cnt[pn] += k
        cnt[name] = cnt.get(name, 0) + k
        return cnt[name]

    def tot(name):
        return cnt.get(name, 0)

    def wge(eng, name, val=None):
        h = sem(name)
        _, base = alias[name]
        eng.wait_ge(h, base + (tot(name) if val is None else val))

    def BAR():
        nc.all_engine_barrier()
        for name, (pn, _) in list(alias.items()):
            if name in LONGLIVED or name in SWDGE:
                continue
            del alias[name]
            dead.add(name)
            if pn not in free_pool:
                free_pool.append(pn)

    with nc.Block() as blk:
        @blk.sync
        def _(sy):
            for o_, i_ in (
                (fst[0:1, 0:D], cvec),
                (biasq2[:, :], bass.AP(bq_d.tensor, 0, [[1, 128], [128, 12]])),
                (biaskv2[:, :], bass.AP(bkv_d.tensor, 0, [[1, 128], [128, 4]])),
                (bias_kc[:, :], bass.AP(bk_uc.tensor, 0, [[1, 128], [128, 16]])),
                (bias_qc[:, :], bass.AP(bq_uc.tensor, 0, [[1, 128], [128, 16]])),
                (mb_sb[:, :], mbias),
                (tri_sb[:, :], bass.AP(tri_h.tensor, 0, [[512, 128], [65536, 4], [1, 512]])),
            ):
                ins = sy.dma_start(out=o_, in_=i_)
                inc(ins, "a0", 16)

        @blk.vector
        def _(v):
            v.memset(ones1[:, :], 1.0)
            v.memset(cst[:, 0:1], EPS_LN)
            v.memset(cst[:, 1:2], EPS_RMS)
            v.memset(cst[:, 2:3], 1.0)

        @blk.gpsimd
        def _(gp):
            ins = gp.dma_start(out=brow_qup[:, :], in_=bq_up)
            inc(ins, "a0g", 16)
            ins = gp.dma_start(out=brow_kup[:, :], in_=bk_up)
            inc(ins, "a0g", 16)

        @blk.scalar
        def _(sc):
            wge(sc, "a0")
            ins = sc.activation(f2[0:1, 0:D], fst[0:1, 0:D], AF.Silu)
            inc(ins, "a1", 1)
            wge(sc, "a1")
            ins = sc.dma_start(out=row_sc[0:1, 0:D], in_=f2[0:1, 0:D])
            inc(ins, "a2", 16)

    BAR()
    if stop == 'B0a':
        es.close()
        return nc

    with nc.Block() as blk:
        @blk.sync
        def _(sy):
            wge(sy, "a2", 16)
            ins = sy.dma_start(out=modt[:, 0:16],
                               in_=bass.AP(row_sc.tensor, 0, [[1, 128], [128, 16]]))
            inc(ins, "a3", 16)
            for cg in range(24):
                for sub in range(8):
                    j = cg * 8 + sub
                    if j >= 2:
                        wge(sy, "modmm", j - 1)
                    ins = sy.dma_start(
                        out=(fst if j % 2 == 0 else f2)[:, 0:1024],
                        in_=bass.AP(wmod.tensor, 2 * sub * 128 * 6 * D + cg * 512,
                                    [[6 * D, 128], [128 * 6 * D, 2], [1, 512]]))
                    inc(ins, "modld%d" % (j % 2), 16)

        @blk.tensor
        def _(t):
            wge(t, "a3", 16)
            for cg in range(24):
                for sub in range(8):
                    j = cg * 8 + sub
                    wge(t, "modld%d" % (j % 2), 16 * (j // 2 + 1))
                    if cg >= 1 and sub == 0:
                        wge(t, "modcp", cg)
                    st = fst if j % 2 == 0 else f2
                    for dc in range(2):
                        ins = t.matmul(psA[0:1, :], modt[:, 2 * sub + dc:2 * sub + dc + 1],
                                       st[:, dc * 512:(dc + 1) * 512],
                                       start=(sub == 0 and dc == 0),
                                       stop=(sub == 7 and dc == 1))
                    inc(ins, "modmm", 1)

        @blk.scalar
        def _(sc):
            for cg in range(24):
                wge(sc, "modmm", 8 * (cg + 1))
                if cg >= 1:
                    wge(sc, "modst", 16 * cg)
                ins = sc.copy(mrow[:, :], psA[0:1, :])
                inc(ins, "modcp", 1)
                wge(sc, "modcp")
                ins = sc.dma_start(out=row_sc[0:1, cg * 512:(cg + 1) * 512], in_=mrow[:, :])
                inc(ins, "modst", 16)
            wge(sc, "modst")

    BAR()
    if stop == 'B0b':
        es.close()
        return nc

    with nc.Block() as blk:
        @blk.sync
        def _(sy):
            ins = sy.dma_start(out=modt[:, 0:96],
                               in_=bass.AP(row_sc.tensor, 0, [[1, 128], [128, 96]]))
            inc(ins, "c0", 16)

        @blk.scalar
        def _(sc):
            wge(sc, "c0", 16)
            sc.activation(gtmp[:, 0:16], modt[:, 32:48], AF.Abs)
            sc.activation(gtmp[:, 16:32], modt[:, 80:96], AF.Abs)
            sc.activation(gtmp[:, 32:48], modt[:, 32:48], AF.Sign)
            ins = sc.activation(gtmp[:, 48:64], modt[:, 80:96], AF.Sign)
            inc(ins, "c1", 1)

        @blk.vector
        def _(v):
            wge(v, "c1", 1)
            v.tensor_scalar_max(gtmp[:, 0:16], gtmp[:, 0:16], 0.25)
            ins = v.tensor_scalar_max(gtmp[:, 16:32], gtmp[:, 16:32], 0.25)
            inc(ins, "c2", 1)
            wge(v, "c2")
            v.tensor_mul(modt[:, 32:48], gtmp[:, 0:16], gtmp[:, 32:48])
            v.tensor_mul(modt[:, 80:96], gtmp[:, 16:32], gtmp[:, 48:64])
            ins = v.tensor_scalar_add(ca1[:, :], modt[:, 16:32], 1.0)
            inc(ins, "c3", 1)

        @blk.scalar
        def _(sc):
            wge(sc, "c3", 1)
            ins = sc.dma_start(out=bass.AP(row_sc.tensor, 2 * D, [[1, 128], [128, 16]]),
                                in_=modt[:, 32:48])
            inc(ins, "c4", 16)
            ins = sc.dma_start(out=bass.AP(row_sc.tensor, 5 * D, [[1, 128], [128, 16]]),
                               in_=modt[:, 80:96])
            inc(ins, "c4", 16)
            wge(sc, "c4")

    BAR()
    if stop == 'B0c':
        es.close()
        return nc

    with nc.Block() as blk:
        @blk.gpsimd
        def _(gp):
            for o_, i_ in (
                (brow[0:1, :], bass.AP(row_sc.tensor, 4 * D, [[1, 1], [1, D]])),
                (sgst[0:1, 0:D], bass.AP(row_sc.tensor, 3 * D, [[1, 1], [1, D]])),
                (sgst[0:1, D:2 * D], bass.AP(row_sc.tensor, 2 * D, [[1, 1], [1, D]])),
                (brow2[0:1, :], bass.AP(row_sc.tensor, 5 * D, [[1, 1], [1, D]])),
                (brow_vu[0:1, :], bv_u),
            ):
                ins = gp.dma_start(out=o_, in_=i_)
                inc(ins, "d0", 16)

        @blk.tensor
        def _(t):
            wge(t, "d0")
            for j in range(4):
                if j >= 1:
                    wge(t, "d2", j)
                t.matmul(psA[:, :], ones1[:, :], brow[0:1, j * 512:(j + 1) * 512],
                         start=True, stop=True)
                t.matmul(psB[:, :], ones1[:, :], sgst[0:1, j * 512:(j + 1) * 512],
                         start=True, stop=True)
                t.matmul(psG[:, :], ones1[:, :], sgst[0:1, D + j * 512:D + (j + 1) * 512],
                         start=True, stop=True)
                ins = t.matmul(psU[:, :], ones1[:, :], brow2[0:1, j * 512:(j + 1) * 512],
                               start=True, stop=True)
                inc(ins, "d1", 1)

        @blk.scalar
        def _(sc):
            for j in range(4):
                wge(sc, "d1", j + 1)
                sc.activation(cb1_rep[:, j * 512:(j + 1) * 512], psA[:, :], AF.Identity, bias=cst[:, 2:3])
                sc.copy(sb_rep[:, j * 512:(j + 1) * 512], psB[:, :])
                sc.copy(ga_rep[:, j * 512:(j + 1) * 512], psG[:, :])
                ins = sc.copy(gb_rep[:, j * 512:(j + 1) * 512], psU[:, :])
                inc(ins, "d2", 1)

        @blk.vector
        def _(v):
            wge(v, "d2", 4)
            ins = v.tensor_mul(brow_vu[0:1, :], brow_vu[0:1, :], ga_rep[0:1, :])
            inc(ins, "d3", 1)

    BAR()
    if stop == 'B0d':
        es.close()
        return nc

    ld0 = fst[:, :]
    ld1 = f2[:, :]

    with nc.Block() as blk:
        @blk.sync
        def _(sy):
            for i in range(64):
                if i >= 2:
                    wge(sy, "b1n", i - 1)
                ins = sy.dma_start(out=(ld0 if i % 2 == 0 else ld1),
                                   in_=x_kv[i * 128:(i + 1) * 128, :])
                inc(ins, "b1x%d" % (i % 2), 16)

        @blk.vector
        def _(v):
            for i in range(64):
                wge(v, "b1x%d" % (i % 2), 16 * (i // 2 + 1))
                if i >= 1:
                    wge(v, "b1ln", i)
                src = ld0 if i % 2 == 0 else ld1
                for j in range(4):
                    ins = v.bn_stats(bn24[:, j * 6:(j + 1) * 6], src[:, j * 512:(j + 1) * 512])
                inc(ins, "b1bn", 1)
                wge(v, "b1bn")
                ins = v.bn_aggr(st2[:, :], bn24[:, :])
                inc(ins, "b1ag", 1)
                wge(v, "b1ex", i + 1)
                if i >= 1:
                    wge(v, "b1n", i)
                ins = v.tensor_scalar(nmean[:, :], st2[:, 0:1], rstd[:, 0:1], -1.0,
                                      ALU.mult, ALU.mult)
                inc(ins, "b1nm", 1)

        @blk.scalar
        def _(sc):
            for i in range(64):
                src = ld0 if i % 2 == 0 else ld1
                wge(sc, "b1ag", i + 1)
                ins = sc.activation(rstd[:, :], st2[:, 1:2], AF.Ln, bias=cst[:, 0:1])
                inc(ins, "b1ln", 1)
                wge(sc, "b1ln")
                ins = sc.activation(rstd[:, :], rstd[:, :], AF.Exp, scale=-0.5)
                inc(ins, "b1ex", 1)
                wge(sc, "b1nm", i + 1)
                if i >= 2:
                    wge(sc, "b1s%d" % (i % 2), 16 * (i // 2))
                ins = sc.activation(sgst[:, (i % 2) * D:(i % 2 + 1) * D], src, AF.Identity,
                                    scale=rstd[:, 0:1], bias=nmean[:, 0:1])
                inc(ins, "b1n", 1)
                wge(sc, "b1n")
                ins = sc.dma_start(out=xn_sc[i * 128:(i + 1) * 128, :],
                                   in_=sgst[:, (i % 2) * D:(i % 2 + 1) * D])
                inc(ins, "b1s%d" % (i % 2), 16)
            wge(sc, "b1s0")
            wge(sc, "b1s1")

    BAR()
    if stop == 'B1':
        es.close()
        return nc

    O_WQD = 0

    with nc.Block() as blk:
        @blk.sync
        def _(sy):
            for dc in range(16):
                if dc >= 2:
                    wge(sy, "e0", dc - 1)
                ins = sy.dma_start(out=(fst if dc % 2 == 0 else f2)[:, 0:1536],
                                   in_=wq_d[dc * 128:(dc + 1) * 128, :])
                inc(ins, "e0ld%d" % (dc % 2), 16)

        @blk.vector
        def _(v):
            ins = v.reciprocal(gtmp[:, 0:16], ca1[:, :])
            inc(ins, "e_sa2r", 1)
            wge(v, "e_sa2r")
            ins = v.tensor_tensor(sa2b[:, :], modt[:, 0:16], gtmp[:, 0:16], ALU.mult)
            inc(ins, "e_sa2", 1)
            for dc in range(16):
                wge(v, "e0ld%d" % (dc % 2), 16 * (dc // 2 + 1))
                src = fst if dc % 2 == 0 else f2
                ins = v.tensor_scalar_mul(A(O_WQD + dc * LQ, LQ), src[:, 0:1536],
                                          ca1[:, dc:dc + 1])
                inc(ins, "e0", 1)

        @blk.tensor
        def _(t):
            wge(t, "e0", 16)
            wge(t, "e_sa2", 1)
            for grp in range(3):
                for dc in range(16):
                    ins = t.matmul(psO[0:1, grp * 512:grp * 512 + 512],
                                   sa2b[:, dc:dc + 1],
                                   A(O_WQD + dc * LQ + grp * 512, 512),
                                   start=(dc == 0), stop=(dc == 15))
                inc(ins, "e1", 1)

        @blk.scalar
        def _(sc):
            wge(sc, "e1", 3)
            for grp in range(3):
                if grp >= 1:
                    wge(sc, "e3", 16 * grp)
                ins = sc.copy(mrow[:, :], psO[0:1, grp * 512:grp * 512 + 512])
                inc(ins, "e2", 1)
                wge(sc, "e2")
                ins = sc.dma_start(out=row_sc[0:1, 6 * D + grp * 512:6 * D + (grp + 1) * 512],
                                   in_=mrow[:, :])
                inc(ins, "e3", 16)
            wge(sc, "e3")

    BAR()
    if stop == 'B2a':
        es.close()
        return nc

    with nc.Block() as blk:
        @blk.sync
        def _(sy):
            ins = sy.dma_start(out=gtmp[:, 32:44],
                               in_=bass.AP(row_sc.tensor, 6 * D, [[1, 128], [128, 12]]))
            inc(ins, "e4", 16)

        @blk.vector
        def _(v):
            wge(v, "e4", 16)
            ins = v.tensor_add(biasq2[:, :], biasq2[:, :], gtmp[:, 32:44])
            inc(ins, "e5", 1)

    BAR()
    if stop == 'B2a2':
        es.close()
        return nc

    O_XNTQ = 24576

    with nc.Block() as blk:
        @blk.sync
        def _(sy):
            for dc in range(16):
                for half, slot in ((0, DIAG_A), (1, DIAG_B)):
                    ins = sy.dma_start(
                        out=A(O_XNTQ + dc * 1024 + half * 512, 512),
                        in_=xn_sc[slot * CH:(slot + 1) * CH, dc * 128:(dc + 1) * 128],
                        transpose=True)
                    inc(ins, "e6", 16)

        @blk.tensor
        def _(t):
            wge(t, "e6", 16 * 32)
            k = 0
            for lqs in range(12):
                for half in range(2):
                    k += 1
                    ps = psA if half == 0 else psB
                    if k > 2:
                        wge(t, "e8", k - 2)
                    for dc in range(16):
                        ins = t.matmul(ps[:, :], A(O_WQD + dc * LQ + lqs * 128, 128),
                                       A(O_XNTQ + dc * 1024 + half * 512, 512),
                                       start=(dc == 0), stop=(dc == 15))
                    inc(ins, "e7", 1)

        @blk.scalar
        def _(sc):
            k = 0
            for lqs in range(12):
                for half in range(2):
                    k += 1
                    ps = psA if half == 0 else psB
                    wge(sc, "e7", k)
                    if lqs >= 1:
                        wge(sc, "e9h%d" % half, 16 * lqs)
                    ins = sc.activation(sgst[:, half * 512:half * 512 + 512], ps[:, :],
                                        AF.Identity, bias=biasq2[:, lqs:lqs + 1])
                    inc(ins, "e8", 1)
                    wge(sc, "e8")
                    ins = sc.dma_start(out=ql_sc[lqs * 128:(lqs + 1) * 128,
                                                 half * 512:(half + 1) * 512],
                                       in_=sgst[:, half * 512:half * 512 + 512])
                    inc(ins, "e9h%d" % half, 16)
            wge(sc, "e9h0")
            wge(sc, "e9h1")

    BAR()
    if stop == 'B2b':
        es.close()
        return nc

    O_WQUP, O_QLST, O_QRT = 0, 12288, 15360

    with nc.Block() as blk:
        @blk.gpsimd
        def _(gp):
            ins = gp.dma_start(
                out=AAP(O_WQUP, [[1024, 12], [1, 1024]]),
                in_=bass.AP(wq_up.tensor, 0, [[1024, 128], [128 * 1024, 12], [1, 1024]]))
            inc(ins, "f0", 16)

        @blk.sync
        def _(sy):
            for qs in range(8):
                if qs >= 2:
                    wge(sy, "f_mm", 2 * (qs - 1))
                ins = sy.dma_start(
                    out=AAP(O_QLST + (qs % 2) * 1536, [[128, 12], [1, 128]]),
                    in_=bass.AP(ql_sc.tensor, qs * 128,
                                [[1024, 128], [128 * 1024, 12], [1, 128]]))
                inc(ins, "f_ql%d" % (qs % 2), 16)
                if qs >= 1:
                    wge(sy, "f_ra", qs)
                slot = DIAG_A if qs < 4 else DIAG_B
                roff = slot * CH + (qs % 4) * 128
                ins = sy.dma_start(out=rcs[:, 0:64], in_=rope_cs[roff:roff + 128, :])
                inc(ins, "f_rc", 16)

        @blk.tensor
        def _(t):
            wge(t, "f0", 16)
            wge(t, "a0g", 32)
            for qs in range(8):
                wge(t, "f_ql%d" % (qs % 2), 16 * (qs // 2 + 1))
                for half in range(2):
                    ps = psA if half == 0 else psB
                    if qs * 2 + half >= 2:
                        wge(t, "f_ev", qs * 2 + half - 1)
                    for lq in range(12):
                        t.matmul(ps[:, :],
                                 AAP(O_QLST + (qs % 2) * 1536 + lq * 128, [[1, 128]]),
                                 A(O_WQUP + lq * 1024 + half * 512, 512),
                                 start=(lq == 0), stop=False)
                    ins = t.matmul(ps[:, :], ones1[:, :],
                                   brow_qup[0:1, half * 512:(half + 1) * 512],
                                   start=False, stop=True)
                    inc(ins, "f_mm", 1)

        @blk.scalar
        def _(sc):
            for qs in range(8):
                for half in range(2):
                    ps = psA if half == 0 else psB
                    wge(sc, "f_mm", qs * 2 + half + 1)
                    if qs >= 1 and half == 0:
                        wge(sc, "f_ra", qs)
                    ins = sc.activation(fst[:, half * 512:(half + 1) * 512], ps[:, :],
                                        AF.Identity, scale=SCALE)
                    inc(ins, "f_ev", 1)
                wge(sc, "f_bf", qs + 1)
                for hp in range(8):
                    wge(sc, "f_tp")
                    ins = sc.dma_start(out=A(O_QRT + (hp % 2) * 128, 128),
                                       in_=sgst[:, hp * 128:(hp + 1) * 128],
                                       transpose=True)
                    inc(ins, "f_tp", 16)
                    wge(sc, "f_tp")
                    ins = sc.dma_start(
                        out=bass.AP(qr_sc.tensor, (2 * hp) * 1024 + qs * 128,
                                    [[H * 1024, 64], [1, 128]]),
                        in_=ar[0:64, O_QRT + (hp % 2) * 128:O_QRT + (hp % 2) * 128 + 128])
                    inc(ins, "f_tp", 16)
                    ins = sc.dma_start(
                        out=bass.AP(qr_sc.tensor, (2 * hp + 1) * 1024 + qs * 128,
                                    [[H * 1024, 64], [1, 128]]),
                        in_=ar[64:128, O_QRT + (hp % 2) * 128:O_QRT + (hp % 2) * 128 + 128])
                    inc(ins, "f_tp", 16)
            wge(sc, "f_tp")

        @blk.vector
        def _(v):
            for qs in range(8):
                wge(v, "f_ev", qs * 2 + 2)
                wge(v, "f_rc", 16 * (qs + 1))
                if qs >= 1:
                    wge(v, "f_tp", 16 * 24 * qs)
                x1v = bass.AP(fst, 0, [[D, 128], [64, 16], [1, 32]])
                x2v = bass.AP(fst, 32, [[D, 128], [64, 16], [1, 32]])
                cosv = bass.AP(rcs, 0, [[4 * RH, 128], [0, 16], [1, 32]])
                sinv = bass.AP(rcs, RH, [[4 * RH, 128], [0, 16], [1, 32]])
                o1 = bass.AP(f2, 0, [[D, 128], [64, 16], [1, 32]])
                o2 = bass.AP(f2, 32, [[D, 128], [64, 16], [1, 32]])
                g1 = bass.AP(f2, 1024, [[D, 128], [64, 16], [1, 32]])
                g2 = bass.AP(f2, 1024 + 32, [[D, 128], [64, 16], [1, 32]])
                v.tensor_tensor(o1, x1v, cosv, ALU.mult)
                v.tensor_tensor(o2, x1v, sinv, ALU.mult)
                v.tensor_tensor(g1, x2v, sinv, ALU.mult)
                ins = v.tensor_tensor(g2, x2v, cosv, ALU.mult)
                inc(ins, "f_ra", 1)
                wge(v, "f_ra")
                v.tensor_tensor(o1, o1, g1, ALU.subtract)
                ins = v.tensor_tensor(o2, o2, g2, ALU.add)
                inc(ins, "f_rb", 1)
                wge(v, "f_rb")
                ins = v.tensor_copy(sgst[:, 0:1024], f2[:, 0:1024])
                inc(ins, "f_bf", 1)


    BAR()
    if stop == 'B2c':
        es.close()
        return nc

    O_WKVD, O_WKUC, O_WVU, O_WKUP = 0, 8192, 16384, 24576

    with nc.Block() as blk:
        @blk.sync
        def _(sy):
            for dc in range(16):
                if dc >= 2:
                    wge(sy, "g0", dc - 1)
                ins = sy.dma_start(out=(fst if dc % 2 == 0 else f2)[:, 0:512],
                                   in_=wkv_d[dc * 128:(dc + 1) * 128, :])
                inc(ins, "g0ld%d" % (dc % 2), 16)
            for lk in range(4):
                if lk == 0:
                    wge(sy, "g0", 15)
                else:
                    wge(sy, "g1", lk)
                ins = sy.dma_start(out=(fst if lk % 2 == 0 else f2)[:, 0:D],
                                   in_=wv_u[lk * 128:(lk + 1) * 128, :])
                inc(ins, "g1ld%d" % (lk % 2), 16)

        @blk.gpsimd
        def _(gp):
            ins = gp.dma_start(
                out=AAP(O_WKUC, [[2048, 4], [1, 2048]]),
                in_=bass.AP(wk_uc.tensor, 0, [[2048, 128], [128 * 2048, 4], [1, 2048]]))
            inc(ins, "g2", 16)
            ins = gp.dma_start(
                out=AAP(O_WKUP, [[64, 4], [1, 64]]),
                in_=bass.AP(wk_up.tensor, 0, [[64, 128], [128 * 64, 4], [1, 64]]))
            inc(ins, "g2", 16)

        @blk.vector
        def _(v):
            for dc in range(16):
                wge(v, "g0ld%d" % (dc % 2), 16 * (dc // 2 + 1))
                src = fst if dc % 2 == 0 else f2
                ins = v.tensor_scalar_mul(A(O_WKVD + dc * 512, 512), src[:, 0:512],
                                          ca1[:, dc:dc + 1])
                inc(ins, "g0", 1)
            for lk in range(4):
                wge(v, "g1ld%d" % (lk % 2), 16 * (lk // 2 + 1))
                src = fst if lk % 2 == 0 else f2
                ins = v.tensor_tensor(A(O_WVU + lk * 2048, 2048), src[:, 0:D],
                                      ga_rep[:, :], ALU.mult)
                inc(ins, "g1", 1)

        @blk.tensor
        def _(t):
            wge(t, "g0", 16)
            for dc in range(16):
                ins = t.matmul(psO[0:1, 0:512], sa2b[:, dc:dc + 1],
                               A(O_WKVD + dc * 512, 512),
                               start=(dc == 0), stop=(dc == 15))
            inc(ins, "g3", 1)

        @blk.scalar
        def _(sc):
            wge(sc, "g3", 1)
            ins = sc.copy(mrow[:, :], psO[0:1, 0:512])
            inc(ins, "g4", 1)
            wge(sc, "g4")
            ins = sc.dma_start(out=row_sc[0:1, 6 * D:6 * D + 512], in_=mrow[:, :])
            inc(ins, "g5", 16)
            wge(sc, "g5")

    BAR()
    if stop == 'B3a':
        es.close()
        return nc

    with nc.Block() as blk:
        @blk.sync
        def _(sy):
            ins = sy.dma_start(out=gtmp[:, 32:36],
                               in_=bass.AP(row_sc.tensor, 6 * D, [[1, 128], [128, 4]]))
            inc(ins, "g6", 16)

        @blk.vector
        def _(v):
            wge(v, "g6", 16)
            ins = v.tensor_add(biaskv2[:, :], biaskv2[:, :], gtmp[:, 32:36])
            inc(ins, "g7", 1)

    BAR()
    if stop == 'B3a2':
        es.close()
        return nc

    O_XNT, O_KVL = 24832, 41216

    def kvl(buf, lk):
        return A(O_KVL + buf * 2048 + lk * 512, 512)

    with nc.Block() as blk:
        @blk.sync
        def _(sy):
            for kc in range(16):
                if kc >= 2:
                    wge(sy, "h_kv", 4 * (kc - 1))
                for dc in range(16):
                    ins = sy.dma_start(
                        out=A(O_XNT + (kc % 2) * 8192 + dc * 512, 512),
                        in_=xn_sc[kc * CH:(kc + 1) * CH, dc * 128:(dc + 1) * 128],
                        transpose=True)
                    inc(ins, "h_xn%d" % (kc % 2), 16)
                for ks in range(4):
                    i = kc * 4 + ks
                    if i >= 2:
                        wge(sy, "h_ra", i - 1)
                    ins = sy.dma_start(
                        out=rcs[:, (i % 2) * 64:(i % 2) * 64 + 64],
                        in_=rope_cs[kc * CH + ks * 128:kc * CH + (ks + 1) * 128, :])
                    inc(ins, "h_rc%d" % (i % 2), 16)

        @blk.tensor
        def _(t):
            wge(t, "g2", 32)
            for kc in range(16):
                wge(t, "h_xn%d" % (kc % 2), 256 * (kc // 2 + 1))
                if kc >= 1:
                    wge(t, "h_kev", 16 * kc)
                for ls in range(4):
                    ps = psA if ls % 2 == 0 else psB
                    if kc * 4 + ls >= 2:
                        wge(t, "h_kvev", kc * 4 + ls - 1)
                    for dc in range(16):
                        ins = t.matmul(ps[:, :], A(O_WKVD + dc * 512 + ls * 128, 128),
                                       A(O_XNT + (kc % 2) * 8192 + dc * 512, 512),
                                       start=(dc == 0), stop=(dc == 15))
                    inc(ins, "h_kv", 1)
                wge(t, "h_kvev", 4 * (kc + 1))
                for hs in range(16):
                    ps = psA if hs % 2 == 0 else psB
                    if kc * 16 + hs >= 2:
                        wge(t, "h_kev", kc * 16 + hs - 1)
                    for lk in range(4):
                        ins = t.matmul(ps[:, :], A(O_WKUC + lk * 2048 + hs * 128, 128),
                                       kvl(kc % 2, lk),
                                       start=(lk == 0), stop=(lk == 3))
                    inc(ins, "h_k", 1)
                for ks in range(4):
                    if kc * 4 + ks >= 1:
                        wge(t, "h_vev", (kc * 4 + ks) * 4)
                    for lk in range(4):
                        for hq in range(4):
                            t.matmul(psO[:, hq * 512:(hq + 1) * 512],
                                     AAP(O_KVL + (kc % 2) * 2048 + lk * 512 + ks * 128,
                                         [[1, 128]]),
                                     A(O_WVU + lk * 2048 + hq * 512, 512),
                                     start=(lk == 0), stop=False)
                    for hq in range(4):
                        ins = t.matmul(psO[:, hq * 512:(hq + 1) * 512], ones1[:, :],
                                       brow_vu[0:1, hq * 512:(hq + 1) * 512],
                                       start=False, stop=True)
                        inc(ins, "h_v", 1)
                for ks in range(4):
                    if kc * 4 + ks >= 1:
                        wge(t, "h_rev", kc * 4 + ks)
                    for lk in range(4):
                        t.matmul(psG[:, 0:64],
                                 AAP(O_KVL + (kc % 2) * 2048 + lk * 512 + ks * 128,
                                     [[1, 128]]),
                                 A(O_WKUP + lk * 64, 64),
                                 start=(lk == 0), stop=False)
                    ins = t.matmul(psG[:, 0:64], ones1[:, :], brow_kup[0:1, :],
                                   start=False, stop=True)
                    inc(ins, "h_r", 1)

        @blk.scalar
        def _(sc):
            for kc in range(16):
                for ls in range(4):
                    ps = psA if ls % 2 == 0 else psB
                    wge(sc, "h_kv", kc * 4 + ls + 1)
                    ins = sc.activation(kvl(kc % 2, ls), ps[:, :], AF.Identity,
                                        bias=biaskv2[:, ls:ls + 1])
                    inc(ins, "h_kvev", 1)
                for hs in range(16):
                    ps = psA if hs % 2 == 0 else psB
                    wge(sc, "h_k", kc * 16 + hs + 1)
                    q = hs % 4
                    nuse = kc * 4 + hs // 4
                    if nuse >= 1:
                        wge(sc, "h_ks%d" % q, 16 * nuse)
                    if kc >= 1 and hs < 4:
                        wge(sc, "h_vst", 64 * kc)
                    ins = sc.activation(sgst[:, q * 512:(q + 1) * 512], ps[:, :],
                                        AF.Identity, bias=bias_kc[:, hs:hs + 1])
                    inc(ins, "h_kev", 1)
                for ks in range(4):
                    if ks >= 1:
                        wge(sc, "h_vst", 16 * (kc * 4 + ks))
                    for hq in range(4):
                        wge(sc, "h_v", kc * 16 + ks * 4 + hq + 1)
                        if ks == 0:
                            wge(sc, "h_ks%d" % hq, 16 * (kc + 1) * 4)
                        ins = sc.copy(sgst[:, hq * 512:(hq + 1) * 512],
                                      psO[:, hq * 512:(hq + 1) * 512])
                        inc(ins, "h_vev", 1)
                for ks in range(4):
                    wge(sc, "h_r", kc * 4 + ks + 1)
                    if kc * 4 + ks >= 4:
                        wge(sc, "h_ra", kc * 4 + ks - 3)
                    ins = sc.activation(fst[:, ks * 64:(ks + 1) * 64], psG[:, 0:64],
                                        AF.Identity)
                    inc(ins, "h_rev", 1)
                for pair in range(2):
                    wge(sc, "h_bf", kc * 4 + pair * 2 + 2)
                    wge(sc, "h_tp")
                    ins = sc.dma_start(out=A(45312, 128),
                                       in_=sgst[:, D:D + 128], transpose=True)
                    inc(ins, "h_tp", 16)
                    wge(sc, "h_tp")
                    k0 = kc * CH + pair * 256
                    ins = sc.dma_start(
                        out=bass.AP(kr_sc.tensor, k0, [[S, 64], [1, 128]]),
                        in_=ar[0:64, 45312:45312 + 128])
                    inc(ins, "h_tp", 16)
                    ins = sc.dma_start(
                        out=bass.AP(kr_sc.tensor, k0 + 128, [[S, 64], [1, 128]]),
                        in_=ar[64:128, 45312:45312 + 128])
                    inc(ins, "h_tp", 16)
            wge(sc, "h_tp")

        @blk.gpsimd
        def _(gp):
            for kc in range(16):
                for hs in range(16):
                    wge(gp, "h_kev", kc * 16 + hs + 1)
                    q = hs % 4
                    ins = gp.dma_start(out=k_sc[hs * 128:(hs + 1) * 128,
                                                kc * 512:(kc + 1) * 512],
                                       in_=sgst[:, q * 512:(q + 1) * 512])
                    inc(ins, "h_ks%d" % q, 16)
                for ks in range(4):
                    wge(gp, "h_vev", kc * 16 + ks * 4 + 4)
                    ins = gp.dma_start(
                        out=v_sc[kc * CH + ks * 128:kc * CH + (ks + 1) * 128, :],
                        in_=sgst[:, 0:2048])
                    inc(ins, "h_vst", 16)
            wge(gp, "h_vst", 16 * 64)
            for q in range(4):
                wge(gp, "h_ks%d" % q, 16 * 64)

        @blk.vector
        def _(v):
            for kc in range(16):
                for ks in range(4):
                    i = kc * 4 + ks
                    pair = i // 2
                    wge(v, "h_rev", i + 1)
                    wge(v, "h_rc%d" % (i % 2), 16 * (i // 2 + 1))
                    if pair >= 1 and i % 2 == 0:
                        wge(v, "h_tp", 48 * pair)
                    src1 = bass.AP(fst, ks * 64, [[D, 128], [1, 32]])
                    src2 = bass.AP(fst, ks * 64 + 32, [[D, 128], [1, 32]])
                    cosv = bass.AP(rcs, (i % 2) * 64, [[4 * RH, 128], [1, 32]])
                    sinv = bass.AP(rcs, (i % 2) * 64 + RH, [[4 * RH, 128], [1, 32]])
                    o1 = bass.AP(f2, (ks % 2) * 64, [[D, 128], [1, 32]])
                    o2 = bass.AP(f2, (ks % 2) * 64 + 32, [[D, 128], [1, 32]])
                    g1 = bass.AP(f2, 1024 + (ks % 2) * 64, [[D, 128], [1, 32]])
                    g2 = bass.AP(f2, 1024 + (ks % 2) * 64 + 32, [[D, 128], [1, 32]])
                    v.tensor_tensor(o1, src1, cosv, ALU.mult)
                    v.tensor_tensor(o2, src1, sinv, ALU.mult)
                    v.tensor_tensor(g1, src2, sinv, ALU.mult)
                    ins = v.tensor_tensor(g2, src2, cosv, ALU.mult)
                    inc(ins, "h_ra", 1)
                    wge(v, "h_ra")
                    v.tensor_tensor(o1, o1, g1, ALU.subtract)
                    ins = v.tensor_tensor(o2, o2, g2, ALU.add)
                    inc(ins, "h_rb", 1)
                    wge(v, "h_rb")
                    ins = v.tensor_copy(sgst[:, D + (ks % 2) * 64:D + (ks % 2) * 64 + 64],
                                        f2[:, (ks % 2) * 64:(ks % 2) * 64 + 64])
                    inc(ins, "h_bf", 1)


    BAR()
    if stop == 'B3b':
        es.close()
        return nc

    if stop == 'B4':
        es.close()
        return nc

    O_KH, O_VA, O_KRT, O_PST, O_QCH, O_WQC, O_QLM, O_QRH = \
        0, 16384, 33280, 41472, 42496, 43520, 46592, 48640


    NKS = (NKA * 4, NKB * 4)
    DIAG0 = (28, 60)
    SPH = NKS[0] + NKS[1]

    def pv_groups_before(h, slot):
        return h * SPH + (NKS[0] if slot == 1 else 0)

    with nc.Block() as blk:
        @blk.sync
        def _(sy):
            ins = sy.dma_start(out=ar[0:64, O_KRT:O_KRT + 8192], in_=kr_sc)
            inc(ins, "j_kr", 16)
            for h in range(16):
                buf = h % 2
                if h >= 2:
                    wge(sy, "j_st", SPH * (h - 1))
                ins = sy.dma_start(out=A(O_KH + buf * 8192, 8192),
                                   in_=k_sc[h * 128:(h + 1) * 128, :])
                inc(ins, "j_k%d" % buf, 16)
                ins = sy.dma_start(
                    out=ar[0:64, O_QRH + buf * 1024:O_QRH + (buf + 1) * 1024],
                    in_=bass.AP(qr_sc.tensor, h * 1024, [[H * 1024, 64], [1, 1024]]))
                inc(ins, "j_qr%d" % buf, 16)
                if h >= 2:
                    wge(sy, "j_pvall", SPH * (h - 1))
                ins = sy.dma_start(
                    out=bass.AP(ar, O_VA + buf * 8448,
                                [[ARN, 128], [VW, 64], [1, 128]]),
                    in_=bass.AP(v_sc.tensor, h * 128,
                                [[H * DC, 128], [128 * H * DC, 64], [1, 128]]))
                inc(ins, "j_v%d" % buf, 16)
                for slot in range(2):
                    m = h * 2 + slot
                    if m >= 2:
                        wge(sy, "j_oxa", 4 * (m - 1))
                    ins = sy.dma_start(
                        out=bass.AP(xout, (m % 2) * 512, [[4 * D, 128], [128, 4], [1, 128]]),
                        in_=bass.AP(x_own.tensor, slot * 512 * D + h * 128,
                                    [[D, 128], [128 * D, 4], [1, 128]]))
                    inc(ins, "j_xsl%d" % (m % 2), 16)
                for lq in range(12):
                    i2 = h * 12 + lq
                    if i2 >= 2:
                        wge(sy, "j_qc", i2 - 1)
                    ins = sy.dma_start(
                        out=A(O_QLM + (lq % 2) * 1024, 1024),
                        in_=bass.AP(ql_sc.tensor, lq * 128 * 1024,
                                    [[1024, 128], [1, 1024]]))
                    inc(ins, "j_ql%d" % (lq % 2), 16)

        @blk.gpsimd
        def _(gp):
            for h in range(16):
                buf = h % 2
                if h >= 2:
                    wge(gp, "j_qcb", h - 1)
                ins = gp.dma_start(
                    out=AAP(O_WQC + buf * 1536, [[128, 12], [1, 128]]),
                    in_=bass.AP(wq_uc.tensor, h * 128,
                                [[2048, 128], [128 * 2048, 12], [1, 128]]))
                inc(ins, "j_wqc%d" % buf, 16)
                for slot in range(2):
                    for qs in range(4):
                        k = h * 8 + slot * 4 + qs
                        wge(gp, "j_oxa", k + 1)
                        rows = slot * 512 + qs * 128
                        ins = gp.dma_start(
                            out=x1_sc[rows:rows + 128, h * 128:(h + 1) * 128],
                            in_=ost[:, (k % 2) * 128:(k % 2) * 128 + 128])
                        inc(ins, "j_acc%d" % (k % 2), 16)
            wge(gp, "j_acc0", 16 * 64)
            wge(gp, "j_acc1", 16 * 64)

        @blk.tensor
        def _(t):
            wge(t, "j_kr", 16)
            wge(t, "j_vone", 1)
            gidx = 0
            tri_n = 0
            for h in range(16):
                buf = h % 2
                wge(t, "j_wqc%d" % buf, 16 * (h // 2 + 1))
                if h >= 1:
                    wge(t, "j_qcb", h)
                for lq in range(12):
                    wge(t, "j_ql%d" % (lq % 2), 16 * ((h * 12 + lq) // 2 + 1))
                    for qh in range(2):
                        ps = psG if qh == 0 else psU
                        ins = t.matmul(ps[:, :],
                                       A(O_WQC + buf * 1536 + lq * 128, 128),
                                       A(O_QLM + (lq % 2) * 1024 + qh * 512, 512),
                                       start=(lq == 0), stop=(lq == 11))
                    inc(ins, "j_qc", 1)
                wge(t, "j_k%d" % buf, 16 * (h // 2 + 1))
                wge(t, "j_qr%d" % buf, 16 * (h // 2 + 1))
                wge(t, "j_v%d" % buf, 16 * (h // 2 + 1))
                wge(t, "j_qcb", h + 1)
                for slot in range(2):
                    nks = NKS[slot]
                    qc0 = slot * 512
                    d0 = DIAG0[slot]
                    base_pv = pv_groups_before(h, slot)
                    for ks in range(nks + 1):
                        if ks < nks:
                            gidx += 1
                            if gidx >= 3:
                                wge(t, "j_ex", gidx - 2)
                            ps = psA if ks % 2 == 0 else psB
                            t.matmul(ps[:, :],
                                     A(O_KH + buf * 8192 + ks * 128, 128),
                                     A(O_QCH + qc0, 512),
                                     start=True, stop=False)
                            ins = t.matmul(ps[:, :],
                                           bass.AP(ar, O_KRT + ks * 128,
                                                   [[ARN, 64], [1, 128]]),
                                           bass.AP(ar, O_QRH + buf * 1024 + qc0,
                                                   [[ARN, 64], [1, 512]]),
                                           start=False, stop=True)
                            inc(ins, "j_st", 1)
                        if ks >= 1:
                            pk = ks - 1
                            exp_n = gidx if ks == nks else gidx - 1
                            wge(t, "j_ex", exp_n)
                            if pk >= d0:
                                wge(t, "j_tri", tri_n + (pk - d0) + 1)
                            for qs in range(4):
                                ins = t.matmul(
                                    psO[:, qs * 512:qs * 512 + 129],
                                    A(O_PST + (pk % 2) * 512 + qs * 128, 128),
                                    A(O_VA + buf * 8448 + pk * VW, 129),
                                    start=(pk == 0), stop=(pk == nks - 1))
                            inc(ins, "j_pvall", 1)
                    tri_n += 4
                    wge(t, "j_oev", h * 8 + slot * 4 + 4)

        @blk.scalar
        def _(sc):
            gidx = 0
            sgi = 0
            for h in range(16):
                for qh in range(2):
                    ps = psG if qh == 0 else psU
                    wge(sc, "j_qc", h * 12 + 12)
                    if h >= 1 and qh == 0:
                        wge(sc, "j_st", SPH * h)
                    ins = sc.activation(A(O_QCH + qh * 512, 512), ps[:, :], AF.Identity,
                                        scale=SCALE, bias=bias_qc[:, h:h + 1])
                inc(ins, "j_qcb", 1)
                for slot in range(2):
                    nks = NKS[slot]
                    base_pv = pv_groups_before(h, slot)
                    for ks in range(nks):
                        gidx += 1
                        wge(sc, "j_st", gidx)
                        if ks >= 2:
                            wge(sc, "j_pvall", base_pv + ks - 1)
                        elif base_pv >= 1:
                            wge(sc, "j_pvall", base_pv)
                        ps = psA if ks % 2 == 0 else psB
                        mcol = ks if slot == 0 else 32 + ks
                        ins = sc.activation(A(O_PST + (ks % 2) * 512, 512), ps[:, :],
                                            AF.Exp, bias=mb_sb[:, mcol:mcol + 1])
                        inc(ins, "j_ex", 1)
                    for qs in range(4):
                        k = h * 8 + slot * 4 + qs
                        wge(sc, "j_lr", k + 1)
                        if k >= 2:
                            wge(sc, "j_acc%d" % (k % 2), 16 * (k // 2))
                        ins = sc.activation(ost[:, (k % 2) * 128:(k % 2) * 128 + 128],
                                            psO[:, qs * 512:qs * 512 + 128], AF.Identity,
                                            scale=lrec[:, 0:1])
                        inc(ins, "j_oev", 1)

        @blk.vector
        def _(v):
            ins = v.memset(bass.AP(ar, O_VA + 128, [[ARN, 128], [VW, 128]]), 1.0)
            inc(ins, "j_vone", 1)
            gidx = 0
            sgi = 0
            for h in range(16):
                for slot in range(2):
                    nks = NKS[slot]
                    d0 = DIAG0[slot]
                    base_pv = pv_groups_before(h, slot)
                    for ks in range(nks):
                        gidx += 1
                        if ks >= d0:
                            j = ks - d0
                            wge(v, "j_ex", gidx)
                            ins = v.tensor_mul(A(O_PST + (ks % 2) * 512, 512),
                                               A(O_PST + (ks % 2) * 512, 512),
                                               tri_sb[:, j * 512:(j + 1) * 512])
                            inc(ins, "j_tri", 1)
                    m = h * 2 + slot
                    for qs in range(4):
                        k = h * 8 + slot * 4 + qs
                        if qs == 0:
                            wge(v, "j_pvall", base_pv + nks)
                        if k >= 1:
                            wge(v, "j_oev", k)
                        ins = v.reciprocal(lrec[:, 0:1],
                                           psO[:, qs * 512 + 128:qs * 512 + 129])
                        inc(ins, "j_lr", 1)
                        wge(v, "j_oev", k + 1)
                        wge(v, "j_xsl%d" % (m % 2), 16 * (m // 2 + 1))
                        ins = v.tensor_tensor(
                            ost[:, (k % 2) * 128:(k % 2) * 128 + 128],
                            ost[:, (k % 2) * 128:(k % 2) * 128 + 128],
                            bass.AP(xout, (m % 2) * 512 + qs * 128, [[4 * D, 128], [1, 128]]),
                            ALU.add)
                        inc(ins, "j_oxa", 1)

    BAR()
    if stop == 'B5':
        es.close()
        return nc

    O_XQT, O_PGR, O_WGS, O_WUS, O_WDS, O_GU = 0, 8192, 16384, 20480, 24576, 25600
    O_SGW, O_SUW, O_SDW = 26624, 30720, 34816

    with nc.Block() as blk:
        @blk.sync
        def _(sy):
            for half in range(2):
                for tq in range(4):
                    i = half * 4 + tq
                    if half == 1:
                        wge(sy, "m_out", 16 * 4)
                    ins = sy.dma_start(out=xout[:, tq * D:(tq + 1) * D],
                                       in_=x1_sc[i * 128:(i + 1) * 128, :])
                    inc(ins, "m_x1", 16)
                wge(sy, "m_xq", 16 * 4 * (half + 1))
                for dc in range(16):
                    ins = sy.dma_start(
                        out=A(O_XQT + dc * 512, 512),
                        in_=xq_sc[half * 512:(half + 1) * 512, dc * 128:(dc + 1) * 128],
                        transpose=True)
                    inc(ins, "m_xqt", 16)
                for tq in range(4):
                    i = half * 4 + tq
                    wge(sy, "m_fl", (half + 1) * 16 * 4)
                    ins = sy.dma_start(out=out_own[i * 128:(i + 1) * 128, :],
                                       in_=xout[:, tq * D:(tq + 1) * D])
                    inc(ins, "m_out", 16)
            wge(sy, "m_out", 16 * 8)

        @blk.gpsimd
        def _(gp):
            for half in range(2):
                for hg in range(4):
                    for hs in range(16):
                        habs = hg * 16 + hs
                        i = half * 64 + habs
                        if i >= 2:
                            wge(gp, "m_wgs", i - 1)
                        ins = gp.dma_start(
                            out=AAP(O_SGW + (i % 2) * 2048, [[128, 16], [1, 128]]),
                            in_=bass.AP(wgT.tensor, habs * 128,
                                        [[FH, 128], [128 * FH, 16], [1, 128]]))
                        inc(ins, "mc_wg%d" % (i % 2), 16)
                        if i >= 2:
                            wge(gp, "m_wus", i - 1)
                        ins = gp.dma_start(
                            out=AAP(O_SUW + (i % 2) * 2048, [[128, 16], [1, 128]]),
                            in_=bass.AP(wuT.tensor, habs * 128,
                                        [[FH, 128], [128 * FH, 16], [1, 128]]))
                        inc(ins, "mc_wu%d" % (i % 2), 16)
                    for dq in range(4):
                        for hs in range(16):
                            habs = hg * 16 + hs
                            j = ((half * 4 + hg) * 4 + dq) * 16 + hs
                            if j >= 2:
                                wge(gp, "m_wds", j - 1)
                            ins = gp.dma_start(
                                out=A(O_SDW + (j % 2) * 512, 512),
                                in_=wdT[habs * 128:(habs + 1) * 128,
                                        dq * 512:(dq + 1) * 512])
                            inc(ins, "mc_wd%d" % (j % 2), 16)

        @blk.vector
        def _(v):
            for half in range(2):
                for tq in range(4):
                    i = half * 4 + tq
                    wge(v, "m_x1", 16 * 4 * (half + 1))
                    if i >= 1:
                        wge(v, "m_ln1", i)
                    for j in range(4):
                        ins = v.bn_stats(bn24[:, j * 6:(j + 1) * 6],
                                         xout[:, tq * D + j * 512:tq * D + (j + 1) * 512])
                    inc(ins, "m_bn", 1)
                    wge(v, "m_bn")
                    ins = v.bn_aggr(st2[:, :], bn24[:, :])
                    inc(ins, "m_ag", 1)
                    wge(v, "m_rs1", i + 1)
                    if i >= 1:
                        wge(v, "m_xn1", i)
                    ins = v.tensor_scalar(nmean[:, :], st2[:, 0:1], rstd[:, 0:1], -1.0,
                                          ALU.mult, ALU.mult)
                    inc(ins, "m_nm", 1)
                    wge(v, "m_xn1", i + 1)
                    ins = v.tensor_tensor(f2[:, :], fst[:, :], cb1_rep[:, :], ALU.mult)
                    inc(ins, "m_h2a", 1)
                    wge(v, "m_h2a")
                    ins = v.tensor_tensor(f2[:, :], f2[:, :], sb_rep[:, :], ALU.add)
                    inc(ins, "m_h2", 1)
                    wge(v, "m_h2")
                    if i >= 1:
                        wge(v, "m_ln2", i)
                    for j in range(4):
                        ins = v.bn_stats(bn24[:, j * 6:(j + 1) * 6],
                                         f2[:, j * 512:(j + 1) * 512])
                    inc(ins, "m_bn2", 1)
                    wge(v, "m_bn2")
                    ins = v.bn_aggr(st2b[:, :], bn24[:, :])
                    inc(ins, "m_ag2", 1)
                    wge(v, "m_rs2", i + 1)
                    if i >= 1:
                        wge(v, "m_y0", i)
                    ins = v.tensor_scalar(nm2[:, :], st2b[:, 0:1], rstd2[:, 0:1], -1.0,
                                          ALU.mult, ALU.mult)
                    inc(ins, "m_nm2", 1)
                    wge(v, "m_y0", i + 1)
                    ins = v.tensor_reduce(amax_t[:, :], fst[:, :], mybir.AxisListType.X,
                                          ALU.max, apply_absolute_value=True)
                    inc(ins, "m_am", 1)
                    wge(v, "m_am")
                    ins = v.tensor_scalar_mul(srec[:, :], amax_t[:, :], 1.0 / 127.0)
                    inc(ins, "m_sr", 1)
                    wge(v, "m_sr")
                    ins = v.reciprocal(sc_ap[:, :], amax_t[:, :])
                    inc(ins, "m_rc2", 1)
                    wge(v, "m_rc2")
                    ins = v.tensor_scalar(fst[:, :], fst[:, :], sc_ap[:, 0:1], 127.0,
                                          ALU.mult, ALU.mult)
                    inc(ins, "m_ys", 1)
                    wge(v, "m_ys")
                    for hh in range(2):
                        c0, c1 = hh * 1024, (1 - hh) * 1024
                        if hh == 1:
                            wge(v, "m_xqv", 2 * i + 1)
                        ins = v.tensor_copy(ti32[:, :], fst[:, c0:c0 + 1024])
                        inc(ins, "m_rnd", 1)
                        wge(v, "m_rnd")
                        ins = v.tensor_copy(f2[:, c0:c0 + 1024], ti32[:, :])
                        inc(ins, "m_rnd", 1)
                        wge(v, "m_rnd")
                        ins = v.tensor_tensor(fst[:, c0:c0 + 1024], fst[:, c0:c0 + 1024],
                                              f2[:, c0:c0 + 1024], ALU.subtract)
                        inc(ins, "m_rnd", 1)
                        wge(v, "m_rnd")
                        for rung in range(4):
                            thr = rung + 0.5
                            ins = v.tensor_scalar(f2[:, c1:c1 + 1024], fst[:, c0:c0 + 1024],
                                                  thr, None, ALU.is_ge)
                            inc(ins, "m_rnd", 1)
                            wge(v, "m_rnd")
                            ins = v.tensor_tensor(f2[:, c0:c0 + 1024], f2[:, c0:c0 + 1024],
                                                  f2[:, c1:c1 + 1024], ALU.add)
                            inc(ins, "m_rnd", 1)
                            wge(v, "m_rnd")
                            ins = v.tensor_scalar(f2[:, c1:c1 + 1024], fst[:, c0:c0 + 1024],
                                                  -thr, None, ALU.is_le)
                            inc(ins, "m_rnd", 1)
                            wge(v, "m_rnd")
                            ins = v.tensor_tensor(f2[:, c0:c0 + 1024], f2[:, c0:c0 + 1024],
                                                  f2[:, c1:c1 + 1024], ALU.subtract)
                            inc(ins, "m_rnd", 1)
                            wge(v, "m_rnd")
                        if i >= 1:
                            wge(v, "m_xq", 16 * i)
                        ins = v.tensor_scalar_mul(sgst[:, D + c0:D + c0 + 1024],
                                                  f2[:, c0:c0 + 1024], srec[:, 0:1])
                        inc(ins, "m_xqv", 1)
                for hg in range(4):
                    g = half * 4 + hg
                    for hs in range(16):
                        i = half * 64 + hg * 16 + hs
                        wge(v, "m_u", i + 1)
                        if hs == 0 and g >= 1:
                            wge(v, "m_pvd", 4 * g)
                        if i >= 1:
                            wge(v, "m_p", i)
                        ins = v.tensor_copy(A(O_GU + 512, 512), psU[:, :])
                        inc(ins, "m_ue", 1)
                        wge(v, "m_ge", i + 1)
                        wge(v, "m_ue")
                        ins = v.tensor_mul(A(O_PGR + hs * 512, 512), A(O_GU, 512),
                                           A(O_GU + 512, 512))
                        inc(ins, "m_p", 1)
                    for dq in range(4):
                        gi = half * 16 + hg * 4 + dq
                        wge(v, "m_pvd", gi + 1)
                        for ss in range(4):
                            if gi + ss > 0:
                                wge(v, "m_fl")
                            ins = v.tensor_tensor(psA[:, :], psO[:, ss * 512:(ss + 1) * 512],
                                                  gb_rep[:, dq * 512:(dq + 1) * 512], ALU.mult)
                            inc(ins, "m_fla", 1)
                            wge(v, "m_fla")
                            ins = v.tensor_tensor(
                                xout[:, ss * D + dq * 512:ss * D + (dq + 1) * 512],
                                xout[:, ss * D + dq * 512:ss * D + (dq + 1) * 512],
                                psA[:, :], ALU.add)
                            inc(ins, "m_fl", 1)

        @blk.scalar
        def _(sc):
            for half in range(2):
                for tq in range(4):
                    i = half * 4 + tq
                    wge(sc, "m_ag", i + 1)
                    ins = sc.activation(rstd[:, :], st2[:, 1:2], AF.Ln, bias=cst[:, 0:1])
                    inc(ins, "m_ln1", 1)
                    wge(sc, "m_ln1")
                    ins = sc.activation(rstd[:, :], rstd[:, :], AF.Exp, scale=-0.5)
                    inc(ins, "m_rs1", 1)
                    wge(sc, "m_nm", i + 1)
                    ins = sc.activation(fst[:, :], xout[:, tq * D:(tq + 1) * D], AF.Identity,
                                        scale=rstd[:, 0:1], bias=nmean[:, 0:1])
                    inc(ins, "m_xn1", 1)
                    wge(sc, "m_ag2", i + 1)
                    ins = sc.activation(rstd2[:, :], st2b[:, 1:2], AF.Ln, bias=cst[:, 1:2])
                    inc(ins, "m_ln2", 1)
                    wge(sc, "m_ln2")
                    ins = sc.activation(rstd2[:, :], rstd2[:, :], AF.Exp, scale=-0.5)
                    inc(ins, "m_rs2", 1)
                    wge(sc, "m_nm2", i + 1)
                    ins = sc.activation(fst[:, :], f2[:, :], AF.Identity,
                                        scale=rstd2[:, 0:1], bias=nm2[:, 0:1])
                    inc(ins, "m_y0", 1)
                    wge(sc, "m_xqv", 2 * (i + 1))
                    ins = sc.dma_start(out=xq_sc[i * 128:(i + 1) * 128, :],
                                       in_=sgst[:, D:2 * D])
                    inc(ins, "m_xq", 16)
                for hg in range(4):
                    for hs in range(16):
                        i = half * 64 + hg * 16 + hs
                        wge(sc, "mc_wg%d" % (i % 2), 16 * (i // 2 + 1))
                        if i >= 2:
                            wge(sc, "m_g", i - 1)
                        ins = sc.activation(A(O_WGS + (i % 2) * 2048, 2048),
                                            A(O_SGW + (i % 2) * 2048, 2048), AF.Sign)
                        inc(ins, "m_wgs", 1)
                        wge(sc, "mc_wu%d" % (i % 2), 16 * (i // 2 + 1))
                        if i >= 2:
                            wge(sc, "m_u", i - 1)
                        ins = sc.activation(A(O_WUS + (i % 2) * 2048, 2048),
                                            A(O_SUW + (i % 2) * 2048, 2048), AF.Sign)
                        inc(ins, "m_wus", 1)
                        wge(sc, "m_g", i + 1)
                        if i >= 1:
                            wge(sc, "m_p", i)
                        ins = sc.activation(A(O_GU, 512), psG[:, :], AF.Silu)
                        inc(ins, "m_ge", 1)
                    for dq in range(4):
                        for hs in range(16):
                            j = ((half * 4 + hg) * 4 + dq) * 16 + hs
                            wge(sc, "mc_wd%d" % (j % 2), 16 * (j // 2 + 1))
                            if j >= 2:
                                jj = j - 2
                                if jj % 16 == 15:
                                    wge(sc, "m_pvd", jj // 16 + 1)
                                else:
                                    wge(sc, "m_dn", 15 * (jj // 16) + jj % 16 + 1)
                            ins = sc.activation(A(O_WDS + (j % 2) * 512, 512),
                                                A(O_SDW + (j % 2) * 512, 512), AF.Sign)
                            inc(ins, "m_wds", 1)

        @blk.tensor
        def _(t):
            for half in range(2):
                wge(t, "m_xqt", 16 * 16 * (half + 1))
                for hg in range(4):
                    for hs in range(16):
                        i = half * 64 + hg * 16 + hs
                        wge(t, "m_wgs", i + 1)
                        wge(t, "m_wus", i + 1)
                        if i >= 1:
                            wge(t, "m_ge", i)
                            wge(t, "m_ue", i)
                        for dc in range(16):
                            ins = t.matmul(psG[:, :],
                                           A(O_WGS + (i % 2) * 2048 + dc * 128, 128),
                                           A(O_XQT + dc * 512, 512),
                                           start=(dc == 0), stop=(dc == 15))
                        inc(ins, "m_g", 1)
                        for dc in range(16):
                            ins = t.matmul(psU[:, :],
                                           A(O_WUS + (i % 2) * 2048 + dc * 128, 128),
                                           A(O_XQT + dc * 512, 512),
                                           start=(dc == 0), stop=(dc == 15))
                        inc(ins, "m_u", 1)
                    for dq in range(4):
                        gi = half * 16 + hg * 4 + dq
                        if gi >= 1:
                            wge(t, "m_fl", gi * 4)
                        wge(t, "m_p", (half * 4 + hg) * 16 + 16)
                        for hs in range(16):
                            i = ((half * 4 + hg) * 4 + dq) * 16 + hs
                            wge(t, "m_wds", i + 1)
                            for ss in range(4):
                                ins = t.matmul(psO[:, ss * 512:(ss + 1) * 512],
                                               AAP(O_PGR + hs * 512 + ss * 128, [[1, 128]]),
                                               A(O_WDS + (i % 2) * 512, 512),
                                               start=(hs == 0), stop=(hs == 15))
                            if hs == 15:
                                inc(ins, "m_pvd", 1)
                            else:
                                inc(ins, "m_dn", 1)

    BAR()
    if stop == 'X14':
        es.close()
        return nc
    es.close()
    return nc


def kernel(**inputs):
    in_maps, plans = _host_prep(inputs)
    nc = build_program()
    res = run_bass_kernel_spmd(nc, in_maps, core_ids=list(range(8)))
    out = np.empty((S, D), np.float32)
    for core in range(8):
        a, b = plans[core]
        oo = res.results[core]["out_own"]
        out[a * CH:(a + 1) * CH] = oo[0:CH]
        out[b * CH:(b + 1) * CH] = oo[CH:2 * CH]
    return out.reshape(1, S, D)
